# version 06
# speedup vs baseline: 1.3604x; 1.0201x over previous
import math
from contextlib import ExitStack

import numpy as np

import concourse.bass as bass
import concourse.mybir as mybir
from concourse.bass_utils import run_bass_kernel_spmd

F32 = mybir.dt.float32
BF16 = mybir.dt.bfloat16
AF = mybir.ActivationFunctionType
ALU = mybir.AluOpType
AXX = mybir.AxisListType.X

B_CORE = 2
T = 2048
D = 1024
NT = T // 128
DC = D // 128
EPS = 1e-5

DEBUG_TAP = None

ENGINES = ("tensor", "vector", "scalar", "gpsimd", "sync")
DMA_QUEUES = ("sync", "scalar", "gpsimd")
NLANES = 8


class Op:
    __slots__ = ("eng", "fn", "reads", "writes", "seq", "dma", "marked", "inc_idx",
                 "waits", "barrier", "lane", "lane_n", "dbg_out", "dbg_in")

    def __init__(self, eng, fn, reads, writes, dma):
        self.eng = eng
        self.fn = fn
        self.reads = reads
        self.writes = writes
        self.dma = dma
        self.marked = dma
        self.inc_idx = None
        self.waits = []
        self.barrier = None
        self.lane = None
        self.lane_n = None


class Prog:

    def __init__(self, nc):
        self.nc = nc
        self.ops = {e: [] for e in ENGINES}
        self.last_write = {}
        self.readers = {}
        self.pending_barrier = {e: None for e in ENGINES}
        self.all_ops = []
        self.dma_count = {q: 0 for q in DMA_QUEUES}

    def _edge(self, consumer, producer):
        if producer is None or producer is consumer:
            return
        if producer.eng == "tensor" and consumer.eng == "tensor" and not producer.dma:
            return
        consumer.waits.append(producer)

    def op(self, eng, fn, reads=(), writes=(), dma=False):
        o = Op(eng, fn, list(reads), list(writes), dma)
        o.seq = len(self.ops[eng])
        if self.pending_barrier[eng] is not None:
            o.barrier = self.pending_barrier[eng]
            self.pending_barrier[eng] = None
        for k in o.reads:
            self._edge(o, self.last_write.get(k))
        for k in o.writes:
            self._edge(o, self.last_write.get(k))
            for r in self.readers.get(k, ()):
                self._edge(o, r)
        for k in o.reads:
            self.readers.setdefault(k, []).append(o)
        for k in o.writes:
            self.last_write[k] = o
            self.readers[k] = []
        self.ops[eng].append(o)
        self.all_ops.append(o)
        return o

    def dma(self, queue, out, in_, reads=(), writes=(), **kw):
        assert queue in DMA_QUEUES
        o = self.op(queue, lambda e: e.dma_start(out=out, in_=in_, **kw),
                    reads=reads, writes=writes, dma=True)
        o.dbg_out, o.dbg_in = out, in_
        i = self.dma_count[queue]
        self.dma_count[queue] = i + 1
        o.lane = (queue, i % NLANES)
        o.lane_n = i // NLANES + 1
        return o

    def barrier(self):
        snap = {e: len(self.ops[e]) for e in ENGINES}
        for e in ENGINES:
            self.pending_barrier[e] = snap

    def finalize(self, sems, dma_sems):
        for o in self.all_ops:
            for p in o.waits:
                p.marked = True
            if o.barrier is not None:
                for e in ENGINES:
                    n = o.barrier[e]
                    if n > 0:
                        self.ops[e][n - 1].marked = True
        for e in ENGINES:
            n = 0
            for o in self.ops[e]:
                if o.marked and not o.dma:
                    n += 1
                    o.inc_idx = n
        def sem_target(eng, upto_seq):
            res = []
            best_e = 0
            lanes = {}
            for o in self.ops[eng][:upto_seq]:
                if o.dma:
                    lanes[o.lane] = o.lane_n
                elif o.marked:
                    best_e = o.inc_idx
            if best_e:
                res.append((sems[eng], best_e))
            for lane, n in lanes.items():
                res.append((dma_sems[lane], 16 * n))
            return res

        self._sem_target = sem_target
        self.sems = sems
        self.dma_sems = dma_sems

    def emit(self, block):
        sems, dma_sems = self.sems, self.dma_sems
        for e in ENGINES:
            ops = self.ops[e]
            if not ops:
                continue

            def make(eng_name, eng_ops):
                def body(eng):
                    waited = {}
                    def wait(sem, val):
                        if waited.get(id(sem), -1) >= val:
                            return
                        eng.wait_ge(sem, val)
                        waited[id(sem)] = val
                    for o in eng_ops:
                        if o.barrier is not None:
                            for e2 in ENGINES:
                                if e2 == eng_name:
                                    continue
                                for sem, val in self._sem_target(e2, o.barrier[e2]):
                                    wait(sem, val)
                        for p in o.waits:
                            if p.dma:
                                wait(dma_sems[p.lane], 16 * p.lane_n)
                            elif p.eng == eng_name:
                                wait(sems[eng_name], p.inc_idx)
                            else:
                                wait(sems[p.eng], p.inc_idx)
                        if o.dma:
                            if o.lane_n > 1:
                                wait(dma_sems[o.lane], 16 * (o.lane_n - 1))
                        ins = o.fn(eng)
                        if o.marked:
                            if o.dma:
                                ins.then_inc(dma_sems[o.lane], 16)
                            else:
                                ins.then_inc(sems[o.eng], 1)
                return body

            body = make(e, ops)
            getattr(block, e)(body)


class Arena:

    def __init__(self, nc, total_bytes):
        self.t = None
        self.nc = nc
        self.total = total_bytes
        self.lo = 0
        self.hi = total_bytes
        self._cm = None

    def open(self):
        cols = self.total // 4
        self._cm = self.nc.sbuf_tensor("ARENA", [128, cols], F32)
        self.t = self._cm.__enter__()
        return self

    def close(self):
        self._cm.__exit__(None, None, None)

    def _carve(self, off_bytes, parts, cols, dtype):
        esz = 2 if dtype == BF16 else 4
        base = self.t[:, :]
        if dtype == BF16:
            base = self.t.ap().bitcast(BF16)
        else:
            base = self.t.ap()
        start = off_bytes // esz
        return base[0:parts, start:start + cols]

    def galloc(self, parts, cols, dtype=F32):
        esz = 2 if dtype == BF16 else 4
        nb = ((cols * esz + 63) // 64) * 64
        off = self.lo
        self.lo += nb
        assert self.lo <= self.hi, "SBUF arena overflow (global)"
        return self._carve(off, parts, cols, dtype)

    def reset_stage(self):
        self.hi = self.total

    def salloc(self, parts, cols, dtype=F32):
        esz = 2 if dtype == BF16 else 4
        nb = ((cols * esz + 63) // 64) * 64
        self.hi -= nb
        assert self.hi >= self.lo, f"SBUF arena overflow: lo={self.lo} hi={self.hi}"
        return self._carve(self.hi, parts, cols, dtype)


class Stage:

    def __init__(self, nc, P, ar, psum, inputs_decl):
        self.nc = nc
        self.P = P
        self.ar = ar
        self.psum = psum
        self.inputs_decl = inputs_decl
        self.uid = 0
        self.w16 = {}
        self.gscr = None

    def tap(self, name, ap, key, cast=False):
        if DEBUG_TAP != name or getattr(self, "_tapped", False):
            return
        self._tapped = True
        dbg = self.nc.dram_tensor("dbg", [128, T], F32 if not cast else BF16,
                                  kind="ExternalOutput").ap()
        sh = ap.shape
        self.P.dma("sync", dbg[0:sh[0], 0:sh[1]], ap, reads=[key],
                   writes=["dbg"])

    def prep16(self, name, dram_ap, rows, cols, pad_rows=0):
        if name in self.w16:
            return self.w16[name]
        rp = rows + pad_rows
        img = self.nc.dram_tensor(f"w16_{name}", [rp, cols], BF16,
                                  kind="Internal").ap()
        key = f"w16_{name}"
        self.P.dma("gpsimd", img[0:rows, :], dram_ap, writes=[key])
        if pad_rows:
            z = self.ar.salloc(pad_rows, cols, BF16)
            self.P.op("vector", lambda e: e.memset(z, 0.0), writes=[key + "z"])
            self.P.dma("sync", img[rows:rp, :], z,
                       reads=[key + "z"], writes=[key])
        self.w16[name] = (img, key)
        return img, key

    def wN(self, name, dram_ap, rows, cols, wk, bufs=None):
        img, ikey = self.prep16(name, dram_ap, rows, cols)
        nr = rows // 128
        if bufs is None:
            bufs = [self.ar.salloc(128, cols, BF16) for _ in range(nr)]
        img_r = img.rearrange("(r p) c -> r p c", p=128)
        for r in range(nr):
            self.P.dma("sync", bufs[r], img_r[r], reads=[ikey], writes=[wk])
        return bufs

    def wT(self, name, dram_ap, rows, cols, wk, pad_rows=0):
        img, ikey = self.prep16(name, dram_ap, rows, cols, pad_rows)
        rp = rows + pad_rows
        nck = cols // 128
        out = [self.ar.salloc(128, rp, BF16) for _ in range(nck)]
        for c in range(nck):
            self.P.dma("sync", out[c], img[:, c * 128:(c + 1) * 128],
                       reads=[ikey], writes=[wk], transpose=True)
        return out

    def inp(self, name, shape, dtype=F32):
        if name not in self.inputs_decl:
            self.inputs_decl[name] = self.nc.dram_tensor(
                name, list(shape), dtype, kind="ExternalInput").ap()
        return self.inputs_decl[name]

    def psum_bf16(self, i):
        return self.psum[i].bitcast(BF16)

    def u(self):
        self.uid += 1
        return self.uid




class LNPool:

    def __init__(self, S, tag, n=4):
        self.S, self.tag, self.n = S, tag, n
        self.st = [S.ar.salloc(128, 8, F32) for _ in range(n)]
        self.scr = S.gscr
        self.i = 0

    def stats(self, x_ap, key_x):
        S = self.S
        P = S.P
        idx = self.i % self.n
        self.i += 1
        st = self.st[idx]
        ks = f"{self.tag}_st{idx}"
        scr = self.scr[self.i % 2]
        kscr = f"gscr{self.i % 2}"
        sum_ = st[:, 0:1]; sq = st[:, 1:2]; m = st[:, 2:3]
        var = st[:, 3:4]; rstd = st[:, 4:5]; negminv = st[:, 5:6]
        tmp = st[:, 6:7]
        P.op("vector", lambda e: e.reduce_sum(sum_, x_ap, axis=AXX),
             reads=[key_x], writes=[ks])
        P.op("scalar", lambda e: e.activation(scr, x_ap, AF.Square,
                                              accum_out=sq),
             reads=[key_x], writes=[kscr, ks])
        P.op("vector", lambda e: e.tensor_scalar_mul(m, sum_, 1.0 / D),
             reads=[ks], writes=[ks])
        P.op("vector", lambda e: e.tensor_scalar(var, sq, 1.0 / D, EPS,
                                                 ALU.mult, ALU.add),
             reads=[ks], writes=[ks])
        P.op("vector", lambda e: e.tensor_mul(tmp, m, m), reads=[ks], writes=[ks])
        P.op("vector", lambda e: e.tensor_sub(var, var, tmp), reads=[ks], writes=[ks])
        P.op("scalar", lambda e: e.activation(rstd, var, AF.Sqrt),
             reads=[ks], writes=[ks])
        P.op("vector", lambda e: e.reciprocal(rstd, rstd), reads=[ks], writes=[ks])
        P.op("vector", lambda e: e.scalar_tensor_tensor(negminv, m, -1.0, rstd,
                                                        ALU.mult, ALU.mult),
             reads=[ks], writes=[ks])
        return rstd, negminv, ks


def transpose_128(S, out_sbuf_ap, in_sbuf_ap, ident_ap, bank, key_in, key_out,
                  dtype, evac_eng="scalar"):
    P = S.P
    pb = S.psum[bank] if dtype == F32 else S.psum_bf16(bank)
    pk = f"psum{bank}"
    pslice = pb[0:128, 0:128]
    P.op("tensor", lambda e: e.transpose(pslice, in_sbuf_ap, ident_ap),
         reads=[key_in, "ident"], writes=[pk])
    P.op(evac_eng, lambda e: (e.copy(out_sbuf_ap, pslice) if evac_eng == "scalar"
                              else e.tensor_copy(out_sbuf_ap, pslice)),
         reads=[pk], writes=[key_out])


class Sandwich:

    def __init__(self, S, consts, stag, want_xtok=False, xn_f32=False,
                 use_bt=True):
        ar = S.ar
        self.S, self.stag = S, stag
        self.ident16 = consts["ident16"]
        self.ident32 = consts["ident32"]
        self.lnp = LNPool(S, stag + "_lnp")
        self.lnq = LNPool(S, stag + "_lnq")
        self.xbuf = [ar.salloc(128, D, F32) for _ in range(2)]
        self.xn16 = None if want_xtok else [ar.salloc(128, D,
                                            F32 if xn_f32 else BF16)
                                            for _ in range(2)]
        self.bt = [ar.salloc(128, D, F32) for _ in range(2)] if use_bt else None
        self.xb2 = [ar.salloc(128, D, F32) for _ in range(2)]
        self.xnT = [ar.salloc(128, T, BF16) for _ in range(DC)]

    def xnT_key(self, c):
        return f"{self.stag}_xnT_{c}"

    def preln(self, in_dram_b, b, xtok_out=None, xtok_key=None, pre_cb=None):
        S, P, stag = self.S, self.S.P, self.stag
        for i in range(NT):
            xb = self.xbuf[i % 2]
            kx = f"{stag}_prex_{i % 2}"
            P.dma("sync", xb, in_dram_b[i * 128:(i + 1) * 128, :],
                  reads=[f"resid_{b}_{i}"], writes=[kx])
            rstd, negminv, ks = self.lnp.stats(xb, kx)
            if xtok_out is None:
                xn = self.xn16[i % 2]
                kxn = f"{stag}_xn16_{i % 2}"
            else:
                xn = xtok_out[i]
                kxn = xtok_key(i)
            P.op("scalar", lambda e, xb=xb, xn=xn, r=rstd, nm=negminv:
                 e.activation(xn, xb, AF.Identity, bias=nm, scale=r),
                 reads=[kx, ks], writes=[kxn])
            if pre_cb is not None:
                r2 = pre_cb(i, xn, kxn)
                if r2 is not None:
                    xn, kxn = r2
            for c in range(DC):
                bank = 4 + (c % 2)
                transpose_128(S, self.xnT[c][:, i * 128:(i + 1) * 128],
                              xn[:, c * 128:(c + 1) * 128], self.ident16, bank,
                              kxn, self.xnT_key(c), BF16,
                              evac_eng="vector" if c % 2 else "scalar")

    def post_tile(self, i, b, bo_tok_ap, kbt, in_dram_b, out_dram_b,
                  pre_add_cb=None):
        S, P, stag = self.S, self.S.P, self.stag
        xb = self.xb2[i % 2]
        kx = f"{stag}_postx_{i % 2}"
        P.dma("sync", xb, in_dram_b[i * 128:(i + 1) * 128, :],
              reads=[f"resid_{b}_{i}"], writes=[kx])
        if pre_add_cb is not None:
            pre_add_cb(i, bo_tok_ap, kbt, xb, kx)
        rstd, negminv, ks = self.lnq.stats(bo_tok_ap, kbt)
        P.op("scalar", lambda e: e.activation(bo_tok_ap, bo_tok_ap, AF.Identity,
                                              bias=negminv, scale=rstd),
             reads=[kbt, ks], writes=[kbt])
        P.op("vector", lambda e: e.tensor_add(bo_tok_ap, bo_tok_ap, xb),
             reads=[kbt, kx], writes=[kbt])
        P.dma("sync", out_dram_b[i * 128:(i + 1) * 128, :], bo_tok_ap,
              reads=[kbt], writes=[f"resid_{b}_{i}"])

    def post_slab_dmajor(self, slab, b, bo_chunks, bo_keys, in_dram_b,
                         out_dram_b, ncols=512, pre_add_cb=None):
        S, P, stag = self.S, self.S.P, self.stag
        ntile = ncols // 128
        for j in range(ntile):
            i = slab * ntile + j
            bt = self.bt[j % 2]
            kbt = f"{stag}_bt_{j % 2}"
            for c in range(DC):
                bank = 6 + (c % 2)
                transpose_128(S, bt[:, c * 128:(c + 1) * 128],
                              bo_chunks[c][:, j * 128:(j + 1) * 128],
                              self.ident32, bank, bo_keys[c], kbt, F32,
                              evac_eng="vector" if c % 2 else "scalar")
            self.post_tile(i, b, bt, kbt, in_dram_b, out_dram_b,
                           pre_add_cb=pre_add_cb)


def load_wT(S, stag, dram_ap, R, C, ident16, wk):
    P, ar = S.P, S.ar
    assert R % 128 == 0 and C % 128 == 0
    nr, nck = R // 128, C // 128
    out = [ar.salloc(128, R, BF16) for _ in range(nck)]
    tmp = [ar.salloc(128, C, BF16) for _ in range(2)]
    dram_r = dram_ap.rearrange("(r p) c -> r p c", p=128)
    for r in range(nr):
        t = tmp[r % 2]
        kt = f"{wk}_t{r % 2}"
        P.dma("gpsimd", t, dram_r[r], writes=[kt])
        for c in range(nck):
            bank = c % 2
            transpose_128(S, out[c][:, r * 128:(r + 1) * 128],
                          t[:, c * 128:(c + 1) * 128], ident16, bank,
                          kt, wk, BF16,
                          evac_eng="vector" if c % 2 else "scalar")
    return out


def broadcast_row(S, out_ap, row_ap, bank, key_row, key_out, n, cols,
                  ones_ap, evac="vector"):
    P = S.P
    pk = f"psum{bank}"
    for c0 in range(0, cols, 512):
        w = min(512, cols - c0)
        P.op("tensor", lambda e, c0=c0, w=w: e.matmul(
            S.psum[bank][0:n, 0:w],
            ones_ap[0:1, 0:n], row_ap[0:1, c0:c0 + w],
            start=True, stop=True),
            reads=[key_row, "ones"], writes=[pk])
        P.op(evac, lambda e, c0=c0, w=w: (
            e.copy(out_ap[0:n, c0:c0 + w], S.psum[bank][0:n, 0:w])
            if evac == "scalar" else
            e.tensor_copy(out_ap[0:n, c0:c0 + w], S.psum[bank][0:n, 0:w])),
            reads=[pk], writes=[key_out])



def stage_l0_ffn(S, consts, in_dram, out_dram):
    nc, P, ar = S.nc, S.P, S.ar
    stag = "l0f"
    ar.reset_stage()

    w_win = S.inp("L0_ffn_Win", (D, 64))
    w_bin = S.inp("L0_ffn_bin_", (64,))
    w_w1 = S.inp("L0_ffn_W1", (64, 16))
    w_b1 = S.inp("L0_ffn_b1", (64, 16))
    w_w2 = S.inp("L0_ffn_W2", (64, 16))
    w_b2 = S.inp("L0_ffn_b2", (64,))
    w_wout = S.inp("L0_ffn_Wout", (64, D))
    w_bout = S.inp("L0_ffn_bout", (D,))

    sw = Sandwich(S, consts, stag)

    win = ar.salloc(128, DC * 64, BF16)
    wout = ar.salloc(64, D, BF16)
    smalls = ar.salloc(64, 64, F32)
    w1 = smalls[:, 0:16]; b1 = smalls[:, 16:32]; w2 = smalls[:, 32:48]
    bin_ = smalls[:, 48:49]; b2 = smalls[:, 49:50]
    fbv = ar.salloc(64, 16, F32)
    boutv = ar.salloc(128, DC, F32)

    wk = f"{stag}_w"
    win_r = w_win.rearrange("(c p) m -> c p m", p=128)
    for c in range(DC):
        P.dma("gpsimd", win[:, c * 64:(c + 1) * 64], win_r[c], writes=[wk])
    P.dma("gpsimd", wout, w_wout, writes=[wk])
    P.dma("sync", w1, w_w1, writes=[wk])
    P.dma("sync", b1, w_b1, writes=[wk])
    P.dma("sync", w2, w_w2, writes=[wk])
    P.dma("sync", bin_, w_bin.rearrange("(p o) -> p o", o=1), writes=[wk])
    P.dma("sync", b2, w_b2.rearrange("(p o) -> p o", o=1), writes=[wk])
    bout_r = w_bout.rearrange("(c p o) -> c p o", p=128, o=1)
    for c in range(DC):
        P.dma("sync", boutv[:, c:c + 1], bout_r[c], writes=[wk])
    P.op("vector", lambda e: e.tensor_mul(fbv, w1, bin_.broadcast_to([64, 16])),
         reads=[wk], writes=[f"{stag}_fb"])
    P.op("vector", lambda e: e.tensor_add(fbv, fbv, b1),
         reads=[wk, f"{stag}_fb"], writes=[f"{stag}_fb"])

    hbuf = [ar.salloc(64, 512, F32) for _ in range(2)]
    g32 = ar.salloc(64, 512, F32)
    g16 = ar.salloc(64, 512, BF16)
    bo = [ar.salloc(128, 512, F32) for _ in range(DC)]
    bo_keys = [f"{stag}_bo_{c}" for c in range(DC)]

    for b in range(B_CORE):
        sw.preln(in_dram[b], b)
        for slab in range(T // 512):
            sl = slice(slab * 512, (slab + 1) * 512)
            pk = "psum0"
            for c in range(DC):
                P.op("tensor", lambda e, c=c, sl=sl: e.matmul(
                    S.psum[0][0:64, 0:512], win[:, c * 64:(c + 1) * 64],
                    sw.xnT[c][:, sl], start=(c == 0), stop=(c == DC - 1)),
                    reads=[wk, sw.xnT_key(c)], writes=[pk])
            kg = f"{stag}_g"
            for i in range(16):
                h = hbuf[i % 2]
                kh = f"{stag}_h{i % 2}"
                P.op("scalar", lambda e, i=i, h=h: e.activation(
                    h, S.psum[0][0:64, 0:512], AF.Relu,
                    bias=fbv[:, i:i + 1], scale=w1[:, i:i + 1]),
                    reads=[pk, f"{stag}_fb", wk], writes=[kh])
                if i == 0:
                    P.op("vector", lambda e, h=h: e.tensor_scalar(
                        g32, h, w2[:, 0:1], None, ALU.mult),
                        reads=[kh, wk], writes=[kg])
                else:
                    P.op("vector", lambda e, i=i, h=h: e.scalar_tensor_tensor(
                        g32, h, w2[:, i:i + 1], g32, ALU.mult, ALU.add),
                        reads=[kh, wk, kg], writes=[kg])
            P.op("vector", lambda e: e.tensor_scalar(g16, g32, b2, None, ALU.add),
                 reads=[kg, wk], writes=[kg + "c"])
            for c in range(DC):
                bank = 1 + (c % 2)
                P.op("tensor", lambda e, c=c, bank=bank: e.matmul(
                    S.psum[bank][0:128, 0:512], wout[:, c * 128:(c + 1) * 128],
                    g16, start=True, stop=True),
                    reads=[kg + "c", wk], writes=[f"psum{bank}"])
                P.op("scalar", lambda e, c=c, bank=bank: e.activation(
                    bo[c], S.psum[bank][0:128, 0:512], AF.Identity,
                    bias=boutv[:, c:c + 1]),
                    reads=[f"psum{bank}", wk], writes=[bo_keys[c]])
            sw.post_slab_dmajor(slab, b, bo, bo_keys, in_dram[b], out_dram[b])


def stage_l1_ffn(S, consts, in_dram, out_dram):
    nc, P, ar = S.nc, S.P, S.ar
    stag = "l1f"
    ar.reset_stage()
    ident16 = consts["ident16"]

    w_wq = S.inp("L1_ffn_Wq", (D, D))
    w_rt = S.inp("L1_ffn_routers", (255, D))
    w_lv = S.inp("L1_ffn_leaf_values", (256, D))
    w_wo = S.inp("L1_ffn_Wout_w", (D, D))
    w_wb = S.inp("L1_ffn_Wout_b", (D,))
    c_tm = S.inp("c_treeM", (512, 256))

    sw = Sandwich(S, consts, stag)

    wqT = S.wT("L1_ffn_Wq", w_wq, D, D, f"{stag}_wq")
    woT = S.wT("L1_ffn_Wout_w", w_wo, D, D, f"{stag}_wo")
    krt = f"{stag}_rt"
    rtT = S.wT("L1_ffn_routers", w_rt, 255, D, krt, pad_rows=1)

    lv = [ar.salloc(128, D, BF16) for _ in range(2)]
    klv = f"{stag}_lv"
    lv_r = w_lv.rearrange("(m p) c -> m p c", p=128)
    for m in range(2):
        P.dma("gpsimd", lv[m], lv_r[m], writes=[klv])
    tm = [ar.salloc(128, 256, F32) for _ in range(4)]
    ktm = f"{stag}_tm"
    tm_r = c_tm.rearrange("(s p) c -> s p c", p=128)
    for s in range(4):
        P.dma("sync", tm[s], tm_r[s], writes=[ktm])
    wbv = ar.salloc(128, DC, F32)
    kwb = f"{stag}_wb"
    wb_r = w_wb.rearrange("(c p o) -> c p o", p=128, o=1)
    for c in range(DC):
        P.dma("sync", wbv[:, c:c + 1], wb_r[c], writes=[kwb])

    qT = [ar.salloc(128, 512, BF16) for _ in range(DC)]
    u = [ar.salloc(128, 512, F32) for _ in range(4)]
    lp = [ar.salloc(128, 512, BF16) for _ in range(2)]
    ret = [ar.salloc(128, 512, BF16) for _ in range(DC)]
    bo = [ar.salloc(128, 512, F32) for _ in range(DC)]
    bo_keys = [f"{stag}_bo_{c}" for c in range(DC)]

    for b in range(B_CORE):
        sw.preln(in_dram[b], b)
        for slab in range(T // 512):
            sl = slice(slab * 512, (slab + 1) * 512)
            for e in range(DC):
                bank = e % 2
                kq = f"{stag}_q_{e}"
                for k in range(DC):
                    P.op("tensor", lambda e_, k=k, e=e, bank=bank, sl=sl:
                         e_.matmul(S.psum[bank][:, :],
                                   wqT[k][:, e * 128:(e + 1) * 128],
                                   sw.xnT[k][:, sl],
                                   start=(k == 0), stop=(k == DC - 1)),
                         reads=[f"{stag}_wq", sw.xnT_key(k)],
                         writes=[f"psum{bank}"])
                P.op("scalar", lambda e_, e=e, bank=bank: e_.copy(
                    qT[e], S.psum[bank][:, :]),
                    reads=[f"psum{bank}"], writes=[kq])
            for nch in range(2):
                bank = 2 + nch
                kz = f"psum{bank}"
                for k in range(DC):
                    P.op("tensor", lambda e_, k=k, nch=nch, bank=bank:
                         e_.matmul(S.psum[bank][:, :],
                                   rtT[k][:, nch * 128:(nch + 1) * 128],
                                   qT[k], start=(k == 0), stop=(k == DC - 1)),
                         reads=[krt, f"{stag}_q_{k}"], writes=[kz])
                for sgn in range(2):
                    sidx = nch + 2 * sgn
                    ku = f"{stag}_u_{sidx}"
                    P.op("scalar", lambda e_, sidx=sidx, bank=bank, sgn=sgn:
                         e_.activation(u[sidx], S.psum[bank][:, :], AF.Sigmoid,
                                       scale=(1.0 if sgn == 0 else -1.0)),
                         reads=[kz], writes=[ku])
                    P.op("scalar", lambda e_, sidx=sidx: e_.activation(
                        u[sidx], u[sidx], AF.Ln),
                        reads=[ku], writes=[ku])
                    P.op("vector", lambda e_, sidx=sidx: e_.tensor_scalar_max(
                        u[sidx], u[sidx], -80.0),
                        reads=[ku], writes=[ku])
            for mch in range(2):
                bank = 4 + mch
                for s in range(4):
                    P.op("tensor", lambda e_, s=s, mch=mch, bank=bank:
                         e_.matmul(S.psum[bank][:, :],
                                   tm[s][:, mch * 128:(mch + 1) * 128],
                                   u[s], start=(s == 0), stop=(s == 3)),
                         reads=[ktm, f"{stag}_u_{s}"], writes=[f"psum{bank}"])
                P.op("scalar", lambda e_, mch=mch, bank=bank: e_.activation(
                    lp[mch], S.psum[bank][:, :], AF.Exp),
                    reads=[f"psum{bank}"], writes=[f"{stag}_lp_{mch}"])
            for e in range(DC):
                bank = 6 + (e % 2)
                for m in range(2):
                    P.op("tensor", lambda e_, m=m, e=e, bank=bank: e_.matmul(
                        S.psum[bank][:, :], lv[m][:, e * 128:(e + 1) * 128],
                        lp[m], start=(m == 0), stop=(m == 1)),
                        reads=[klv, f"{stag}_lp_{m}"], writes=[f"psum{bank}"])
                P.op("scalar", lambda e_, e=e, bank=bank: e_.copy(
                    ret[e], S.psum[bank][:, :]),
                    reads=[f"psum{bank}"], writes=[f"{stag}_ret_{e}"])
            for e in range(DC):
                bank = e % 2
                for k in range(DC):
                    P.op("tensor", lambda e_, k=k, e=e, bank=bank: e_.matmul(
                        S.psum[bank][:, :], woT[k][:, e * 128:(e + 1) * 128],
                        ret[k], start=(k == 0), stop=(k == DC - 1)),
                        reads=[f"{stag}_wo", f"{stag}_ret_{k}"],
                        writes=[f"psum{bank}"])
                P.op("scalar", lambda e_, e=e, bank=bank: e_.activation(
                    bo[e], S.psum[bank][:, :], AF.Identity,
                    bias=wbv[:, e:e + 1]),
                    reads=[f"psum{bank}", kwb], writes=[bo_keys[e]])
            sw.post_slab_dmajor(slab, b, bo, bo_keys, in_dram[b], out_dram[b])



def stage_l3_attn(S, consts, in_dram, out_dram):
    nc, P, ar = S.nc, S.P, S.ar
    stag = "l3a"
    ar.reset_stage()
    ident16 = consts["ident16"]
    ones = consts["ones"]

    w_td = S.inp("L3_attn_time_decay", (D,))
    w_tf = S.inp("L3_attn_time_first", (D,))
    w_wk = S.inp("L3_attn_Wk", (D, D))
    w_wv = S.inp("L3_attn_Wv", (D, D))
    w_wr = S.inp("L3_attn_Wr", (D, D))
    w_wo = S.inp("L3_attn_Wo", (D, D))
    w_g = S.inp("L3_attn_ln_gamma", (D,))
    w_bta = S.inp("L3_attn_ln_beta", (D,))
    c_iota = S.inp("c_iota_t", (1, T))

    sw = Sandwich(S, consts, stag, xn_f32=True)
    lnr = LNPool(S, stag + "_lnr")

    wkT = S.wT("L3_attn_Wk", w_wk, D, D, f"{stag}_wk")
    wvT = S.wT("L3_attn_Wv", w_wv, D, D, f"{stag}_wv")
    wrT = S.wT("L3_attn_Wr", w_wr, D, D, f"{stag}_wr")
    woT = S.wT("L3_attn_Wo", w_wo, D, D, f"{stag}_wo")

    tdc = ar.salloc(128, DC, F32)
    tfc = ar.salloc(128, DC, F32)
    gcol = ar.salloc(128, DC, F32)
    bcol = ar.salloc(128, DC, F32)
    ltd = ar.salloc(1, D, F32)
    iota = ar.salloc(1, T, F32)
    kc = f"{stag}_c"
    P.dma("sync", iota, c_iota, writes=[kc + "i"])
    td_r = w_td.rearrange("(c p o) -> c p o", p=128, o=1)
    tf_r = w_tf.rearrange("(c p o) -> c p o", p=128, o=1)
    g_r = w_g.rearrange("(c p o) -> c p o", p=128, o=1)
    bta_r = w_bta.rearrange("(c p o) -> c p o", p=128, o=1)
    for c in range(DC):
        P.dma("sync", tdc[:, c:c + 1], td_r[c], writes=[kc + "td"])
        P.dma("sync", tfc[:, c:c + 1], tf_r[c], writes=[kc + "tf"])
        P.dma("sync", gcol[:, c:c + 1], g_r[c], writes=[kc + "g"])
        P.dma("sync", bcol[:, c:c + 1], bta_r[c], writes=[kc + "b"])
    P.op("scalar", lambda e: e.activation(tdc, tdc, AF.Sigmoid),
         reads=[kc + "td"], writes=[kc + "td"])
    P.op("scalar", lambda e: e.activation(tfc, tfc, AF.Sigmoid),
         reads=[kc + "tf"], writes=[kc + "tf"])
    P.dma("sync", ltd, w_td.rearrange("(o c) -> o c", o=1), writes=[kc + "ltd"])
    P.op("scalar", lambda e: e.activation(ltd, ltd, AF.Sigmoid),
         reads=[kc + "ltd"], writes=[kc + "ltd"])
    P.op("scalar", lambda e: e.activation(ltd, ltd, AF.Ln),
         reads=[kc + "ltd"], writes=[kc + "ltd"])

    xnn16 = [ar.salloc(128, D, BF16) for _ in range(2)]

    def inner_ln(i, xn32, kxn):
        r2, nm2, ks2 = lnr.stats(xn32, kxn)
        out = xnn16[i % 2]
        ko = f"{stag}_xnn_{i % 2}"
        P.op("scalar", lambda e: e.activation(out, xn32, AF.Identity,
                                              bias=nm2, scale=r2),
             reads=[kxn, ks2], writes=[ko])
        return out, ko

    HT = T // 2
    kvr2 = [[ar.salloc(128, HT, BF16) for _ in range(3)] for _ in range(2)]
    bufA = ar.salloc(128, HT, F32)
    bufB = ar.salloc(128, HT, F32)
    bufC = ar.salloc(128, HT, F32)
    bufD = bufC
    carry = ar.salloc(128, 2, F32)
    zq = bufB.bitcast(BF16)[:, 0:HT]
    zscr = nc.dram_tensor("l3a_zscr", [DC, 128, T], BF16, kind="Internal").ap()
    zsl = [ar.salloc(128, 512, BF16) for _ in range(DC)]
    bo = [ar.salloc(128, 512, F32) for _ in range(DC)]
    bo_keys = [f"{stag}_bo_{c}" for c in range(DC)]

    for b in range(B_CORE):
        sw.preln(in_dram[b], b, pre_cb=inner_ln)
        for c in range(DC):
            P.op("scalar", lambda e, c=c: e.activation(
                sw.xnT[c], sw.xnT[c], AF.Identity,
                bias=bcol[:, c:c + 1], scale=gcol[:, c:c + 1]),
                reads=[sw.xnT_key(c), kc + "g", kc + "b"],
                writes=[sw.xnT_key(c)])
        for c in range(DC):
            P.op("vector", lambda e: e.memset(carry[:, 0:1], 0.0),
                 writes=[f"{stag}_carry"])
            for hf in range(2):
                h0 = hf * HT
                pu = (c * 2 + hf) % 2
                kvr = kvr2[pu]
                wkeys = [f"{stag}_wk", f"{stag}_wv", f"{stag}_wr"]
                for pi, wT in enumerate((wkT, wvT, wrT)):
                    kp = f"{stag}_kvr_{pu}_{pi}"
                    kwT = wkeys[pi]
                    for s2 in range(HT // 512):
                        bank = s2 % 2
                        sl = slice(h0 + s2 * 512, h0 + (s2 + 1) * 512)
                        for k in range(DC):
                            P.op("tensor", lambda e, wT=wT, k=k, c=c, sl=sl,
                                 bank=bank, kwT=kwT: e.matmul(
                                     S.psum[bank][:, :],
                                     wT[k][:, c * 128:(c + 1) * 128],
                                     sw.xnT[k][:, sl],
                                     start=(k == 0), stop=(k == DC - 1)),
                                 reads=[kwT, sw.xnT_key(k)],
                                 writes=[f"psum{bank}"])
                        P.op("scalar", lambda e, pi=pi, s2=s2, bank=bank,
                             kvr=kvr:
                             e.copy(kvr[pi][:, s2 * 512:(s2 + 1) * 512],
                                    S.psum[bank][:, :]),
                             reads=[f"psum{bank}"], writes=[kp])
                kA, kB, kC = f"{stag}_A", f"{stag}_B", f"{stag}_C"
                kD = kC
                for s2 in range(HT // 512):
                    sl = slice(s2 * 512, (s2 + 1) * 512)
                    gsl = slice(h0 + s2 * 512, h0 + (s2 + 1) * 512)
                    P.op("vector", lambda e, sl=sl, kvr=kvr: e.tensor_mul(
                        bufA[:, sl], kvr[0][:, sl], kvr[1][:, sl]),
                        reads=[f"{stag}_kvr_{pu}_0", f"{stag}_kvr_{pu}_1"],
                        writes=[kA])
                    bank = 2 + s2 % 2
                    P.op("tensor", lambda e, c=c, gsl=gsl, bank=bank: e.matmul(
                        S.psum[bank][:, :], ltd[0:1, c * 128:(c + 1) * 128],
                        iota[0:1, gsl], start=True, stop=True),
                        reads=[kc + "ltd", kc + "i"], writes=[f"psum{bank}"])
                    P.op("scalar", lambda e, sl=sl, bank=bank: e.activation(
                        bufB[:, sl], S.psum[bank][:, :], AF.Exp),
                        reads=[f"psum{bank}"], writes=[kB])
                    P.op("vector", lambda e, sl=sl: e.tensor_mul(
                        bufB[:, sl], bufB[:, sl], bufA[:, sl]),
                        reads=[kA, kB], writes=[kB])
                P.op("vector", lambda e, hf=hf: e.tensor_tensor_scan(
                    bufC, bufB, bufB,
                    (0.0 if hf == 0 else carry[:, 0:1]), ALU.add, ALU.bypass),
                    reads=[kB] + ([] if hf == 0 else [f"{stag}_carry"]),
                    writes=[kC])
                P.op("vector", lambda e: e.tensor_copy(carry[:, 1:2],
                                                       bufC[:, HT - 1:HT]),
                     reads=[kC], writes=[f"{stag}_carry2"])
                P.op("vector", lambda e, hf=hf: (
                    e.memset(bufB[:, 0:1], 0.0) if hf == 0 else
                    e.tensor_copy(bufB[:, 0:1], carry[:, 0:1])),
                    reads=([] if hf == 0 else [f"{stag}_carry"]), writes=[kB])
                P.op("vector", lambda e: e.tensor_copy(bufB[:, 1:HT],
                                                       bufC[:, 0:HT - 1]),
                     reads=[kC], writes=[kB])
                P.op("vector", lambda e, c=c: e.tensor_scalar_mul(
                    bufA, bufA, tfc[:, c:c + 1]),
                    reads=[kA, kc + "tf"], writes=[kA])
                P.op("vector", lambda e, c=c: e.scalar_tensor_tensor(
                    bufD, bufB, tdc[:, c:c + 1], bufA, ALU.mult, ALU.add),
                    reads=[kA, kB, kc + "td"], writes=[kD])
                P.op("scalar", lambda e, kvr=kvr: e.activation(
                    bufA, kvr[2], AF.Sigmoid),
                    reads=[f"{stag}_kvr_{pu}_2"], writes=[kA])
                P.op("vector", lambda e: e.tensor_mul(zq, bufD, bufA),
                    reads=[kA, kD], writes=[kB])
                P.dma("sync", zscr[c][:, h0:h0 + HT], zq,
                      reads=[kB], writes=[f"{stag}_z_{c}_{hf}"])
                P.op("vector", lambda e: e.tensor_copy(carry[:, 0:1],
                                                       carry[:, 1:2]),
                     reads=[f"{stag}_carry2"], writes=[f"{stag}_carry"])
        for slab in range(T // 512):
            sl = slice(slab * 512, (slab + 1) * 512)
            hf = slab // 2
            for k in range(DC):
                P.dma("sync", zsl[k], zscr[k][:, sl],
                      reads=[f"{stag}_z_{k}_{hf}"],
                      writes=[f"{stag}_zsl_{k}"])
            for e0 in range(DC):
                bank = e0 % 2
                for k in range(DC):
                    P.op("tensor", lambda e, k=k, e0=e0, bank=bank:
                         e.matmul(S.psum[bank][:, :],
                                  woT[k][:, e0 * 128:(e0 + 1) * 128],
                                  zsl[k],
                                  start=(k == 0), stop=(k == DC - 1)),
                         reads=[f"{stag}_wo", f"{stag}_zsl_{k}"],
                         writes=[f"psum{bank}"])
                P.op("scalar", lambda e, e0=e0, bank=bank: e.copy(
                    bo[e0], S.psum[bank][:, :]),
                    reads=[f"psum{bank}"], writes=[bo_keys[e0]])
            sw.post_slab_dmajor(slab, b, bo, bo_keys, in_dram[b], out_dram[b])




def stage_l2_attn(S, consts, in_dram, out_dram):
    nc, P, ar = S.nc, S.P, S.ar
    stag = "l2t"
    ar.reset_stage()
    ones = consts["ones"]
    ones16 = consts["ones16"]
    ident16 = consts["ident16"]

    w_wq = S.inp("L2_attn_Wq", (D, D))
    w_bq = S.inp("L2_attn_bq", (D,))
    w_wk = S.inp("L2_attn_Wk", (D, D))
    w_bk = S.inp("L2_attn_bk", (D,))
    w_wv = S.inp("L2_attn_Wv", (D, D))
    w_bv = S.inp("L2_attn_bv", (D,))
    w_wo = S.inp("L2_attn_Wo", (D, D))
    w_bo = S.inp("L2_attn_bo", (D,))
    c_rc = S.inp("c_ropeC", (128, T), BF16)
    c_rs = S.inp("c_ropeS", (128, T), BF16)

    sw = Sandwich(S, consts, stag, use_bt=False)

    kwA, kwB = f"{stag}_wA", f"{stag}_wB"
    wA = [ar.salloc(128, D, BF16) for _ in range(DC)]
    wB = [ar.salloc(128, D, BF16) for _ in range(DC)]
    wsrc = {"L2_attn_Wq": w_wq, "L2_attn_Wk": w_wk,
            "L2_attn_Wv": w_wv, "L2_attn_Wo": w_wo}

    bqc = ar.salloc(128, DC, F32)
    bkc = ar.salloc(128, DC, F32)
    kb = f"{stag}_b"
    for c in range(DC):
        P.dma("sync", bqc[:, c:c + 1],
              w_bq.rearrange("(c p o) -> c p o", p=128, o=1)[c], writes=[kb])
        P.dma("sync", bkc[:, c:c + 1],
              w_bk.rearrange("(c p o) -> c p o", p=128, o=1)[c], writes=[kb])
    bv_row = ar.salloc(1, D, BF16)
    P.dma("gpsimd", bv_row, w_bv.rearrange("(o c) -> o c", o=1), writes=[kb + "v"])
    bo_row = ar.salloc(1, D, BF16)
    P.dma("gpsimd", bo_row, w_bo.rearrange("(o c) -> o c", o=1), writes=[kb + "o"])
    ropeC = ar.salloc(128, T, BF16)
    ropeS = ar.salloc(128, T, BF16)
    P.dma("sync", ropeC, c_rc, writes=["ropeC"])
    P.dma("sync", ropeS, c_rs, writes=["ropeS"])

    k16 = [ar.salloc(128, T, BF16) for _ in range(DC)]
    qx = [ar.salloc(128, 512, BF16) for _ in range(DC)]
    v_tok = [ar.salloc(128, 16 * 65, BF16) for _ in range(NT)]
    rsw = ar.salloc(128, 512, BF16)
    ctxT = [ar.salloc(128, 512, BF16) for _ in range(DC)]
    pT = [ar.salloc(128, 512, BF16) for _ in range(8)]
    recip_row2 = [ar.salloc(1, 512, F32) for _ in range(2)]
    po = ar.salloc(128, D, F32)

    def q_ap(c, qsl):
        if qsl == 0:
            return qx[c]
        return sw.xnT[c][:, (qsl - 1) * 512:qsl * 512]

    def q_key(c, qsl):
        return f"{stag}_qx" if qsl == 0 else sw.xnT_key(c)

    def pT_ap(kt):
        return pT[kt % 8]

    def pT_key(kt):
        return f"{stag}_pT_{kt % 8}"

    def load_w(name, bufs, wk):
        src_r = wsrc[name].rearrange("(r p) c -> r p c", p=128)
        for r in range(DC):
            P.dma("gpsimd", bufs[r], src_r[r], writes=[wk])

    def rope_cs(dst_ap, kd, src_cols):
        for pc in range(4):
            srcp = (pc // 2) * 64 + (1 - pc % 2) * 32
            d0 = (pc // 2) * 64 + (pc % 2) * 32
            P.op("vector", lambda e, srcp=srcp, d0=d0, dst_ap=dst_ap:
                 e.tensor_copy(rsw[d0:d0 + 32, :], dst_ap[srcp:srcp + 32, :]),
                 reads=[kd], writes=[f"{stag}_rsw"])
        P.op("vector", lambda e: e.tensor_mul(
            rsw, rsw, ropeS[:, src_cols]),
            reads=[f"{stag}_rsw", "ropeC"], writes=[f"{stag}_rsw"])
        P.op("vector", lambda e, dst_ap=dst_ap: e.tensor_mul(
            dst_ap, dst_ap, ropeC[:, src_cols]),
            reads=[kd, "ropeC"], writes=[kd])
        P.op("vector", lambda e, dst_ap=dst_ap: e.tensor_add(
            dst_ap, dst_ap, rsw),
            reads=[kd, f"{stag}_rsw"], writes=[kd])

    def emit_B_ctx(b, qsl, h, pending=None):
        ch, off = h // 2, (h % 2) * 64
        kq = q_key(ch, qsl)
        kk = f"{stag}_k_{ch}"
        qh = q_ap(ch, qsl)[off:off + 64, :]
        kh = k16[ch][off:off + 64, :]
        for kth in range(2):
            if kth == 1 and pending is not None:
                pending(); pending = None
            for kt in range(kth * 8, kth * 8 + 8):
                bank = 2 + kt % 4
                P.op("tensor", lambda e, kt=kt, bank=bank, qh=qh, kh=kh:
                     e.matmul(S.psum[bank][:, :],
                              kh[:, kt * 128:(kt + 1) * 128],
                              qh, start=True, stop=True),
                     reads=[kq, kk], writes=[f"psum{bank}"])
                P.op("scalar", lambda e, kt=kt, bank=bank: e.activation(
                    pT_ap(kt), S.psum[bank][:, :], AF.Exp, scale=0.125),
                    reads=[f"psum{bank}"], writes=[pT_key(kt)])
            for kt in range(kth * 8, kth * 8 + 8):
                P.op("tensor", lambda e, kt=kt, h=h: e.matmul(
                    S.psum[6 + h % 2][0:65, 0:512],
                    v_tok[kt][:, h * 65:h * 65 + 65], pT_ap(kt),
                    start=(kt == 0), stop=(kt == NT - 1)),
                    reads=[f"{stag}_vt_{kt}", pT_key(kt)],
                    writes=[f"psum{6 + h % 2}"])
        cb, bb = 6 + h % 2, h % 2
        rr = recip_row2[h % 2]
        krr = f"{stag}_rr_{h % 2}"
        P.op("vector", lambda e, cb=cb, rr=rr: e.reciprocal(
            rr, S.psum[cb][64:65, 0:512]),
            reads=[f"psum{cb}"], writes=[krr])

        def norm(ch=ch, off=off, cb=cb, bb=bb, rr=rr, krr=krr):
            P.op("tensor", lambda e: e.matmul(
                S.psum[bb][0:64, 0:512], ones[0:1, 0:64], rr,
                start=True, stop=True),
                reads=[krr, "ones"], writes=[f"psum{bb}"])
            P.op("scalar", lambda e: e.copy(
                ctxT[ch][off:off + 64, :], S.psum[cb][0:64, 0:512]),
                reads=[f"psum{cb}"], writes=[f"{stag}_ctx_{ch}"])
            P.op("vector", lambda e: e.tensor_mul(
                ctxT[ch][off:off + 64, :], ctxT[ch][off:off + 64, :],
                S.psum[bb][0:64, 0:512]),
                reads=[f"{stag}_ctx_{ch}", f"psum{bb}"],
                writes=[f"{stag}_ctx_{ch}"])
        return norm

    for b in range(B_CORE):
        sw.preln(in_dram[b], b)
        load_w("L2_attn_Wq", wA, kwA)
        load_w("L2_attn_Wk", wB, kwB)
        for c in range(DC):
            for s2 in range(T // 512):
                bank = s2 % 2
                sl = slice(s2 * 512, (s2 + 1) * 512)
                for k in range(DC):
                    P.op("tensor", lambda e, k=k, c=c, sl=sl, bank=bank:
                         e.matmul(S.psum[bank][:, :],
                                  wB[k][:, c * 128:(c + 1) * 128],
                                  sw.xnT[k][:, sl],
                                  start=(k == 0), stop=(k == DC - 1)),
                         reads=[kwB, sw.xnT_key(k)], writes=[f"psum{bank}"])
                P.op("scalar", lambda e, c=c, sl=sl, bank=bank: e.activation(
                    k16[c][:, sl], S.psum[bank][:, :], AF.Identity,
                    bias=bkc[:, c:c + 1]),
                    reads=[f"psum{bank}", kb], writes=[f"{stag}_k_{c}"])
                rope_cs(k16[c][:, sl], f"{stag}_k_{c}", sl)
        load_w("L2_attn_Wv", wB, kwB)
        for i in range(NT):
            kvt = f"{stag}_vt_{i}"
            for hh in range(16):
                P.op("vector", lambda e, i=i, hh=hh: e.memset(
                    v_tok[i][:, hh * 65 + 64:hh * 65 + 65], 1.0),
                    writes=[kvt])
            for half in range(2):
                bank = half
                for k in range(DC):
                    P.op("tensor", lambda e, i=i, k=k, half=half, bank=bank:
                         e.matmul(S.psum[bank][:, :],
                                  sw.xnT[k][:, i * 128:(i + 1) * 128],
                                  wB[k][:, half * 512:(half + 1) * 512],
                                  start=(k == 0), stop=False),
                         reads=[sw.xnT_key(k), kwB], writes=[f"psum{bank}"])
                P.op("tensor", lambda e, half=half, bank=bank: e.matmul(
                    S.psum[bank][:, :], ones16[0:1, 0:128],
                    bv_row[0:1, half * 512:(half + 1) * 512],
                    start=False, stop=True),
                    reads=[kb + "v", "ones16"], writes=[f"psum{bank}"])
                for hh in range(8):
                    h = half * 8 + hh
                    P.op("scalar", lambda e, i=i, h=h, hh=hh, bank=bank:
                         e.copy(v_tok[i][:, h * 65:h * 65 + 64],
                                S.psum[bank][:, hh * 64:(hh + 1) * 64]),
                         reads=[f"psum{bank}"], writes=[kvt])
        for s2 in range(T // 512):
            sl = slice(s2 * 512, (s2 + 1) * 512)
            for c in range(DC):
                bank = c % 2
                for k in range(DC):
                    P.op("tensor", lambda e, k=k, c=c, sl=sl, bank=bank:
                         e.matmul(S.psum[bank][:, :],
                                  wA[k][:, c * 128:(c + 1) * 128],
                                  sw.xnT[k][:, sl],
                                  start=(k == 0), stop=(k == DC - 1)),
                         reads=[kwA, sw.xnT_key(k)], writes=[f"psum{bank}"])
                P.op("scalar", lambda e, c=c, s2=s2, bank=bank: e.activation(
                    q_ap(c, s2), S.psum[bank][:, :], AF.Identity,
                    bias=bqc[:, c:c + 1]),
                    reads=[f"psum{bank}", kb], writes=[q_key(c, s2)])
                rope_cs(q_ap(c, s2), q_key(c, s2), sl)
        load_w("L2_attn_Wo", wB, kwB)
        pending = None
        for qsl in range(T // 512):
            for h in range(16):
                pending = emit_B_ctx(b, qsl, h, pending)
            pending(); pending = None
            for jt in range(4):
                i = qsl * 4 + jt
                kpt = f"{stag}_po"
                for half in range(2):
                    bank = half
                    for k in range(DC):
                        P.op("tensor", lambda e, k=k, jt=jt, half=half,
                             bank=bank: e.matmul(
                                 S.psum[bank][:, :],
                                 ctxT[k][:, jt * 128:(jt + 1) * 128],
                                 wB[k][:, half * 512:(half + 1) * 512],
                                 start=(k == 0), stop=False),
                             reads=[f"{stag}_ctx_{k}", kwB],
                             writes=[f"psum{bank}"])
                    P.op("tensor", lambda e, half=half, bank=bank: e.matmul(
                        S.psum[bank][:, :], ones16[0:1, 0:128],
                        bo_row[0:1, half * 512:(half + 1) * 512],
                        start=False, stop=True),
                        reads=[kb + "o", "ones16"], writes=[f"psum{bank}"])
                    P.op("scalar", lambda e, half=half, bank=bank: e.copy(
                        po[:, half * 512:(half + 1) * 512], S.psum[bank][:, :]),
                        reads=[f"psum{bank}"], writes=[kpt])
                sw.post_tile(i, b, po, kpt, in_dram[b], out_dram[b])



def stage_l3_ffn(S, consts, in_dram, out_dram):
    nc, P, ar = S.nc, S.P, S.ar
    stag = "l3f"
    ar.reset_stage()
    ones16 = consts["ones16"]

    w_gw = S.inp("L3_ffn_gate_w", (D, 8))
    w_gb = S.inp("L3_ffn_bias", (8,))
    w_up = S.inp("L3_ffn_up_w", (8, D, 4096))
    w_upb = S.inp("L3_ffn_up_b", (8, 4096))
    w_dn = S.inp("L3_ffn_down_w", (8, 4096, D))
    w_dnb = S.inp("L3_ffn_down_b", (8, D))
    w_su = S.inp("L3_ffn_sw_up", (D, 4096))
    w_sub = S.inp("L3_ffn_sb_up", (4096,))
    w_sd = S.inp("L3_ffn_sw_down", (4096, D))
    w_sdb = S.inp("L3_ffn_sb_down", (D,))

    sw = Sandwich(S, consts, stag)

    xn_scr = nc.dram_tensor("l3f_xn", [B_CORE, DC, 128, T], BF16,
                            kind="Internal").ap()
    acc = nc.dram_tensor("l3f_acc", [B_CORE, T, D], F32, kind="Internal").ap()

    gate16 = ar.salloc(128, DC * 8, BF16)
    gbrow = ar.salloc(1, 8, BF16)
    w_all = ar.salloc(128, B_CORE * NT * 8, F32)
    gst = ar.salloc(128, 16, F32)
    kg = f"{stag}_g"
    g_r = w_gw.rearrange("(c p) m -> c p m", p=128)
    for c in range(DC):
        P.dma("gpsimd", gate16[:, c * 8:(c + 1) * 8], g_r[c], writes=[kg])
    P.dma("gpsimd", gbrow, w_gb.rearrange("(o c) -> o c", o=1), writes=[kg])

    up_half = [ar.salloc(128, 2048, BF16) for _ in range(DC)]
    dn_half = [ar.salloc(128, D, BF16) for _ in range(16)]
    hact = [ar.salloc(128, 512, BF16) for _ in range(16)]
    eo_sb = ar.salloc(128, 4096, F32)
    xnsl = [[ar.salloc(128, 512, BF16) for _ in range(DC)] for _ in range(2)]
    upb = [ar.salloc(128, 16, F32) for _ in range(2)]
    dnb_row = [ar.salloc(1, D, BF16) for _ in range(2)]
    kw = f"{stag}_w"
    upB_src = [sw.xbuf[0], sw.xbuf[1], sw.bt[0], sw.bt[1],
               sw.xb2[0], sw.xb2[1], S.gscr[0], S.gscr[1]]
    up_sets = [up_half, [a.bitcast(BF16)[:, 0:2048] for a in upB_src]]
    dn_sets = [dn_half,
               [sw.xnT[kk // 2][:, (kk % 2) * 1024:(kk % 2 + 1) * 1024]
                for kk in range(16)]]

    for b in range(B_CORE):
        sw.preln(in_dram[b], b)
        for c in range(DC):
            P.dma("sync", xn_scr[b][c], sw.xnT[c],
                  reads=[sw.xnT_key(c)], writes=[f"{stag}_xs_{b}_{c}"])
        for i in range(NT):
            kgs = f"{stag}_gs"
            for c in range(DC):
                P.op("tensor", lambda e, c=c, i=i: e.matmul(
                    S.psum[6][0:128, 0:8],
                    sw.xnT[c][:, i * 128:(i + 1) * 128],
                    gate16[:, c * 8:(c + 1) * 8],
                    start=(c == 0), stop=False),
                    reads=[sw.xnT_key(c), kg], writes=["psum6"])
            P.op("tensor", lambda e: e.matmul(
                S.psum[6][0:128, 0:8], ones16[0:1, 0:128], gbrow,
                start=False, stop=True),
                reads=[kg, "ones16"], writes=["psum6"])
            sc = gst[:, 0:8]
            P.op("scalar", lambda e: e.activation(sc, S.psum[6][0:128, 0:8],
                                                  AF.Sigmoid),
                 reads=["psum6"], writes=[kgs])
            wcol = w_all[:, (b * NT + i) * 8:(b * NT + i + 1) * 8]
            kwa = f"{stag}_wall"
            msk = gst[:, 8:16]
            m1 = gst[:, 8:9]
            kms = f"{stag}_msk"
            P.op("vector", lambda e, sc=sc, wcol=wcol: e.reduce_max(
                wcol[:, 0:1], sc, axis=AXX), reads=[kgs], writes=[kwa])
            P.op("vector", lambda e, sc=sc, wcol=wcol: e.tensor_scalar(
                msk, sc, wcol[:, 0:1], None, ALU.is_equal),
                reads=[kgs, kwa], writes=[kms])
            P.op("vector", lambda e, sc=sc: e.scalar_tensor_tensor(
                msk, msk, -1e30, sc, ALU.mult, ALU.add),
                reads=[kgs, kms], writes=[kms])
            P.op("vector", lambda e, wcol=wcol: e.reduce_max(
                wcol[:, 1:2], msk, axis=AXX),
                reads=[kms], writes=[kwa])
            P.op("vector", lambda e, sc=sc, wcol=wcol: e.tensor_scalar(
                msk, sc, wcol[:, 1:2], None, ALU.is_ge),
                reads=[kgs, kwa], writes=[kms])
            P.op("vector", lambda e, sc=sc: e.tensor_mul(msk, msk, sc),
                reads=[kgs, kms], writes=[kms])
            P.op("vector", lambda e: e.tensor_scalar(
                gst[:, 8:9], msk, 0.0, 1e-6, ALU.mult, ALU.add,
                accum_out=None), reads=[kms], writes=[f"{stag}_den"]) \
                if False else None
            P.op("vector", lambda e, wcol=wcol: e.reduce_sum(
                wcol[:, 2:3], msk, axis=AXX),
                reads=[kms], writes=[kwa])
            P.op("vector", lambda e, wcol=wcol: e.tensor_scalar_add(
                wcol[:, 2:3], wcol[:, 2:3], 1e-6),
                reads=[kwa], writes=[kwa])
            P.op("vector", lambda e, wcol=wcol: e.reciprocal(
                wcol[:, 3:4], wcol[:, 2:3]),
                reads=[kwa], writes=[kwa])
            P.op("vector", lambda e, wcol=wcol: e.tensor_scalar(
                wcol, msk, wcol[:, 3:4], None, ALU.mult),
                reads=[kms, kwa], writes=[kwa])

    def emit_loads(e_, ih):
        u = (e_ * 2 + ih) % 2
        upS, dnS = up_sets[u], dn_sets[u]
        up_src = w_up[e_] if e_ < 8 else w_su
        dn_src = w_dn[e_] if e_ < 8 else w_sd
        for k in range(DC):
            P.dma("gpsimd", upS[k],
                  up_src[k * 128:(k + 1) * 128,
                         ih * 2048:(ih + 1) * 2048],
                  writes=[kw + f"u{u}_{k}"])
        for kk in range(16):
            P.dma("gpsimd", dnS[kk],
                  dn_src[ih * 2048 + kk * 128:ih * 2048 + (kk + 1) * 128, :],
                  writes=[kw + f"d{u}_{kk}"])
        if e_ < 8:
            ub_r = w_upb[e_].rearrange("(m p o) -> m p o", p=128, o=1)
        else:
            ub_r = w_sub.rearrange("(m p o) -> m p o", p=128, o=1)
        for m in range(16):
            P.dma("sync", upb[u][:, m:m + 1], ub_r[ih * 16 + m],
                  writes=[kw + f"ub{u}"])
        P.dma("gpsimd", dnb_row[u],
              (w_dnb[e_] if e_ < 8 else w_sdb).rearrange("(o c) -> o c", o=1),
              writes=[kw + f"db{u}"])

    emit_loads(0, 0)
    P.barrier()
    for e_ in range(9):
        for ih in range(2):
            u = (e_ * 2 + ih) % 2
            upS, dnS = up_sets[u], dn_sets[u]
            if not (e_ == 0 and ih == 0):
                emit_loads(e_, ih)
            for b in range(B_CORE):
                for slab in range(T // 512):
                    par = slab % 2
                    ksl = f"{stag}_xnsl_{par}"
                    for k in range(DC):
                        P.dma("scalar", xnsl[par][k],
                              xn_scr[b][k][:, slab * 512:(slab + 1) * 512],
                              reads=[f"{stag}_xs_{b}_{k}"],
                              writes=[ksl + f"_{k}"])
                    for m in range(16):
                        bank = m % 2
                        for k in range(DC):
                            P.op("tensor", lambda e, k=k, m=m, bank=bank,
                                 par=par, upS=upS, u=u: e.matmul(
                                     S.psum[bank][:, :],
                                     upS[k][:, m * 128:(m + 1) * 128],
                                     xnsl[par][k],
                                     start=(k == 0), stop=(k == DC - 1)),
                                 reads=[kw + f"u{u}_{k}", ksl + f"_{k}",
                                        ] , writes=[f"psum{bank}"])
                        P.op("scalar", lambda e, m=m, bank=bank, u=u:
                             e.activation(
                                 hact[m], S.psum[bank][:, :], AF.Gelu,
                                 bias=upb[u][:, m:m + 1]),
                             reads=[f"psum{bank}", kw + f"ub{u}"],
                             writes=[f"{stag}_h_{m}"])
                    for tc in range(4):
                        i = slab * 4 + tc
                        for dh in range(2):
                            bank = 2 + (tc * 2 + dh) % 2
                            for kk in range(16):
                                P.op("tensor", lambda e, kk=kk, tc=tc, dh=dh,
                                     bank=bank, ih=ih, dnS=dnS, u=u: e.matmul(
                                         S.psum[bank][:, :],
                                         hact[kk][:, tc * 128:(tc + 1) * 128],
                                         dnS[kk][:, dh * 512:(dh + 1) * 512],
                                         start=(kk == 0),
                                         stop=(kk == 15 and ih == 0)),
                                     reads=[f"{stag}_h_{kk}", kw + f"d{u}_{kk}"],
                                     writes=[f"psum{bank}"])
                            if ih == 1:
                                P.op("tensor", lambda e, dh=dh, bank=bank, u=u:
                                     e.matmul(
                                         S.psum[bank][:, :], ones16[0:1, 0:128],
                                         dnb_row[u][0:1,
                                                    dh * 512:(dh + 1) * 512],
                                         start=False, stop=True),
                                     reads=[kw + f"db{u}", "ones16"],
                                     writes=[f"psum{bank}"])
                            if e_ < 8:
                                P.op("vector", lambda e, tc=tc, dh=dh,
                                     bank=bank, b=b, i=i, e_=e_:
                                     e.tensor_scalar(
                                         eo_sb[:, tc * 1024 + dh * 512:
                                               tc * 1024 + (dh + 1) * 512],
                                         S.psum[bank][:, :],
                                         w_all[:, (b * NT + i) * 8 + e_:
                                               (b * NT + i) * 8 + e_ + 1],
                                         0.1, ALU.mult, ALU.mult),
                                     reads=[f"psum{bank}", f"{stag}_wall"],
                                     writes=[f"{stag}_eos"])
                            else:
                                P.op("vector", lambda e, tc=tc, dh=dh,
                                     bank=bank: e.tensor_scalar_mul(
                                         eo_sb[:, tc * 1024 + dh * 512:
                                               tc * 1024 + (dh + 1) * 512],
                                         S.psum[bank][:, :], 0.1),
                                     reads=[f"psum{bank}"],
                                     writes=[f"{stag}_eos"])
                    first = (e_ == 0 and ih == 0)
                    P.dma("gpsimd",
                          acc[b].rearrange("(s tc p) d -> s p tc d",
                                           tc=4, p=128)[slab],
                          eo_sb.rearrange("p (tc d) -> p tc d", tc=4),
                          reads=[f"{stag}_eos"],
                          writes=[f"{stag}_acc_{b}_{slab}"],
                          **({} if first else {"accum_op": ALU.add}))

    P.barrier()
    fin = [ar.salloc(128, D, F32) for _ in range(2)]
    for b in range(B_CORE):
        for i in range(NT):
            ft = fin[i % 2]
            kf = f"{stag}_fin_{i % 2}"
            P.dma("sync", ft, acc[b][i * 128:(i + 1) * 128, :],
                  reads=[f"{stag}_acc_{b}_{i // 4}"], writes=[kf])
            P.op("vector", lambda e, ft=ft: e.reduce_max(
                gst[:, 10:11], ft, axis=AXX, apply_absolute_value=True),
                reads=[kf], writes=[f"{stag}_fm"])
            P.op("vector", lambda e: e.tensor_scalar_add(
                gst[:, 10:11], gst[:, 10:11], 1e-6),
                reads=[f"{stag}_fm"], writes=[f"{stag}_fm"])
            P.op("vector", lambda e: e.reciprocal(gst[:, 11:12], gst[:, 10:11]),
                 reads=[f"{stag}_fm"], writes=[f"{stag}_fm"])
            P.op("scalar", lambda e, ft=ft: e.activation(
                ft, ft, AF.Identity, scale=gst[:, 11:12]),
                reads=[kf, f"{stag}_fm"], writes=[kf])
            sw.post_tile(i, b, ft, kf, in_dram[b], out_dram[b])



def stage_l2_ffn(S, consts, in_dram, out_dram):
    nc, P, ar = S.nc, S.P, S.ar
    stag = "l2f"
    ar.reset_stage()
    ones = consts["ones"]
    ones16 = consts["ones16"]
    ident16 = consts["ident16"]

    w_wq = S.inp("L2_ffn_W_Q", (D, D))
    w_wk = S.inp("L2_ffn_W_K", (D, D))
    w_wo = S.inp("L2_ffn_W_O", (D, D))
    w_wi = S.inp("L2_ffn_W_inv", (D, D))
    w_bf = S.inp("L2_ffn_b_fwd", (8, D))
    w_bi = S.inp("L2_ffn_b_inv", (8, D))
    w_r1b = S.inp("L2_ffn_router_b1", (D,))
    w_r2 = S.inp("L2_ffn_router_w2", (8, D))
    w_r2b = S.inp("L2_ffn_router_b2", (8,))
    w_al = S.inp("L2_ffn_alpha_bi", (1, 1))
    w_eb = S.inp("L2_ffn_expert_bias", (8,))
    c_invn = S.inp("c_invn", (128, NT))
    c_sel = S.inp("c_sel64", (32, 2048), BF16)

    sw = Sandwich(S, consts, stag)

    kw1, kw2 = f"{stag}_w1", f"{stag}_w2"
    wb1 = [ar.salloc(128, D, BF16) for _ in range(DC)]
    wb2 = [ar.salloc(128, D, BF16) for _ in range(DC)]
    for nm, dap in (("L2_ffn_W_Q", w_wq), ("L2_ffn_W_K", w_wk),
                    ("L2_ffn_W_inv", w_wi), ("L2_ffn_W_O", w_wo)):
        S.prep16(nm, dap, D, D)
    S.prep16("L2_ffn_router_w1", S.inp("L2_ffn_router_w1", (D, D)), D, D)

    def loadT(name, bufs, wk):
        img, ikey = S.w16[name]
        for c in range(DC):
            P.dma("sync", bufs[c], img[:, c * 128:(c + 1) * 128],
                  reads=[ikey], writes=[wk], transpose=True)

    for nm in ("L2_ffn_V_fwd", "L2_ffn_W_fwd", "L2_ffn_U_fwd",
               "L2_ffn_V_inv", "L2_ffn_W_inv_exp", "L2_ffn_U_inv"):
        img = nc.dram_tensor(f"w16_{nm}", [8, D, 64], BF16, kind="Internal").ap()
        P.dma("gpsimd", img, S.inp(nm, (8, D, 64)), writes=[f"w16_{nm}"])
        S.w16[nm] = (img, f"w16_{nm}")

    vk = [ar.salloc(128, 8 * 64, BF16) for _ in range(DC)]
    wk_ = [ar.salloc(128, 8 * 64, BF16) for _ in range(DC)]
    u_sb = [ar.salloc(128, 8 * 64, BF16) for _ in range(DC)]
    uaug = [ar.salloc(128, D, BF16) for _ in range(8)]
    kvw, kua, kusb = f"{stag}_vw", f"{stag}_ua", f"{stag}_usb"

    def load_branch(xv_name, yw_name, u_name, b_ap):
        xv_img, kxi = S.w16[xv_name]
        yw_img, kyi = S.w16[yw_name]
        u_img, kui = S.w16[u_name]
        for c in range(DC):
            P.dma("sync", vk[c].rearrange("p (k m) -> p k m", k=8),
                  xv_img.rearrange("k (c p) m -> c p k m", p=128)[c],
                  reads=[kxi], writes=[kvw])
            P.dma("sync", wk_[c].rearrange("p (k m) -> p k m", k=8),
                  yw_img.rearrange("k (c p) m -> c p k m", p=128)[c],
                  reads=[kyi], writes=[kvw])
            P.dma("scalar", u_sb[c].rearrange("p (k m) -> p k m", k=8),
                  u_img.rearrange("k (c p) m -> c p k m", p=128)[c],
                  reads=[kui], writes=[kusb])
        for k in range(8):
            for c in range(DC):
                bank = c % 2
                P.op("tensor", lambda e, k=k, c=c, bank=bank: e.transpose(
                    S.psum_bf16(bank)[0:64, 0:128],
                    u_sb[c][:, k * 64:(k + 1) * 64], ident16),
                    reads=[kusb, "ident"], writes=[f"psum{bank}"])
                P.op("scalar", lambda e, k=k, c=c, bank=bank: e.copy(
                    uaug[k][0:64, c * 128:(c + 1) * 128],
                    S.psum_bf16(bank)[0:64, 0:128]),
                    reads=[f"psum{bank}"], writes=[kua])
            P.dma("gpsimd", uaug[k][64:65, :],
                  b_ap[k].rearrange("(o c) -> o c", o=1), writes=[kua])

    r1bc = ar.salloc(128, DC, F32)
    r2T = ar.salloc(128, 64, BF16)
    r2nat = ar.salloc(8, D, BF16)
    biasrows = ar.salloc(1, 16, BF16)
    acol = ar.salloc(128, 2, F32)
    invn = ar.salloc(128, NT, F32)
    selc = ar.salloc(32, 2048, BF16)
    alrow = ar.salloc(1, 2, F32)
    kc = f"{stag}_c"
    for c in range(DC):
        P.dma("sync", r1bc[:, c:c + 1],
              w_r1b.rearrange("(c p o) -> c p o", p=128, o=1)[c], writes=[kc])
    P.dma("gpsimd", r2nat, w_r2, writes=[kc + "r2n"])
    for c in range(DC):
        bank = c % 2
        P.op("tensor", lambda e, c=c, bank=bank: e.transpose(
            S.psum_bf16(bank)[0:128, 0:8], r2nat[:, c * 128:(c + 1) * 128],
            ident16[0:8, 0:8]),
            reads=[kc + "r2n", "ident"], writes=[f"psum{bank}"])
        P.op("scalar", lambda e, c=c, bank=bank: e.copy(
            r2T[:, c * 8:(c + 1) * 8], S.psum_bf16(bank)[0:128, 0:8]),
            reads=[f"psum{bank}"], writes=[kc + "r2T"])
    P.dma("gpsimd", biasrows[0:1, 0:8],
          w_r2b.rearrange("(o c) -> o c", o=1), writes=[kc + "br"])
    P.dma("gpsimd", biasrows[0:1, 8:16],
          w_eb.rearrange("(o c) -> o c", o=1), writes=[kc + "br"])
    P.dma("sync", invn, c_invn, writes=[kc + "n"])
    P.dma("sync", selc, c_sel, writes=[kc + "sel"])
    P.dma("sync", alrow[0:1, 0:1], w_al, writes=[kc + "al"])
    P.op("tensor", lambda e: e.matmul(S.psum[0][0:128, 0:1], ones[0:1, 0:128],
                                      alrow[0:1, 0:1], start=True, stop=True),
         reads=[kc + "al", "ones"], writes=["psum0"])
    P.op("scalar", lambda e: e.copy(acol[:, 0:1], S.psum[0][0:128, 0:1]),
         reads=["psum0"], writes=[kc + "ac"])

    qsl2 = [[ar.salloc(128, 512, BF16) for _ in range(DC)] for _ in range(2)]
    ksl2 = [[ar.salloc(128, 512, BF16) for _ in range(DC)] for _ in range(2)]
    qk_scr = nc.dram_tensor("l2f_qk", [2, DC, 128, T], BF16,
                            kind="Internal").ap()
    inv_scr = nc.dram_tensor("l2f_inv", [2, DC, 128, T], BF16,
                             kind="Internal").ap()
    facc = nc.dram_tensor("l2f_facc", [DC, 128, T], F32, kind="Internal").ap()

    ywb2 = [ar.salloc(64, 512, F32) for _ in range(2)]
    cumb2 = [ar.salloc(64, 512, F32) for _ in range(2)]
    prodb2 = [ar.salloc(64, 512, F32) for _ in range(2)]
    carries = ar.salloc(64, 8, F32)
    wts = ar.salloc(128, 4 * 8, F32)
    wts16 = ar.salloc(128, 4 * 8, BF16)
    wrow_sb = ar.salloc(32, 128, BF16)
    fch = [ar.salloc(128, 512, F32) for _ in range(2)]
    bo16 = [ar.salloc(128, 512, BF16) for _ in range(DC)]
    bo32 = [None] * DC
    bo_keys = [f"{stag}_bo32_{c}" for c in range(DC)]

    def h_ap(c):
        return sw.xnT[c][:, 0:512]

    def bo32_ap(c):
        return sw.xnT[c].bitcast(F32)[:, 512:1024]

    def aug_ap(k):
        return sw.xnT[k][:, 512:1024]

    def mm_to_dram(dst_scr, kdst_fn, wbufs, kwb, src_slab_scr, ksrc_fn,
                   from_xnt=False):
        for s2 in range(T // 512):
            sl = slice(s2 * 512, (s2 + 1) * 512)
            qsl = qsl2[s2 % 2]
            kq = f"{stag}_qsl{s2 % 2}"
            if not from_xnt:
                for k in range(DC):
                    P.dma("scalar", qsl[k], src_slab_scr[k][:, sl],
                          reads=[ksrc_fn(k, s2)], writes=[kq])
            for c in range(DC):
                bank = 2 + c % 2
                for k in range(DC):
                    P.op("tensor", lambda e, k=k, c=c, sl=sl, bank=bank,
                         qsl=qsl:
                         e.matmul(S.psum[bank][:, :],
                                  wbufs[k][:, c * 128:(c + 1) * 128],
                                  (sw.xnT[k][:, sl] if from_xnt else qsl[k]),
                                  start=(k == 0), stop=(k == DC - 1)),
                         reads=[kwb, (sw.xnT_key(k) if from_xnt
                                      else kq)],
                         writes=[f"psum{bank}"])
                P.op("scalar", lambda e, c=c, bank=bank: e.copy(
                    fch[c % 2].bitcast(BF16)[:, 0:512], S.psum[bank][:, :]),
                    reads=[f"psum{bank}"], writes=[f"{stag}_fc{c % 2}"])
                P.dma("sync", dst_scr[c][:, sl],
                      fch[c % 2].bitcast(BF16)[:, 0:512],
                      reads=[f"{stag}_fc{c % 2}"], writes=[kdst_fn(c, s2)])

    for b in range(B_CORE):
        P.barrier()
        sw.preln(in_dram[b], b)
        loadT("L2_ffn_W_Q", wb1, kw1)
        loadT("L2_ffn_W_K", wb2, kw2)
        mm_to_dram(qk_scr[0], lambda c, s: f"{stag}_qs_{c}_{s}", wb1, kw1,
                   None, None, from_xnt=True)
        mm_to_dram(qk_scr[1], lambda c, s: f"{stag}_ks_{c}_{s}", wb2, kw2,
                   None, None, from_xnt=True)
        loadT("L2_ffn_W_inv", wb1, kw1)
        mm_to_dram(inv_scr[0], lambda c, s: f"{stag}_is0_{c}_{s}", wb1, kw1,
                   qk_scr[0], lambda k, s: f"{stag}_qs_{k}_{s}")
        mm_to_dram(inv_scr[1], lambda c, s: f"{stag}_is1_{c}_{s}", wb1, kw1,
                   qk_scr[1], lambda k, s: f"{stag}_ks_{k}_{s}")
        loadT("L2_ffn_router_w1", wb2, kw2)
        P.barrier()
        for branch in range(2):
            if branch == 1:
                loadT("L2_ffn_W_O", wb1, kw1)
                load_branch("L2_ffn_W_inv_exp", "L2_ffn_V_inv",
                            "L2_ffn_U_inv", w_bi)
                src_q, src_k = inv_scr[0], inv_scr[1]
                kqf = lambda c, s: f"{stag}_is0_{c}_{s}"
                kkf = lambda c, s: f"{stag}_is1_{c}_{s}"
            else:
                load_branch("L2_ffn_V_fwd", "L2_ffn_W_fwd",
                            "L2_ffn_U_fwd", w_bf)
                src_q, src_k = qk_scr[0], qk_scr[1]
                kqf = lambda c, s: f"{stag}_qs_{c}_{s}"
                kkf = lambda c, s: f"{stag}_ks_{c}_{s}"
            for slab in range(T // 512):
                sl = slice(slab * 512, (slab + 1) * 512)
                sp = slab % 2
                qsl, ksl = qsl2[sp], ksl2[sp]
                kqs, kks = f"{stag}_qsl{sp}", f"{stag}_ksl{sp}"
                for k in range(DC):
                    P.dma("scalar", qsl[k], src_q[k][:, sl],
                          reads=[kqf(k, slab)], writes=[kqs])
                    P.dma("scalar", ksl[k], src_k[k][:, sl],
                          reads=[kkf(k, slab)], writes=[kks])
                for c in range(DC):
                    bank = 4 + c % 2
                    for k in range(DC):
                        P.op("tensor", lambda e, k=k, c=c, bank=bank, qsl=qsl:
                             e.matmul(S.psum[bank][:, :],
                                      wb2[k][:, c * 128:(c + 1) * 128],
                                      qsl[k],
                                      start=(k == 0), stop=(k == DC - 1)),
                             reads=[kw2, kqs], writes=[f"psum{bank}"])
                    P.op("scalar", lambda e, c=c, bank=bank: e.activation(
                        h_ap(c), S.psum[bank][:, :], AF.Gelu,
                        bias=r1bc[:, c:c + 1]),
                        reads=[f"psum{bank}", kc], writes=[f"{stag}_h"])
                for tc in range(4):
                    i = slab * 4 + tc
                    rb = 6 + tc % 2
                    krb = f"psum{rb}"
                    for c in range(DC):
                        P.op("tensor", lambda e, c=c, tc=tc, rb=rb: e.matmul(
                            S.psum[rb][0:128, 0:8],
                            h_ap(c)[:, tc * 128:(tc + 1) * 128],
                            r2T[:, c * 8:(c + 1) * 8],
                            start=(c == 0), stop=False),
                            reads=[f"{stag}_h", kc + "r2T"], writes=[krb])
                    P.op("tensor", lambda e, rb=rb: e.matmul(
                        S.psum[rb][0:128, 0:8], ones16[0:1, 0:128],
                        biasrows[0:1, 0:8], start=False, stop=False),
                        reads=[kc + "br", "ones16"], writes=[krb])
                    P.op("tensor", lambda e, rb=rb: e.matmul(
                        S.psum[rb][0:128, 0:8], ones16[0:1, 0:128],
                        biasrows[0:1, 8:16], start=False, stop=True),
                        reads=[kc + "br", "ones16"], writes=[krb])
                    wt = wts[:, tc * 8:(tc + 1) * 8]
                    kwt = f"{stag}_wt{tc % 2}"
                    P.op("vector", lambda e, wt=wt, rb=rb: e.reduce_max(
                        wt[:, 0:1], S.psum[rb][0:128, 0:8], axis=AXX),
                        reads=[krb], writes=[kwt])
                    P.op("vector", lambda e, wt=wt: e.tensor_scalar_mul(
                        wt[:, 1:2], wt[:, 0:1], -1.0),
                        reads=[kwt], writes=[kwt])
                    P.op("scalar", lambda e, wt=wt, rb=rb: e.activation(
                        wt, S.psum[rb][0:128, 0:8], AF.Exp, bias=wt[:, 1:2]),
                        reads=[krb, kwt], writes=[kwt])
                    P.op("vector", lambda e, wt=wt: e.reduce_sum(
                        acol[:, 1:2], wt, axis=AXX),
                        reads=[kwt], writes=[f"{stag}_den"])
                    P.op("vector", lambda e: e.reciprocal(
                        acol[:, 1:2], acol[:, 1:2]),
                        reads=[f"{stag}_den"], writes=[f"{stag}_den"])
                    P.op("vector", lambda e, i=i: e.tensor_mul(
                        acol[:, 1:2], acol[:, 1:2], invn[:, i:i + 1]),
                        reads=[f"{stag}_den", kc + "n"], writes=[f"{stag}_den"])
                    if branch == 1:
                        P.op("vector", lambda e: e.tensor_mul(
                            acol[:, 1:2], acol[:, 1:2], acol[:, 0:1]),
                            reads=[f"{stag}_den", kc + "ac"],
                            writes=[f"{stag}_den"])
                    P.op("vector", lambda e, wt=wt, tc=tc: e.tensor_scalar(
                        wts16[:, tc * 8:(tc + 1) * 8], wt, acol[:, 1:2], None,
                        ALU.mult),
                        reads=[kwt, f"{stag}_den"], writes=[f"{stag}_wt16"])
                P.op("tensor", lambda e: e.matmul(
                    S.psum[6][0:32, 0:128], wts16[:, 0:32], ident16,
                    start=True, stop=True),
                    reads=[f"{stag}_wt16", "ident"], writes=["psum6"])
                P.op("scalar", lambda e: e.copy(wrow_sb, S.psum[6][0:32, 0:128]),
                     reads=["psum6"], writes=[f"{stag}_wr"])
                for k in range(8):
                    kp = k % 2
                    ywb, cumb, prodb = ywb2[kp], cumb2[kp], prodb2[kp]
                    byw, bxv, bw = 5 - kp * 3, 4 - kp * 1 if False else 0, 0
                    byw = (5, 3)[kp]
                    bxv = (4, 2)[kp]
                    bw = (7, 6)[kp]
                    kyw, kcum, kpr = (f"{stag}_yw{kp}", f"{stag}_cum{kp}",
                                      f"{stag}_pr{kp}")
                    for c in range(DC):
                        P.op("tensor", lambda e, k=k, c=c, byw=byw, ksl=ksl:
                             e.matmul(
                                 S.psum[byw][0:64, 0:512],
                                 wk_[c][:, k * 64:(k + 1) * 64], ksl[c],
                                 start=(c == 0), stop=(c == DC - 1)),
                             reads=[kvw, kks], writes=[f"psum{byw}"])
                        P.op("tensor", lambda e, k=k, c=c, bxv=bxv, qsl=qsl:
                             e.matmul(
                                 S.psum[bxv][0:64, 0:512],
                                 vk[c][:, k * 64:(k + 1) * 64], qsl[c],
                                 start=(c == 0), stop=(c == DC - 1)),
                             reads=[kvw, kqs], writes=[f"psum{bxv}"])
                    P.op("scalar", lambda e, ywb=ywb, byw=byw: e.copy(
                        ywb, S.psum[byw][0:64, 0:512]),
                        reads=[f"psum{byw}"], writes=[kyw])
                    P.op("vector", lambda e, k=k, slab=slab, ywb=ywb,
                         cumb=cumb: e.tensor_tensor_scan(
                             cumb, ywb, ywb,
                             (0.0 if slab == 0 else carries[:, k:k + 1]),
                             ALU.add, ALU.bypass),
                        reads=[kyw] + ([] if slab == 0
                                       else [f"{stag}_cy_{k}"]),
                        writes=[kcum])
                    P.op("vector", lambda e, k=k, cumb=cumb: e.tensor_copy(
                        carries[:, k:k + 1], cumb[:, 511:512]),
                        reads=[kcum], writes=[f"{stag}_cy_{k}"])
                    P.op("vector", lambda e, cumb=cumb, prodb=prodb, bxv=bxv:
                         e.tensor_mul(prodb, cumb, S.psum[bxv][0:64, 0:512]),
                         reads=[kcum, f"psum{bxv}"], writes=[kpr])
                    for tc in range(4):
                        P.op("tensor", lambda e, k=k, tc=tc, bw=bw: e.matmul(
                            S.psum[bw][0:64, tc * 128:(tc + 1) * 128],
                            selc[:, (tc * 8 + k) * 64:(tc * 8 + k) * 64 + 64],
                            wrow_sb, start=True, stop=True),
                            reads=[f"{stag}_wr", kc + "sel"],
                            writes=[f"psum{bw}"])
                    P.op("vector", lambda e, k=k, prodb=prodb, bw=bw:
                         e.tensor_mul(
                             aug_ap(k)[0:64, :], prodb,
                             S.psum[bw][0:64, 0:512]),
                         reads=[kpr, f"psum{bw}"], writes=[f"{stag}_aug"])
                    P.op("scalar", lambda e, k=k, bw=bw: e.copy(
                        aug_ap(k)[64:65, :], S.psum[bw][0:1, 0:512]),
                        reads=[f"psum{bw}"], writes=[f"{stag}_aug"])
                for c in range(DC):
                    bank = c % 2
                    for k in range(8):
                        P.op("tensor", lambda e, k=k, c=c, bank=bank: e.matmul(
                            S.psum[bank][:, :],
                            uaug[k][0:65, c * 128:(c + 1) * 128],
                            aug_ap(k)[0:65, :],
                            start=(k == 0), stop=(k == 7)),
                            reads=[kua, f"{stag}_aug"], writes=[f"psum{bank}"])
                    if branch == 0:
                        P.op("scalar", lambda e, c=c, bank=bank: e.copy(
                            fch[c % 2], S.psum[bank][:, :]),
                            reads=[f"psum{bank}"], writes=[f"{stag}_fc{c % 2}"])
                        P.dma("sync", facc[c][:, sl], fch[c % 2],
                              reads=[f"{stag}_fc{c % 2}"],
                              writes=[f"{stag}_fa_{c}_{slab}"])
                    else:
                        P.dma("sync", fch[c % 2], facc[c][:, sl],
                              reads=[f"{stag}_fa_{c}_{slab}"],
                              writes=[f"{stag}_fc{c % 2}"])
                        P.op("vector", lambda e, c=c, bank=bank: e.tensor_add(
                            bo16[c], fch[c % 2], S.psum[bank][:, :]),
                            reads=[f"psum{bank}", f"{stag}_fc{c % 2}"],
                            writes=[f"{stag}_bo16"])
                if branch == 1:
                    for c in range(DC):
                        bank = 2 + c % 2
                        for k in range(DC):
                            P.op("tensor", lambda e, k=k, c=c, bank=bank:
                                 e.matmul(S.psum[bank][:, :],
                                          wb1[k][:, c * 128:(c + 1) * 128],
                                          bo16[k],
                                          start=(k == 0), stop=(k == DC - 1)),
                                 reads=[kw1, f"{stag}_bo16"],
                                 writes=[f"psum{bank}"])
                        P.op("scalar", lambda e, c=c, bank=bank: e.copy(
                            bo32_ap(c), S.psum[bank][:, :]),
                            reads=[f"psum{bank}"], writes=[bo_keys[c]])
                    sw.post_slab_dmajor(slab, b, [bo32_ap(c) for c in range(DC)],
                                        bo_keys, in_dram[b], out_dram[b])



def stage_l1_attn(S, consts, in_dram, out_dram):
    nc, P, ar = S.nc, S.P, S.ar
    stag = "l1a"
    ar.reset_stage()
    ones = consts["ones"]
    ident16 = consts["ident16"]
    G = 256
    NG = T // G

    w_lnw = S.inp("L1_attn_ln_w", (D,))
    w_lnb = S.inp("L1_attn_ln_b", (D,))
    w_wh = S.inp("L1_attn_Wh", (D, 4096))
    w_bh = S.inp("L1_attn_bh", (4096,))
    w_wqk = S.inp("L1_attn_Wqk", (D, 128))
    w_bqk = S.inp("L1_attn_bqk", (128,))
    w_qkw = S.inp("L1_attn_qk_w", (4, 128))
    w_qkb = S.inp("L1_attn_qk_b", (4, 128))
    w_rel = S.inp("L1_attn_rel_table", (32,))
    w_wo = S.inp("L1_attn_Wout", (2 * D, D))
    w_bo = S.inp("L1_attn_bout", (D,))
    c_bkt = S.inp("c_bucketT", (32, G, G), BF16)
    c_tri = S.inp("c_triuTneg", (G, G))

    sw = Sandwich(S, consts, stag)
    lnr = LNPool(S, stag + "_lnr")
    lnr2 = LNPool(S, stag + "_lnr2")

    kwh, kwo, kwqk = f"{stag}_wh", f"{stag}_wo", f"{stag}_wqk"
    wh = [ar.salloc(128, 4096, BF16) for _ in range(DC)]
    wo = [ar.salloc(128, D, BF16) for _ in range(16)]
    wqk16 = [ar.salloc(128, 128, BF16) for _ in range(DC)]
    wh_r = w_wh.rearrange("(r p) c -> r p c", p=128)
    for r in range(DC):
        P.dma("gpsimd", wh[r], wh_r[r], writes=[kwh])
    wo_r = w_wo.rearrange("(r p) c -> r p c", p=128)
    for r in range(16):
        P.dma("gpsimd", wo[r], wo_r[r], writes=[kwo])
    wqk_r = w_wqk.rearrange("(r p) c -> r p c", p=128)
    for r in range(DC):
        P.dma("gpsimd", wqk16[r], wqk_r[r], writes=[kwqk])

    cols = ar.salloc(128, 64, F32)
    lnw_c = cols[:, 0:8]; lnb_c = cols[:, 8:16]
    bh_c = None
    bhc = ar.salloc(128, 32, F32)
    bqk_c = cols[:, 16:17]
    qkw_c = cols[:, 20:24]
    qkb_c = cols[:, 24:28]
    bo_c = cols[:, 32:40]
    kc = f"{stag}_c"
    for c in range(DC):
        P.dma("sync", lnw_c[:, c:c + 1],
              w_lnw.rearrange("(c p o) -> c p o", p=128, o=1)[c], writes=[kc])
        P.dma("sync", lnb_c[:, c:c + 1],
              w_lnb.rearrange("(c p o) -> c p o", p=128, o=1)[c], writes=[kc])
        P.dma("sync", bo_c[:, c:c + 1],
              w_bo.rearrange("(c p o) -> c p o", p=128, o=1)[c], writes=[kc])
    for m in range(32):
        P.dma("sync", bhc[:, m:m + 1],
              w_bh.rearrange("(m p o) -> m p o", p=128, o=1)[m], writes=[kc])
    P.dma("sync", bqk_c, w_bqk.rearrange("(p o) -> p o", o=1), writes=[kc])
    for h in range(4):
        P.dma("sync", qkw_c[:, h:h + 1],
              w_qkw.rearrange("h (p o) -> h p o", o=1)[h], writes=[kc])
        P.dma("sync", qkb_c[:, h:h + 1],
              w_qkb.rearrange("h (p o) -> h p o", o=1)[h], writes=[kc])

    biasT = [ar.salloc(128, G, F32) for _ in range(2)]
    kbt = f"{stag}_bT"
    sstmp = ar.salloc(128, G, F32)
    relrow = sstmp[0:1, 32:64]
    relbc = sstmp[:, 0:32]
    goutbuf = ar.salloc(128, 16 * G, BF16)
    mbuf = [goutbuf[:, 0:G], goutbuf[:, G:2 * G]]
    tbuf = goutbuf.bitcast(F32)[:, G:2 * G]
    P.dma("sync", relrow, w_rel.rearrange("(o c) -> o c", o=1),
          writes=[kc + "rel"])
    P.op("scalar", lambda e: e.mul(relrow, relrow, float(np.sqrt(128.0))),
         reads=[kc + "rel"], writes=[kc + "rel"])
    P.op("tensor", lambda e: e.matmul(S.psum[0][0:128, 0:32],
                                      ones[0:1, 0:128], relrow,
                                      start=True, stop=True),
         reads=[kc + "rel", "ones"], writes=["psum0"])
    P.op("scalar", lambda e: e.copy(relbc, S.psum[0][0:128, 0:32]),
         reads=["psum0"], writes=[kc + "relb"])
    for jb in range(2):
        P.op("vector", lambda e, jb=jb: e.memset(biasT[jb], 0.0), writes=[kbt])
    for v in range(32):
        for jb in range(2):
            km = f"{stag}_mb{v % 2}"
            P.dma("sync", mbuf[v % 2], c_bkt[v][jb * 128:(jb + 1) * 128, :],
                  writes=[km])
            P.op("vector", lambda e, v=v, jb=jb: e.scalar_tensor_tensor(
                biasT[jb], mbuf[v % 2], relbc[:, v:v + 1], biasT[jb],
                ALU.mult, ALU.add),
                reads=[km, kc + "relb", kbt], writes=[kbt])
    for jb in range(2):
        P.dma("sync", tbuf, c_tri[jb * 128:(jb + 1) * 128, :],
              writes=[kc + "tri"])
        P.op("vector", lambda e, jb=jb: e.tensor_add(biasT[jb], biasT[jb],
                                                     tbuf),
             reads=[kc + "tri", kbt], writes=[kbt])

    xn_scr = nc.dram_tensor("l1a_xn", [B_CORE, DC, 128, T], BF16,
                            kind="Internal").ap()
    xng = [ar.salloc(128, G, BF16) for _ in range(DC)]
    qkg = ar.salloc(128, G, BF16)
    qk4 = [ar.salloc(128, G, BF16) for _ in range(4)]
    lktok = [ar.salloc(128, 128, BF16) for _ in range(2)]
    attn16 = [ar.salloc(128, G, BF16) for _ in range(2)]
    state16 = ar.salloc(128, 2 * D, BF16)
    bo32 = [ar.salloc(128, G, F32) for _ in range(DC)]
    bo_keys = [f"{stag}_bo_{c}" for c in range(DC)]
    xn1buf = sw.xbuf[0]
    kx1 = f"{stag}_prex_0"
    xnn16 = [ar.salloc(128, D, BF16) for _ in range(2)]

    def hv_ap(m):
        return sw.xnT[m // 8][:, (m % 8) * G:(m % 8 + 1) * G]

    def hv_key(m):
        return sw.xnT_key(m // 8)

    def vtok_ap(jb):
        return sw.xnT[4 + jb]

    def vtok_key(jb):
        return sw.xnT_key(4 + jb)

    def st32_ap(e5):
        return sw.xnT[6 + e5 // 2].bitcast(F32)[:, (e5 % 2) * 512:
                                                (e5 % 2 + 1) * 512]

    def st32_key(e5):
        return sw.xnT_key(6 + e5 // 2)

    def gout_ap(ec):
        return goutbuf[:, ec * G:(ec + 1) * G]

    def inner_ln(i, xn32, kxn):
        r2, nm2, ks2 = lnr.stats(xn32, kxn)
        out = xnn16[i % 2]
        ko = f"{stag}_xnn_{i % 2}"
        P.op("scalar", lambda e, xn32=xn32, out=out, r2=r2, nm2=nm2:
             e.activation(out, xn32, AF.Identity, bias=nm2, scale=r2),
             reads=[kxn, ks2], writes=[ko])
        return out, ko

    def pre_add(i, bo_tok, kbt_, xb, kx):
        r3, nm3, ks3 = lnr2.stats(xb, kx)
        P.op("scalar", lambda e: e.activation(xn1buf, xb, AF.Identity,
                                              bias=nm3, scale=r3),
             reads=[kx, ks3], writes=[kx1])
        P.op("vector", lambda e, bo_tok=bo_tok: e.tensor_add(
            bo_tok, bo_tok, xn1buf),
            reads=[kbt_, kx1], writes=[kbt_])

    for b in range(B_CORE):
        P.barrier()
        sw.preln(in_dram[b], b, pre_cb=inner_ln)
        for c in range(DC):
            P.op("scalar", lambda e, c=c: e.activation(
                sw.xnT[c], sw.xnT[c], AF.Identity,
                bias=lnb_c[:, c:c + 1], scale=lnw_c[:, c:c + 1]),
                reads=[sw.xnT_key(c), kc], writes=[sw.xnT_key(c)])
            P.dma("sync", xn_scr[b][c], sw.xnT[c],
                  reads=[sw.xnT_key(c)], writes=[f"{stag}_xs_{b}_{c}"])
        for e5 in range(4):
            P.op("vector", lambda e, e5=e5: e.memset(st32_ap(e5), 0.0),
                 writes=[st32_key(e5)])
        P.op("vector", lambda e: e.memset(state16, 0.0),
             writes=[f"{stag}_s16"])
        for g in range(NG):
            ksl = f"{stag}_xng"
            gs = slice(g * G, (g + 1) * G)
            for c in range(DC):
                P.dma("scalar", xng[c], xn_scr[b][c][:, gs],
                      reads=[f"{stag}_xs_{b}_{c}"], writes=[ksl])
            for m in range(32):
                bank = m % 2
                for k in range(DC):
                    P.op("tensor", lambda e, k=k, m=m, bank=bank:
                         e.matmul(S.psum[bank][0:128, 0:G],
                                  wh[k][:, m * 128:(m + 1) * 128],
                                  xng[k],
                                  start=(k == 0), stop=(k == DC - 1)),
                         reads=[kwh, ksl], writes=[f"psum{bank}"])
                P.op("scalar", lambda e, m=m, bank=bank: e.activation(
                    hv_ap(m), S.psum[bank][0:128, 0:G], AF.Silu,
                    bias=bhc[:, m:m + 1]),
                    reads=[f"psum{bank}", kc], writes=[hv_key(m)])
            for k in range(DC):
                P.op("tensor", lambda e, k=k: e.matmul(
                    S.psum[2][0:128, 0:G], wqk16[k], xng[k],
                    start=(k == 0), stop=(k == DC - 1)),
                    reads=[kwqk, ksl], writes=["psum2"])
            P.op("scalar", lambda e: e.activation(
                qkg, S.psum[2][0:128, 0:G], AF.Silu, bias=bqk_c),
                reads=["psum2", kc], writes=[f"{stag}_qkg"])
            for h in range(4):
                P.op("scalar", lambda e, h=h: e.activation(
                    qk4[h], qkg, AF.Identity, bias=qkb_c[:, h:h + 1],
                    scale=qkw_c[:, h:h + 1]),
                    reads=[f"{stag}_qkg", kc], writes=[f"{stag}_qk4_{h}"])
            for jb in range(2):
                for ec in range(16):
                    bank = 4 + (ec % 2)
                    transpose_128(S, vtok_ap(jb)[:, ec * 128:(ec + 1) * 128],
                                  hv_ap(ec)[:, jb * 128:(jb + 1) * 128],
                                  ident16, bank, hv_key(ec),
                                  vtok_key(jb), BF16,
                                  evac_eng="vector" if ec % 2 else "scalar")
                bank = 4 + (jb % 2)
                transpose_128(S, lktok[jb],
                              qk4[3][:, jb * 128:(jb + 1) * 128],
                              ident16, bank, f"{stag}_qk4_3",
                              f"{stag}_lk_{jb}", BF16,
                              evac_eng="vector")
            for jb in range(2):
                P.op("tensor", lambda e, jb=jb: e.matmul(
                    S.psum[6][0:128, 0:G],
                    qk4[2][:, jb * 128:(jb + 1) * 128], qk4[0],
                    start=True, stop=True),
                    reads=[f"{stag}_qk4_2", f"{stag}_qk4_0"],
                    writes=["psum6"])
                P.op("vector", lambda e, jb=jb: e.scalar_tensor_tensor(
                    sstmp, S.psum[6][0:128, 0:G], 1.0 / G, biasT[jb],
                    ALU.mult, ALU.add),
                    reads=["psum6", kbt], writes=[f"{stag}_sst"])
                P.op("scalar", lambda e: e.activation(sstmp, sstmp, AF.Relu),
                     reads=[f"{stag}_sst"], writes=[f"{stag}_sst"])
                P.op("vector", lambda e, jb=jb: e.tensor_mul(
                    attn16[jb], sstmp, sstmp),
                    reads=[f"{stag}_sst"], writes=[f"{stag}_at_{jb}"])
            for ec in range(16):
                bank = ec % 2
                for jb in range(2):
                    P.op("tensor", lambda e, ec=ec, jb=jb, bank=bank: e.matmul(
                        S.psum[bank][0:128, 0:G],
                        vtok_ap(jb)[:, ec * 128:(ec + 1) * 128], attn16[jb],
                        start=(jb == 0), stop=False),
                        reads=[vtok_key(jb), f"{stag}_at_{jb}"],
                        writes=[f"psum{bank}"])
                P.op("tensor", lambda e, ec=ec, bank=bank: e.matmul(
                    S.psum[bank][0:128, 0:G],
                    state16[:, ec * 128:(ec + 1) * 128], qk4[1],
                    start=False, stop=True),
                    reads=[f"{stag}_s16", f"{stag}_qk4_1"],
                    writes=[f"psum{bank}"])
                P.op("vector", lambda e, ec=ec, bank=bank: e.tensor_mul(
                    gout_ap(ec), S.psum[bank][0:128, 0:G], hv_ap(16 + ec)),
                    reads=[f"psum{bank}", hv_key(16 + ec)],
                    writes=[f"{stag}_go_{ec}"])
            for e5 in range(4):
                bank = 2 + e5 % 2
                for jb in range(2):
                    P.op("tensor", lambda e, e5=e5, jb=jb, bank=bank: e.matmul(
                        S.psum[bank][0:128, 0:512],
                        lktok[jb], vtok_ap(jb)[:, e5 * 512:(e5 + 1) * 512],
                        start=(jb == 0), stop=(jb == 1)),
                        reads=[f"{stag}_lk_{jb}", vtok_key(jb)],
                        writes=[f"psum{bank}"])
                P.op("vector", lambda e, e5=e5, bank=bank: e.scalar_tensor_tensor(
                    st32_ap(e5),
                    S.psum[bank][0:128, 0:512], 1.0 / G,
                    st32_ap(e5),
                    ALU.mult, ALU.add),
                    reads=[f"psum{bank}", st32_key(e5)],
                    writes=[st32_key(e5)])
            for e5 in range(4):
                P.op("vector", lambda e, e5=e5: e.tensor_copy(
                    state16[:, e5 * 512:(e5 + 1) * 512], st32_ap(e5)),
                    reads=[st32_key(e5)], writes=[f"{stag}_s16"])
            for c in range(DC):
                bank = 6 + c % 2
                for k in range(16):
                    P.op("tensor", lambda e, k=k, c=c, bank=bank: e.matmul(
                        S.psum[bank][0:128, 0:G],
                        wo[k][:, c * 128:(c + 1) * 128], gout_ap(k),
                        start=(k == 0), stop=(k == 15)),
                        reads=[kwo, f"{stag}_go_{k}"], writes=[f"psum{bank}"])
                P.op("scalar", lambda e, c=c, bank=bank: e.activation(
                    bo32[c], S.psum[bank][0:128, 0:G], AF.Identity,
                    bias=bo_c[:, c:c + 1]),
                    reads=[f"psum{bank}", kc], writes=[bo_keys[c]])
            sw.post_slab_dmajor(g, b, bo32, bo_keys, in_dram[b], out_dram[b],
                                ncols=G, pre_add_cb=pre_add)



def stage_l0_attn(S, consts, in_dram, out_dram):
    nc, P, ar = S.nc, S.P, S.ar
    stag = "l0a"
    ar.reset_stage()
    ones = consts["ones"]
    consts_id16 = consts["ident16"]

    w_win = S.inp("L0_attn_W_in", (D, D))
    w_wat = S.inp("L0_attn_W_attn", (656, D))
    w_wo = S.inp("L0_attn_W_out", (D, D))
    w_sw = S.inp("L0_attn_scale_weights", (2,))

    CS = (21, 41)
    OFFS = (10, 0)
    COL0 = (0, 328)
    PADR = 20

    sw = Sandwich(S, consts, stag)

    winT = S.wT("L0_attn_W_in", w_win, D, D, f"{stag}_win")
    watT = S.wT("L0_attn_W_attn", w_wat, 656, D, f"{stag}_wat", pad_rows=112)
    woutT = S.wT("L0_attn_W_out", w_wo, D, D, f"{stag}_wout")

    NPAR = 2
    NR = 3
    A = nc.dram_tensor("l0a_A", [NT, 2, 8, 128, 256], BF16,
                       kind="Internal").ap()
    zpad = nc.dram_tensor("l0a_zpad", [2304, D], BF16, kind="Internal").ap()

    zbuf = [ar.salloc(128, T, BF16) for _ in range(DC)]
    kz = f"{stag}_z"
    ztok = [ar.salloc(128, D, BF16) for _ in range(2)]
    zslab = [ar.salloc(128, D, BF16) for _ in range(NR)]
    attn_sb = [ar.salloc(128, 768, F32) for _ in range(NPAR)]
    w16s = [[ar.salloc(128, 8 * c, BF16) for c in CS] for _ in range(NPAR)]
    atb = [[ar.salloc(128, 128, BF16) for _ in range(8)] for _ in range(2)]
    abig = [[ar.salloc(128, 2048, BF16) for _ in range(2)] for _ in range(2)]
    stt8 = None
    bo = [ar.salloc(128, 512, F32) for _ in range(DC)]
    bo_keys = [f"{stag}_bo_{c}" for c in range(DC)]
    stt_ = ar.salloc(128, 8 + NPAR * 16, F32)
    stt8p = [[stt_[:, 8 + p * 16:16 + p * 16],
              stt_[:, 16 + p * 16:24 + p * 16]] for p in range(NPAR)]
    swrow = ar.salloc(1, 4, F32)
    swcol = ar.salloc(128, 2, F32)
    kc = f"{stag}_c"

    def zr_ap(c):
        return sw.xnT[c]

    P.op("vector", lambda e: e.memset(ztok[0], 0.0), writes=[f"{stag}_zt_0"])
    P.dma("gpsimd", zpad[0:PADR, :], ztok[0][0:PADR, :],
          reads=[f"{stag}_zt_0"], writes=[f"{stag}_zz"])
    P.dma("gpsimd", zpad[2068:2196, :], ztok[0],
          reads=[f"{stag}_zt_0"], writes=[f"{stag}_zz"])
    P.dma("gpsimd", zpad[2196:2304, :], ztok[0][0:108, :],
          reads=[f"{stag}_zt_0"], writes=[f"{stag}_zz"])
    P.op("vector", lambda e: e.memset(zbuf[0], 0.0), writes=[kz])
    for i in range(NT):
        for sc in range(2):
            P.dma(DMA_QUEUES[(i * 2 + sc) % 3],
                  A[i][sc].rearrange("h p c -> p h c"),
                  zbuf[0].rearrange("p (h c) -> p h c", h=8),
                  reads=[kz],
                  writes=[f"{stag}_A_{i}_{sc}_{h}" for h in range(8)])
    P.dma("sync", swrow[0:1, 0:2], w_sw.rearrange("(o c) -> o c", o=1),
          writes=[kc])
    P.op("vector", lambda e: e.reduce_max(swrow[0:1, 2:3], swrow[0:1, 0:2],
                                          axis=AXX), reads=[kc], writes=[kc])
    P.op("vector", lambda e: e.tensor_scalar_mul(swrow[0:1, 2:3],
                                                 swrow[0:1, 2:3], -1.0),
         reads=[kc], writes=[kc])
    P.op("scalar", lambda e: e.activation(swrow[0:1, 0:2], swrow[0:1, 0:2],
                                          AF.Exp, bias=swrow[0:1, 2:3],
                                          accum_out=swrow[0:1, 3:4]),
         reads=[kc], writes=[kc])
    P.op("vector", lambda e: e.reciprocal(swrow[0:1, 3:4], swrow[0:1, 3:4]),
         reads=[kc], writes=[kc])
    P.op("vector", lambda e: e.tensor_scalar(swrow[0:1, 0:2], swrow[0:1, 0:2],
                                             swrow[0:1, 3:4], None, ALU.mult),
         reads=[kc], writes=[kc])
    P.op("tensor", lambda e: e.matmul(S.psum[0][0:128, 0:2], ones[0:1, 0:128],
                                      swrow[0:1, 0:2], start=True, stop=True),
         reads=[kc, "ones"], writes=["psum0"])
    P.op("scalar", lambda e: e.copy(swcol, S.psum[0][0:128, 0:2]),
         reads=["psum0"], writes=[kc + "sc"])

    for b in range(B_CORE):
        P.barrier()
        sw.preln(in_dram[b], b)
        for c in range(DC):
            for s2 in range(T // 512):
                bank = s2 % 2
                sl = slice(s2 * 512, (s2 + 1) * 512)
                for k in range(DC):
                    P.op("tensor", lambda e, k=k, c=c, sl=sl, bank=bank:
                         e.matmul(S.psum[bank][:, :],
                                  winT[k][:, c * 128:(c + 1) * 128],
                                  sw.xnT[k][:, sl],
                                  start=(k == 0), stop=(k == DC - 1)),
                         reads=[f"{stag}_win", sw.xnT_key(k)],
                         writes=[f"psum{bank}"])
                P.op("scalar", lambda e, c=c, sl=sl, bank=bank: e.copy(
                    zbuf[c][:, sl], S.psum[bank][:, :]),
                    reads=[f"psum{bank}"], writes=[kz])
        for i in range(NT):
            zt = ztok[i % 2]
            kzt = f"{stag}_zt_{i % 2}"
            for c in range(DC):
                bank = 4 + c % 2
                transpose_128(S, zt[:, c * 128:(c + 1) * 128],
                              zbuf[c][:, i * 128:(i + 1) * 128], 
                              consts["ident16"], bank, kz, kzt, BF16,
                              evac_eng="vector" if c % 2 else "scalar")
            P.dma("gpsimd", zpad[PADR + i * 128:PADR + (i + 1) * 128, :], zt,
                  reads=[kzt], writes=[f"{stag}_zp_{i}"])
        for c in range(DC):
            P.op("scalar", lambda e, c=c: e.activation(
                zr_ap(c), zbuf[c], AF.Relu),
                reads=[kz, sw.xnT_key(c)], writes=[sw.xnT_key(c)])
        for i in range(NT):
            par = i % NPAR
            asb = attn_sb[par]
            kat = f"{stag}_at_{par}"
            for nh in range(2):
                bank = 6 + nh
                ncols = 512 if nh == 0 else 256
                for k in range(DC):
                    P.op("tensor", lambda e, k=k, i=i, nh=nh, bank=bank,
                         ncols=ncols: e.matmul(
                             S.psum[bank][0:128, 0:ncols],
                             zr_ap(k)[:, i * 128:(i + 1) * 128],
                             watT[k][:, nh * 512:nh * 512 + ncols],
                             start=(k == 0), stop=(k == DC - 1)),
                         reads=[sw.xnT_key(k), f"{stag}_wat"],
                         writes=[f"psum{bank}"])
                P.op("scalar", lambda e, nh=nh, bank=bank, ncols=ncols,
                     asb=asb: e.copy(asb[:, nh * 512:nh * 512 + ncols],
                                     S.psum[bank][0:128, 0:ncols]),
                     reads=[f"psum{bank}"], writes=[kat])
            for sc in range(2):
                c = CS[sc]
                kst = f"{stag}_st{par}"
                stt8 = stt8p[par]
                a8 = asb[:, COL0[sc]:COL0[sc] + 8 * c]
                a8v = a8.rearrange("p (h j) -> p h j", h=8)
                P.op("vector", lambda e, a8v=a8v, sc=sc: e.reduce_max(
                    stt8[sc], a8v, axis=AXX),
                    reads=[kat], writes=[kst])
                P.op("vector", lambda e, a8=a8, a8v=a8v, sc=sc, c=c:
                     e.tensor_sub(a8v, a8v,
                                  stt8[sc].unsqueeze(2).broadcast_to(
                                      [128, 8, c])),
                     reads=[kat, kst], writes=[kat])
                P.op("scalar", lambda e, a8=a8: e.activation(a8, a8, AF.Exp),
                     reads=[kat], writes=[kat])
                P.op("vector", lambda e, a8v=a8v, sc=sc: e.reduce_sum(
                    stt8[sc], a8v, axis=AXX),
                    reads=[kat], writes=[kst])
                P.op("vector", lambda e, sc=sc: e.reciprocal(
                    stt8[sc], stt8[sc]),
                    reads=[kst], writes=[kst])
                P.op("vector", lambda e, sc=sc: e.tensor_scalar(
                    stt8[sc], stt8[sc], swcol[:, sc:sc + 1], None, ALU.mult),
                    reads=[kst, kc + "sc"], writes=[kst])
                P.op("vector", lambda e, a8v=a8v, sc=sc, par=par, c=c:
                     e.tensor_mul(
                         w16s[par][sc].rearrange("p (h j) -> p h j", h=8),
                         a8v,
                         stt8[sc].unsqueeze(2).broadcast_to([128, 8, c])),
                     reads=[kat, kst], writes=[f"{stag}_w16_{par}_{sc}"])
                for hg in range(2):
                    base = (((i * 2 + sc) * 8 + hg * 4) * 128 * 256
                            + OFFS[sc])
                    dst = bass.AP(A.tensor, base,
                                  [[257, 128], [128 * 256, 4], [1, c]])
                    src = w16s[par][sc][:, hg * 4 * c:(hg * 4 + 4) * c] \
                        .rearrange("p (h j) -> p h j", h=4)
                    P.dma(DMA_QUEUES[(i * 2 + sc + hg) % 3], dst, src,
                          reads=[f"{stag}_w16_{par}_{sc}"],
                          writes=[f"{stag}_A_{i}_{sc}_{hg * 4 + hh}"
                                  for hh in range(4)])
        def zr_reads(sl_i):
            r = [f"{stag}_zz"]
            for tt in (sl_i - 1, sl_i):
                if 0 <= tt < NT:
                    r.append(f"{stag}_zp_{tt}")
            return r
        for i in range(NT):
            pb = i % 2
            if i == 0:
                P.dma("sync", zslab[0], zpad[0:128, :],
                      reads=zr_reads(0), writes=[f"{stag}_zsl_0"])
            P.dma("sync", zslab[(i + 1) % NR],
                  zpad[(i + 1) * 128:(i + 2) * 128, :],
                  reads=zr_reads(i + 1), writes=[f"{stag}_zsl_{(i + 1) % NR}"])
            for sc in range(2):
                kab = f"{stag}_ab_{pb}_{sc}"
                P.dma(DMA_QUEUES[(i * 2 + sc) % 3],
                      abig[pb][sc].rearrange("p (h c) -> p h c", h=8),
                      A[i][sc].rearrange("h p c -> p h c"),
                      reads=[f"{stag}_A_{i}_{sc}_{h}" for h in range(8)],
                      writes=[kab])
            for h in range(8):
                bank = 2 + h % 2
                hp = h % 2
                for sc in range(2):
                    kab = f"{stag}_ab_{pb}_{sc}"
                    for pc in range(2):
                        transpose_128(
                            S, atb[pb][hp * 4 + sc * 2 + pc],
                            abig[pb][sc][:, h * 256 + pc * 128:
                                         h * 256 + (pc + 1) * 128],
                            consts_id16, 4 + (sc * 2 + pc) % 2, kab,
                            f"{stag}_AT_{pb}_{hp}_{sc}_{pc}", BF16,
                            evac_eng="vector" if pc else "scalar")
                for n4 in range(4):
                    sc, pc = n4 // 2, n4 % 2
                    P.op("tensor", lambda e, h=h, pb=pb, n4=n4, hp=hp,
                         pc=pc, bank=bank, i=i: e.matmul(
                             S.psum[bank][0:128, 0:128],
                             zslab[(i + pc) % NR][:, h * 128:(h + 1) * 128],
                             atb[pb][hp * 4 + n4],
                             start=(n4 == 0), stop=(n4 == 3)),
                         reads=[f"{stag}_AT_{pb}_{hp}_{sc}_{pc}",
                                f"{stag}_zsl_{(i + pc) % NR}"],
                         writes=[f"psum{bank}"])
                P.op("scalar", lambda e, h=h, i=i, bank=bank: e.copy(
                    zbuf[h][:, i * 128:(i + 1) * 128],
                    S.psum[bank][0:128, 0:128]),
                    reads=[f"psum{bank}"], writes=[kz])
        for slab in range(T // 512):
            sl = slice(slab * 512, (slab + 1) * 512)
            for c in range(DC):
                bank = 6 + c % 2
                for k in range(DC):
                    P.op("tensor", lambda e, k=k, c=c, sl=sl, bank=bank:
                         e.matmul(S.psum[bank][:, :],
                                  woutT[k][:, c * 128:(c + 1) * 128],
                                  zbuf[k][:, sl],
                                  start=(k == 0), stop=(k == DC - 1)),
                         reads=[f"{stag}_wout", kz],
                         writes=[f"psum{bank}"])
                P.op("scalar", lambda e, c=c, bank=bank: e.copy(
                    bo[c], S.psum[bank][:, :]),
                    reads=[f"psum{bank}"], writes=[bo_keys[c]])
            sw.post_slab_dmajor(slab, b, bo, bo_keys, in_dram[b], out_dram[b])


LAST_PROG = None

STAGE_FNS = {
    "l0_attn": stage_l0_attn,
    "l0_ffn": stage_l0_ffn,
    "l1_attn": stage_l1_attn,
    "l1_ffn": stage_l1_ffn,
    "l2_attn": stage_l2_attn,
    "l2_ffn": stage_l2_ffn,
    "l3_attn": stage_l3_attn,
    "l3_ffn": stage_l3_ffn,
}

STAGE_ORDER = ["l0_attn", "l0_ffn", "l1_attn", "l1_ffn",
               "l2_attn", "l2_ffn", "l3_attn", "l3_ffn"]


def make_consts():
    c = {}
    c["c_ident16"] = np.eye(128, dtype=np.float32)
    c["c_ones"] = np.ones((1, 128), dtype=np.float32)
    M = np.zeros((512, 256), np.float32)
    for leaf in range(256):
        for lev in range(8):
            j = leaf >> (8 - lev)
            side = (leaf >> (7 - lev)) & 1
            node = (1 << lev) - 1 + j
            M[node if side == 1 else 256 + node, leaf] = 1.0
    c["c_treeM"] = M
    c["c_iota_t"] = np.arange(T, dtype=np.float32).reshape(1, T)
    nv = 1.0 / (np.arange(T, dtype=np.float64) + 1.0)
    c["c_invn"] = nv.reshape(NT, 128).T.astype(np.float32).copy()
    import ml_dtypes as _md
    sel = np.zeros((32, 32 * 64), np.float32)
    for j in range(32):
        sel[j, j * 64:(j + 1) * 64] = 1.0
    c["c_sel64"] = sel.astype(_md.bfloat16)
    G = 256
    pos = np.arange(G)
    rel = pos[None, :] - pos[:, None]
    n = np.maximum(-rel, 0)
    max_exact = 16
    is_small = n < max_exact
    ln_ratio = np.log(128 / max_exact)
    with np.errstate(divide="ignore"):
        val_large = max_exact + (np.log(np.maximum(n, 1) / max_exact)
                                 / ln_ratio * 16).astype(np.int32)
    val_large = np.minimum(val_large, 31)
    bucket = np.where(is_small, n, val_large)
    bktT = np.zeros((32, G, G), np.float32)
    for v in range(32):
        bktT[v] = (bucket == v).T
    c["c_bucketT"] = bktT.astype(_md.bfloat16)
    triu = (rel > 0)
    c["c_triuTneg"] = (np.where(triu, -1e30, 0.0).T).astype(np.float32)
    j = np.arange(32)
    invf = 1.0 / (10000.0 ** (j / 32.0))
    t = np.arange(T)
    ang = np.outer(invf, t)
    cosr = np.cos(ang); sinr = np.sin(ang)
    Cb = np.concatenate([cosr, cosr], 0)
    Sb = np.concatenate([-sinr, sinr], 0)
    import ml_dtypes
    c["c_ropeC"] = np.concatenate([Cb, Cb], 0).astype(ml_dtypes.bfloat16)
    c["c_ropeS"] = np.concatenate([Sb, Sb], 0).astype(ml_dtypes.bfloat16)
    return c


def build_program(stages=None, snapshots=False):
    if stages is None:
        stages = [s for s in STAGE_ORDER if s in STAGE_FNS]
    nc = bass.Bass("TRN2", target_bir_lowering=False)
    inputs_decl = {}

    x_in = nc.dram_tensor("L0_attn_query", [B_CORE, T, D], F32,
                          kind="ExternalInput").ap()
    inputs_decl["L0_attn_query"] = x_in
    y_out = nc.dram_tensor("y", [B_CORE, T, D], F32, kind="ExternalOutput").ap()
    if snapshots:
        snaps = [nc.dram_tensor(f"ysnap{i}", [B_CORE, T, D], F32,
                                kind="ExternalOutput").ap()
                 for i in range(len(stages) - 1)]
    resid = nc.dram_tensor("resid", [B_CORE, T, D], F32, kind="Internal").ap()

    es = ExitStack()
    with es:
        ar = Arena(nc, 212800)
        ar.open()
        es.callback(ar.close)
        psum_cms = [nc.psum_tensor(f"pb{i}", [128, 512], F32) for i in range(8)]
        psum = [es.enter_context(cm).ap() for cm in psum_cms]
        sems = {}
        dma_sems = {}
        for e in ENGINES:
            sems[e] = es.enter_context(nc.semaphore(f"s_{e}"))
        for q in DMA_QUEUES:
            for ln in range(NLANES):
                dma_sems[(q, ln)] = es.enter_context(
                    nc.semaphore(f"d_{q}_{ln}"))

        P = Prog(nc)
        global LAST_PROG
        LAST_PROG = P
        S = Stage(nc, P, ar, psum, inputs_decl)
        S.gscr = [ar.galloc(128, D, F32) for _ in range(2)]

        ident32 = ar.galloc(128, 128, F32)
        ident16 = ar.galloc(128, 128, BF16)
        ones = ar.galloc(1, 128, F32)
        ones16 = ar.galloc(1, 128, BF16)
        ci = S.inp("c_ident16", (128, 128))
        co = S.inp("c_ones", (1, 128))
        P.dma("sync", ident32, ci, writes=["ident"])
        P.dma("gpsimd", ident16, ci, writes=["ident"])
        P.dma("sync", ones, co, writes=["ones"])
        P.dma("gpsimd", ones16, co, writes=["ones16"])
        consts = {"ident16": ident16, "ident32": ident32, "ones": ones,
                  "ones16": ones16}

        for si, st in enumerate(stages):
            if snapshots:
                in_dram = x_in if si == 0 else snaps[si - 1]
                out_dram = y_out if si == len(stages) - 1 else snaps[si]
            else:
                in_dram = x_in if si == 0 else resid
                out_dram = y_out if si == len(stages) - 1 else resid
            STAGE_FNS[st](S, consts, in_dram, out_dram)
            P.barrier()

        nfin = {}
        for q in DMA_QUEUES:
            n = P.dma_count[q]
            for ln in range(NLANES):
                cnt = (n - ln + NLANES - 1) // NLANES
                if cnt:
                    nfin[(q, ln)] = 16 * cnt
        P.finalize(sems, dma_sems)
        with nc.Block() as block:
            P.emit(block)

            @block.sync
            def _(sync):
                for lane, val in nfin.items():
                    sync.wait_ge(dma_sems[lane], val)

    return nc, sorted(inputs_decl.keys())


def make_in_map(inputs, names, consts, core):
    m = {}
    for n in names:
        if n == "L0_attn_query":
            m[n] = np.ascontiguousarray(
                inputs[n][core * B_CORE:(core + 1) * B_CORE])
        elif n.startswith("c_"):
            m[n] = consts[n]
        else:
            a = np.ascontiguousarray(inputs[n])
            if a.ndim == 0:
                a = a.reshape(1, 1)
            m[n] = a
    return m


def kernel(**inputs):
    nc, names = build_program()
    consts = make_consts()
    in_maps = [make_in_map(inputs, names, consts, core) for core in range(8)]
    res = run_bass_kernel_spmd(nc, in_maps, core_ids=list(range(8)))
    return np.concatenate([r["y"] for r in res.results], axis=0)


# version 07 (final)
# speedup vs baseline: 1.3698x; 1.0069x over previous
import math
from contextlib import ExitStack

import numpy as np

import concourse.bass as bass
import concourse.mybir as mybir
from concourse.bass_utils import run_bass_kernel_spmd

F32 = mybir.dt.float32
BF16 = mybir.dt.bfloat16
AF = mybir.ActivationFunctionType
ALU = mybir.AluOpType
AXX = mybir.AxisListType.X

B_CORE = 2
T = 2048
D = 1024
NT = T // 128
DC = D // 128
EPS = 1e-5

DEBUG_TAP = None

ENGINES = ("tensor", "vector", "scalar", "gpsimd", "sync")
DMA_QUEUES = ("sync", "scalar", "gpsimd")
NLANES = 8


class Op:
    __slots__ = ("eng", "fn", "reads", "writes", "seq", "dma", "marked", "inc_idx",
                 "waits", "barrier", "lane", "lane_n", "dbg_out", "dbg_in")

    def __init__(self, eng, fn, reads, writes, dma):
        self.eng = eng
        self.fn = fn
        self.reads = reads
        self.writes = writes
        self.dma = dma
        self.marked = dma
        self.inc_idx = None
        self.waits = []
        self.barrier = None
        self.lane = None
        self.lane_n = None


class Prog:

    def __init__(self, nc):
        self.nc = nc
        self.ops = {e: [] for e in ENGINES}
        self.last_write = {}
        self.readers = {}
        self.pending_barrier = {e: None for e in ENGINES}
        self.all_ops = []
        self.dma_count = {q: 0 for q in DMA_QUEUES}

    def _edge(self, consumer, producer):
        if producer is None or producer is consumer:
            return
        if producer.eng == "tensor" and consumer.eng == "tensor" and not producer.dma:
            return
        consumer.waits.append(producer)

    def op(self, eng, fn, reads=(), writes=(), dma=False):
        o = Op(eng, fn, list(reads), list(writes), dma)
        o.seq = len(self.ops[eng])
        if self.pending_barrier[eng] is not None:
            o.barrier = self.pending_barrier[eng]
            self.pending_barrier[eng] = None
        for k in o.reads:
            self._edge(o, self.last_write.get(k))
        for k in o.writes:
            self._edge(o, self.last_write.get(k))
            for r in self.readers.get(k, ()):
                self._edge(o, r)
        for k in o.reads:
            self.readers.setdefault(k, []).append(o)
        for k in o.writes:
            self.last_write[k] = o
            self.readers[k] = []
        self.ops[eng].append(o)
        self.all_ops.append(o)
        return o

    def dma(self, queue, out, in_, reads=(), writes=(), **kw):
        assert queue in DMA_QUEUES
        o = self.op(queue, lambda e: e.dma_start(out=out, in_=in_, **kw),
                    reads=reads, writes=writes, dma=True)
        o.dbg_out, o.dbg_in = out, in_
        i = self.dma_count[queue]
        self.dma_count[queue] = i + 1
        o.lane = (queue, i % NLANES)
        o.lane_n = i // NLANES + 1
        return o

    def barrier(self):
        snap = {e: len(self.ops[e]) for e in ENGINES}
        for e in ENGINES:
            self.pending_barrier[e] = snap

    def finalize(self, sems, dma_sems):
        for o in self.all_ops:
            for p in o.waits:
                p.marked = True
            if o.barrier is not None:
                for e in ENGINES:
                    n = o.barrier[e]
                    if n > 0:
                        self.ops[e][n - 1].marked = True
        for e in ENGINES:
            n = 0
            for o in self.ops[e]:
                if o.marked and not o.dma:
                    n += 1
                    o.inc_idx = n
        def sem_target(eng, upto_seq):
            res = []
            best_e = 0
            lanes = {}
            for o in self.ops[eng][:upto_seq]:
                if o.dma:
                    lanes[o.lane] = o.lane_n
                elif o.marked:
                    best_e = o.inc_idx
            if best_e:
                res.append((sems[eng], best_e))
            for lane, n in lanes.items():
                res.append((dma_sems[lane], 16 * n))
            return res

        self._sem_target = sem_target
        self.sems = sems
        self.dma_sems = dma_sems

    def emit(self, block):
        sems, dma_sems = self.sems, self.dma_sems
        for e in ENGINES:
            ops = self.ops[e]
            if not ops:
                continue

            def make(eng_name, eng_ops):
                def body(eng):
                    waited = {}
                    def wait(sem, val):
                        if waited.get(id(sem), -1) >= val:
                            return
                        eng.wait_ge(sem, val)
                        waited[id(sem)] = val
                    for o in eng_ops:
                        if o.barrier is not None:
                            for e2 in ENGINES:
                                if e2 == eng_name:
                                    continue
                                for sem, val in self._sem_target(e2, o.barrier[e2]):
                                    wait(sem, val)
                        for p in o.waits:
                            if p.dma:
                                wait(dma_sems[p.lane], 16 * p.lane_n)
                            elif p.eng == eng_name:
                                wait(sems[eng_name], p.inc_idx)
                            else:
                                wait(sems[p.eng], p.inc_idx)
                        if o.dma:
                            if o.lane_n > 1:
                                wait(dma_sems[o.lane], 16 * (o.lane_n - 1))
                        ins = o.fn(eng)
                        if o.marked:
                            if o.dma:
                                ins.then_inc(dma_sems[o.lane], 16)
                            else:
                                ins.then_inc(sems[o.eng], 1)
                return body

            body = make(e, ops)
            getattr(block, e)(body)


class Arena:

    def __init__(self, nc, total_bytes):
        self.t = None
        self.nc = nc
        self.total = total_bytes
        self.lo = 0
        self.hi = total_bytes
        self._cm = None

    def open(self):
        cols = self.total // 4
        self._cm = self.nc.sbuf_tensor("ARENA", [128, cols], F32)
        self.t = self._cm.__enter__()
        return self

    def close(self):
        self._cm.__exit__(None, None, None)

    def _carve(self, off_bytes, parts, cols, dtype):
        esz = 2 if dtype == BF16 else 4
        base = self.t[:, :]
        if dtype == BF16:
            base = self.t.ap().bitcast(BF16)
        else:
            base = self.t.ap()
        start = off_bytes // esz
        return base[0:parts, start:start + cols]

    def galloc(self, parts, cols, dtype=F32):
        esz = 2 if dtype == BF16 else 4
        nb = ((cols * esz + 63) // 64) * 64
        off = self.lo
        self.lo += nb
        assert self.lo <= self.hi, "SBUF arena overflow (global)"
        return self._carve(off, parts, cols, dtype)

    def reset_stage(self):
        self.hi = self.total

    def salloc(self, parts, cols, dtype=F32):
        esz = 2 if dtype == BF16 else 4
        nb = ((cols * esz + 63) // 64) * 64
        self.hi -= nb
        assert self.hi >= self.lo, f"SBUF arena overflow: lo={self.lo} hi={self.hi}"
        return self._carve(self.hi, parts, cols, dtype)


class Stage:

    def __init__(self, nc, P, ar, psum, inputs_decl):
        self.nc = nc
        self.P = P
        self.ar = ar
        self.psum = psum
        self.inputs_decl = inputs_decl
        self.uid = 0
        self.w16 = {}
        self.gscr = None

    def tap(self, name, ap, key, cast=False):
        if DEBUG_TAP != name or getattr(self, "_tapped", False):
            return
        self._tapped = True
        dbg = self.nc.dram_tensor("dbg", [128, T], F32 if not cast else BF16,
                                  kind="ExternalOutput").ap()
        sh = ap.shape
        self.P.dma("sync", dbg[0:sh[0], 0:sh[1]], ap, reads=[key],
                   writes=["dbg"])

    def prep16(self, name, dram_ap, rows, cols, pad_rows=0):
        if name in self.w16:
            return self.w16[name]
        rp = rows + pad_rows
        img = self.nc.dram_tensor(f"w16_{name}", [rp, cols], BF16,
                                  kind="Internal").ap()
        key = f"w16_{name}"
        self.P.dma("gpsimd", img[0:rows, :], dram_ap, writes=[key])
        if pad_rows:
            z = self.ar.salloc(pad_rows, cols, BF16)
            self.P.op("vector", lambda e: e.memset(z, 0.0), writes=[key + "z"])
            self.P.dma("sync", img[rows:rp, :], z,
                       reads=[key + "z"], writes=[key])
        self.w16[name] = (img, key)
        return img, key

    def wN(self, name, dram_ap, rows, cols, wk, bufs=None):
        img, ikey = self.prep16(name, dram_ap, rows, cols)
        nr = rows // 128
        if bufs is None:
            bufs = [self.ar.salloc(128, cols, BF16) for _ in range(nr)]
        img_r = img.rearrange("(r p) c -> r p c", p=128)
        for r in range(nr):
            self.P.dma("sync", bufs[r], img_r[r], reads=[ikey], writes=[wk])
        return bufs

    def wT(self, name, dram_ap, rows, cols, wk, pad_rows=0):
        img, ikey = self.prep16(name, dram_ap, rows, cols, pad_rows)
        rp = rows + pad_rows
        nck = cols // 128
        out = [self.ar.salloc(128, rp, BF16) for _ in range(nck)]
        for c in range(nck):
            self.P.dma("sync", out[c], img[:, c * 128:(c + 1) * 128],
                       reads=[ikey], writes=[wk], transpose=True)
        return out

    def inp(self, name, shape, dtype=F32):
        if name not in self.inputs_decl:
            self.inputs_decl[name] = self.nc.dram_tensor(
                name, list(shape), dtype, kind="ExternalInput").ap()
        return self.inputs_decl[name]

    def psum_bf16(self, i):
        return self.psum[i].bitcast(BF16)

    def u(self):
        self.uid += 1
        return self.uid




class LNPool:

    def __init__(self, S, tag, n=4):
        self.S, self.tag, self.n = S, tag, n
        self.st = [S.ar.salloc(128, 8, F32) for _ in range(n)]
        self.scr = S.gscr
        self.i = 0

    def stats(self, x_ap, key_x):
        S = self.S
        P = S.P
        idx = self.i % self.n
        self.i += 1
        st = self.st[idx]
        ks = f"{self.tag}_st{idx}"
        scr = self.scr[self.i % 2]
        kscr = f"gscr{self.i % 2}"
        sum_ = st[:, 0:1]; sq = st[:, 1:2]; m = st[:, 2:3]
        var = st[:, 3:4]; rstd = st[:, 4:5]; negminv = st[:, 5:6]
        tmp = st[:, 6:7]
        P.op("vector", lambda e: e.reduce_sum(sum_, x_ap, axis=AXX),
             reads=[key_x], writes=[ks])
        P.op("scalar", lambda e: e.activation(scr, x_ap, AF.Square,
                                              accum_out=sq),
             reads=[key_x], writes=[kscr, ks])
        P.op("vector", lambda e: e.tensor_scalar_mul(m, sum_, 1.0 / D),
             reads=[ks], writes=[ks])
        P.op("vector", lambda e: e.tensor_scalar(var, sq, 1.0 / D, EPS,
                                                 ALU.mult, ALU.add),
             reads=[ks], writes=[ks])
        P.op("vector", lambda e: e.tensor_mul(tmp, m, m), reads=[ks], writes=[ks])
        P.op("vector", lambda e: e.tensor_sub(var, var, tmp), reads=[ks], writes=[ks])
        P.op("scalar", lambda e: e.activation(rstd, var, AF.Sqrt),
             reads=[ks], writes=[ks])
        P.op("vector", lambda e: e.reciprocal(rstd, rstd), reads=[ks], writes=[ks])
        P.op("vector", lambda e: e.scalar_tensor_tensor(negminv, m, -1.0, rstd,
                                                        ALU.mult, ALU.mult),
             reads=[ks], writes=[ks])
        return rstd, negminv, ks


def transpose_128(S, out_sbuf_ap, in_sbuf_ap, ident_ap, bank, key_in, key_out,
                  dtype, evac_eng="scalar"):
    P = S.P
    pb = S.psum[bank] if dtype == F32 else S.psum_bf16(bank)
    pk = f"psum{bank}"
    pslice = pb[0:128, 0:128]
    P.op("tensor", lambda e: e.transpose(pslice, in_sbuf_ap, ident_ap),
         reads=[key_in, "ident"], writes=[pk])
    P.op(evac_eng, lambda e: (e.copy(out_sbuf_ap, pslice) if evac_eng == "scalar"
                              else e.tensor_copy(out_sbuf_ap, pslice)),
         reads=[pk], writes=[key_out])


class Sandwich:

    def __init__(self, S, consts, stag, want_xtok=False, xn_f32=False,
                 use_bt=True):
        ar = S.ar
        self.S, self.stag = S, stag
        self.ident16 = consts["ident16"]
        self.ident32 = consts["ident32"]
        self.lnp = LNPool(S, stag + "_lnp")
        self.lnq = LNPool(S, stag + "_lnq")
        self.xbuf = [ar.salloc(128, D, F32) for _ in range(2)]
        self.xn16 = None if want_xtok else [ar.salloc(128, D,
                                            F32 if xn_f32 else BF16)
                                            for _ in range(2)]
        self.bt = [ar.salloc(128, D, F32) for _ in range(2)] if use_bt else None
        self.xb2 = [ar.salloc(128, D, F32) for _ in range(2)]
        self.xnT = [ar.salloc(128, T, BF16) for _ in range(DC)]

    def xnT_key(self, c):
        return f"{self.stag}_xnT_{c}"

    def preln(self, in_dram_b, b, xtok_out=None, xtok_key=None, pre_cb=None):
        S, P, stag = self.S, self.S.P, self.stag
        for i in range(NT):
            xb = self.xbuf[i % 2]
            kx = f"{stag}_prex_{i % 2}"
            P.dma("sync", xb, in_dram_b[i * 128:(i + 1) * 128, :],
                  reads=[f"resid_{b}_{i}"], writes=[kx])
            rstd, negminv, ks = self.lnp.stats(xb, kx)
            if xtok_out is None:
                xn = self.xn16[i % 2]
                kxn = f"{stag}_xn16_{i % 2}"
            else:
                xn = xtok_out[i]
                kxn = xtok_key(i)
            P.op("scalar", lambda e, xb=xb, xn=xn, r=rstd, nm=negminv:
                 e.activation(xn, xb, AF.Identity, bias=nm, scale=r),
                 reads=[kx, ks], writes=[kxn])
            if pre_cb is not None:
                r2 = pre_cb(i, xn, kxn)
                if r2 is not None:
                    xn, kxn = r2
            for c in range(DC):
                bank = 4 + (c % 2)
                transpose_128(S, self.xnT[c][:, i * 128:(i + 1) * 128],
                              xn[:, c * 128:(c + 1) * 128], self.ident16, bank,
                              kxn, self.xnT_key(c), BF16,
                              evac_eng="vector" if c % 2 else "scalar")

    def post_tile(self, i, b, bo_tok_ap, kbt, in_dram_b, out_dram_b,
                  pre_add_cb=None):
        S, P, stag = self.S, self.S.P, self.stag
        xb = self.xb2[i % 2]
        kx = f"{stag}_postx_{i % 2}"
        P.dma("sync", xb, in_dram_b[i * 128:(i + 1) * 128, :],
              reads=[f"resid_{b}_{i}"], writes=[kx])
        if pre_add_cb is not None:
            pre_add_cb(i, bo_tok_ap, kbt, xb, kx)
        rstd, negminv, ks = self.lnq.stats(bo_tok_ap, kbt)
        P.op("scalar", lambda e: e.activation(bo_tok_ap, bo_tok_ap, AF.Identity,
                                              bias=negminv, scale=rstd),
             reads=[kbt, ks], writes=[kbt])
        P.op("vector", lambda e: e.tensor_add(bo_tok_ap, bo_tok_ap, xb),
             reads=[kbt, kx], writes=[kbt])
        P.dma("sync", out_dram_b[i * 128:(i + 1) * 128, :], bo_tok_ap,
              reads=[kbt], writes=[f"resid_{b}_{i}"])

    def post_slab_dmajor(self, slab, b, bo_chunks, bo_keys, in_dram_b,
                         out_dram_b, ncols=512, pre_add_cb=None):
        S, P, stag = self.S, self.S.P, self.stag
        ntile = ncols // 128
        for j in range(ntile):
            i = slab * ntile + j
            bt = self.bt[j % 2]
            kbt = f"{stag}_bt_{j % 2}"
            for c in range(DC):
                bank = 6 + (c % 2)
                transpose_128(S, bt[:, c * 128:(c + 1) * 128],
                              bo_chunks[c][:, j * 128:(j + 1) * 128],
                              self.ident32, bank, bo_keys[c], kbt, F32,
                              evac_eng="vector" if c % 2 else "scalar")
            self.post_tile(i, b, bt, kbt, in_dram_b, out_dram_b,
                           pre_add_cb=pre_add_cb)


def load_wT(S, stag, dram_ap, R, C, ident16, wk):
    P, ar = S.P, S.ar
    assert R % 128 == 0 and C % 128 == 0
    nr, nck = R // 128, C // 128
    out = [ar.salloc(128, R, BF16) for _ in range(nck)]
    tmp = [ar.salloc(128, C, BF16) for _ in range(2)]
    dram_r = dram_ap.rearrange("(r p) c -> r p c", p=128)
    for r in range(nr):
        t = tmp[r % 2]
        kt = f"{wk}_t{r % 2}"
        P.dma("gpsimd", t, dram_r[r], writes=[kt])
        for c in range(nck):
            bank = c % 2
            transpose_128(S, out[c][:, r * 128:(r + 1) * 128],
                          t[:, c * 128:(c + 1) * 128], ident16, bank,
                          kt, wk, BF16,
                          evac_eng="vector" if c % 2 else "scalar")
    return out


def broadcast_row(S, out_ap, row_ap, bank, key_row, key_out, n, cols,
                  ones_ap, evac="vector"):
    P = S.P
    pk = f"psum{bank}"
    for c0 in range(0, cols, 512):
        w = min(512, cols - c0)
        P.op("tensor", lambda e, c0=c0, w=w: e.matmul(
            S.psum[bank][0:n, 0:w],
            ones_ap[0:1, 0:n], row_ap[0:1, c0:c0 + w],
            start=True, stop=True),
            reads=[key_row, "ones"], writes=[pk])
        P.op(evac, lambda e, c0=c0, w=w: (
            e.copy(out_ap[0:n, c0:c0 + w], S.psum[bank][0:n, 0:w])
            if evac == "scalar" else
            e.tensor_copy(out_ap[0:n, c0:c0 + w], S.psum[bank][0:n, 0:w])),
            reads=[pk], writes=[key_out])



def stage_l0_ffn(S, consts, in_dram, out_dram):
    nc, P, ar = S.nc, S.P, S.ar
    stag = "l0f"
    ar.reset_stage()

    w_win = S.inp("L0_ffn_Win", (D, 64))
    w_bin = S.inp("L0_ffn_bin_", (64,))
    w_w1 = S.inp("L0_ffn_W1", (64, 16))
    w_b1 = S.inp("L0_ffn_b1", (64, 16))
    w_w2 = S.inp("L0_ffn_W2", (64, 16))
    w_b2 = S.inp("L0_ffn_b2", (64,))
    w_wout = S.inp("L0_ffn_Wout", (64, D))
    w_bout = S.inp("L0_ffn_bout", (D,))

    sw = Sandwich(S, consts, stag)

    win = ar.salloc(128, DC * 64, BF16)
    wout = ar.salloc(64, D, BF16)
    smalls = ar.salloc(64, 64, F32)
    w1 = smalls[:, 0:16]; b1 = smalls[:, 16:32]; w2 = smalls[:, 32:48]
    bin_ = smalls[:, 48:49]; b2 = smalls[:, 49:50]
    fbv = ar.salloc(64, 16, F32)
    boutv = ar.salloc(128, DC, F32)

    wk = f"{stag}_w"
    win_r = w_win.rearrange("(c p) m -> c p m", p=128)
    for c in range(DC):
        P.dma("gpsimd", win[:, c * 64:(c + 1) * 64], win_r[c], writes=[wk])
    P.dma("gpsimd", wout, w_wout, writes=[wk])
    P.dma("sync", w1, w_w1, writes=[wk])
    P.dma("sync", b1, w_b1, writes=[wk])
    P.dma("sync", w2, w_w2, writes=[wk])
    P.dma("sync", bin_, w_bin.rearrange("(p o) -> p o", o=1), writes=[wk])
    P.dma("sync", b2, w_b2.rearrange("(p o) -> p o", o=1), writes=[wk])
    bout_r = w_bout.rearrange("(c p o) -> c p o", p=128, o=1)
    for c in range(DC):
        P.dma("sync", boutv[:, c:c + 1], bout_r[c], writes=[wk])
    P.op("vector", lambda e: e.tensor_mul(fbv, w1, bin_.broadcast_to([64, 16])),
         reads=[wk], writes=[f"{stag}_fb"])
    P.op("vector", lambda e: e.tensor_add(fbv, fbv, b1),
         reads=[wk, f"{stag}_fb"], writes=[f"{stag}_fb"])

    hbuf = [ar.salloc(64, 512, F32) for _ in range(2)]
    g32 = ar.salloc(64, 512, F32)
    g16 = ar.salloc(64, 512, BF16)
    bo = [ar.salloc(128, 512, F32) for _ in range(DC)]
    bo_keys = [f"{stag}_bo_{c}" for c in range(DC)]

    for b in range(B_CORE):
        sw.preln(in_dram[b], b)
        for slab in range(T // 512):
            sl = slice(slab * 512, (slab + 1) * 512)
            pk = "psum0"
            for c in range(DC):
                P.op("tensor", lambda e, c=c, sl=sl: e.matmul(
                    S.psum[0][0:64, 0:512], win[:, c * 64:(c + 1) * 64],
                    sw.xnT[c][:, sl], start=(c == 0), stop=(c == DC - 1)),
                    reads=[wk, sw.xnT_key(c)], writes=[pk])
            kg = f"{stag}_g"
            for i in range(16):
                h = hbuf[i % 2]
                kh = f"{stag}_h{i % 2}"
                P.op("scalar", lambda e, i=i, h=h: e.activation(
                    h, S.psum[0][0:64, 0:512], AF.Relu,
                    bias=fbv[:, i:i + 1], scale=w1[:, i:i + 1]),
                    reads=[pk, f"{stag}_fb", wk], writes=[kh])
                if i == 0:
                    P.op("vector", lambda e, h=h: e.tensor_scalar(
                        g32, h, w2[:, 0:1], None, ALU.mult),
                        reads=[kh, wk], writes=[kg])
                else:
                    P.op("vector", lambda e, i=i, h=h: e.scalar_tensor_tensor(
                        g32, h, w2[:, i:i + 1], g32, ALU.mult, ALU.add),
                        reads=[kh, wk, kg], writes=[kg])
            P.op("vector", lambda e: e.tensor_scalar(g16, g32, b2, None, ALU.add),
                 reads=[kg, wk], writes=[kg + "c"])
            for c in range(DC):
                bank = 1 + (c % 2)
                P.op("tensor", lambda e, c=c, bank=bank: e.matmul(
                    S.psum[bank][0:128, 0:512], wout[:, c * 128:(c + 1) * 128],
                    g16, start=True, stop=True),
                    reads=[kg + "c", wk], writes=[f"psum{bank}"])
                P.op("scalar", lambda e, c=c, bank=bank: e.activation(
                    bo[c], S.psum[bank][0:128, 0:512], AF.Identity,
                    bias=boutv[:, c:c + 1]),
                    reads=[f"psum{bank}", wk], writes=[bo_keys[c]])
            sw.post_slab_dmajor(slab, b, bo, bo_keys, in_dram[b], out_dram[b])


def stage_l1_ffn(S, consts, in_dram, out_dram):
    nc, P, ar = S.nc, S.P, S.ar
    stag = "l1f"
    ar.reset_stage()
    ident16 = consts["ident16"]

    w_wq = S.inp("L1_ffn_Wq", (D, D))
    w_rt = S.inp("L1_ffn_routers", (255, D))
    w_lv = S.inp("L1_ffn_leaf_values", (256, D))
    w_wo = S.inp("L1_ffn_Wout_w", (D, D))
    w_wb = S.inp("L1_ffn_Wout_b", (D,))
    c_tm = S.inp("c_treeM", (512, 256))

    sw = Sandwich(S, consts, stag)

    wqT = S.wT("L1_ffn_Wq", w_wq, D, D, f"{stag}_wq")
    woT = S.wT("L1_ffn_Wout_w", w_wo, D, D, f"{stag}_wo")
    krt = f"{stag}_rt"
    rtT = S.wT("L1_ffn_routers", w_rt, 255, D, krt, pad_rows=1)

    lv = [ar.salloc(128, D, BF16) for _ in range(2)]
    klv = f"{stag}_lv"
    lv_r = w_lv.rearrange("(m p) c -> m p c", p=128)
    for m in range(2):
        P.dma("gpsimd", lv[m], lv_r[m], writes=[klv])
    tm = [ar.salloc(128, 256, F32) for _ in range(4)]
    ktm = f"{stag}_tm"
    tm_r = c_tm.rearrange("(s p) c -> s p c", p=128)
    for s in range(4):
        P.dma("sync", tm[s], tm_r[s], writes=[ktm])
    wbv = ar.salloc(128, DC, F32)
    kwb = f"{stag}_wb"
    wb_r = w_wb.rearrange("(c p o) -> c p o", p=128, o=1)
    for c in range(DC):
        P.dma("sync", wbv[:, c:c + 1], wb_r[c], writes=[kwb])

    qT = [ar.salloc(128, 512, BF16) for _ in range(DC)]
    u = [ar.salloc(128, 512, F32) for _ in range(4)]
    lp = [ar.salloc(128, 512, BF16) for _ in range(2)]
    ret = [ar.salloc(128, 512, BF16) for _ in range(DC)]
    bo = [ar.salloc(128, 512, F32) for _ in range(DC)]
    bo_keys = [f"{stag}_bo_{c}" for c in range(DC)]

    for b in range(B_CORE):
        sw.preln(in_dram[b], b)
        for slab in range(T // 512):
            sl = slice(slab * 512, (slab + 1) * 512)
            for e in range(DC):
                bank = e % 2
                kq = f"{stag}_q_{e}"
                for k in range(DC):
                    P.op("tensor", lambda e_, k=k, e=e, bank=bank, sl=sl:
                         e_.matmul(S.psum[bank][:, :],
                                   wqT[k][:, e * 128:(e + 1) * 128],
                                   sw.xnT[k][:, sl],
                                   start=(k == 0), stop=(k == DC - 1)),
                         reads=[f"{stag}_wq", sw.xnT_key(k)],
                         writes=[f"psum{bank}"])
                P.op("scalar", lambda e_, e=e, bank=bank: e_.copy(
                    qT[e], S.psum[bank][:, :]),
                    reads=[f"psum{bank}"], writes=[kq])
            for nch in range(2):
                bank = 2 + nch
                kz = f"psum{bank}"
                for k in range(DC):
                    P.op("tensor", lambda e_, k=k, nch=nch, bank=bank:
                         e_.matmul(S.psum[bank][:, :],
                                   rtT[k][:, nch * 128:(nch + 1) * 128],
                                   qT[k], start=(k == 0), stop=(k == DC - 1)),
                         reads=[krt, f"{stag}_q_{k}"], writes=[kz])
                for sgn in range(2):
                    sidx = nch + 2 * sgn
                    ku = f"{stag}_u_{sidx}"
                    P.op("scalar", lambda e_, sidx=sidx, bank=bank, sgn=sgn:
                         e_.activation(u[sidx], S.psum[bank][:, :], AF.Sigmoid,
                                       scale=(1.0 if sgn == 0 else -1.0)),
                         reads=[kz], writes=[ku])
                    P.op("scalar", lambda e_, sidx=sidx: e_.activation(
                        u[sidx], u[sidx], AF.Ln),
                        reads=[ku], writes=[ku])
                    P.op("vector", lambda e_, sidx=sidx: e_.tensor_scalar_max(
                        u[sidx], u[sidx], -80.0),
                        reads=[ku], writes=[ku])
            for mch in range(2):
                bank = 4 + mch
                for s in range(4):
                    P.op("tensor", lambda e_, s=s, mch=mch, bank=bank:
                         e_.matmul(S.psum[bank][:, :],
                                   tm[s][:, mch * 128:(mch + 1) * 128],
                                   u[s], start=(s == 0), stop=(s == 3)),
                         reads=[ktm, f"{stag}_u_{s}"], writes=[f"psum{bank}"])
                P.op("scalar", lambda e_, mch=mch, bank=bank: e_.activation(
                    lp[mch], S.psum[bank][:, :], AF.Exp),
                    reads=[f"psum{bank}"], writes=[f"{stag}_lp_{mch}"])
            for e in range(DC):
                bank = 6 + (e % 2)
                for m in range(2):
                    P.op("tensor", lambda e_, m=m, e=e, bank=bank: e_.matmul(
                        S.psum[bank][:, :], lv[m][:, e * 128:(e + 1) * 128],
                        lp[m], start=(m == 0), stop=(m == 1)),
                        reads=[klv, f"{stag}_lp_{m}"], writes=[f"psum{bank}"])
                P.op("scalar", lambda e_, e=e, bank=bank: e_.copy(
                    ret[e], S.psum[bank][:, :]),
                    reads=[f"psum{bank}"], writes=[f"{stag}_ret_{e}"])
            for e in range(DC):
                bank = e % 2
                for k in range(DC):
                    P.op("tensor", lambda e_, k=k, e=e, bank=bank: e_.matmul(
                        S.psum[bank][:, :], woT[k][:, e * 128:(e + 1) * 128],
                        ret[k], start=(k == 0), stop=(k == DC - 1)),
                        reads=[f"{stag}_wo", f"{stag}_ret_{k}"],
                        writes=[f"psum{bank}"])
                P.op("scalar", lambda e_, e=e, bank=bank: e_.activation(
                    bo[e], S.psum[bank][:, :], AF.Identity,
                    bias=wbv[:, e:e + 1]),
                    reads=[f"psum{bank}", kwb], writes=[bo_keys[e]])
            sw.post_slab_dmajor(slab, b, bo, bo_keys, in_dram[b], out_dram[b])



def stage_l3_attn(S, consts, in_dram, out_dram):
    nc, P, ar = S.nc, S.P, S.ar
    stag = "l3a"
    ar.reset_stage()
    ident16 = consts["ident16"]
    ones = consts["ones"]

    w_td = S.inp("L3_attn_time_decay", (D,))
    w_tf = S.inp("L3_attn_time_first", (D,))
    w_wk = S.inp("L3_attn_Wk", (D, D))
    w_wv = S.inp("L3_attn_Wv", (D, D))
    w_wr = S.inp("L3_attn_Wr", (D, D))
    w_wo = S.inp("L3_attn_Wo", (D, D))
    w_g = S.inp("L3_attn_ln_gamma", (D,))
    w_bta = S.inp("L3_attn_ln_beta", (D,))
    c_iota = S.inp("c_iota_t", (1, T))

    sw = Sandwich(S, consts, stag, xn_f32=True)
    lnr = LNPool(S, stag + "_lnr")

    wkT = S.wT("L3_attn_Wk", w_wk, D, D, f"{stag}_wk")
    wvT = S.wT("L3_attn_Wv", w_wv, D, D, f"{stag}_wv")
    wrT = S.wT("L3_attn_Wr", w_wr, D, D, f"{stag}_wr")
    woT = S.wT("L3_attn_Wo", w_wo, D, D, f"{stag}_wo")

    tdc = ar.salloc(128, DC, F32)
    tfc = ar.salloc(128, DC, F32)
    gcol = ar.salloc(128, DC, F32)
    bcol = ar.salloc(128, DC, F32)
    ltd = ar.salloc(1, D, F32)
    iota = ar.salloc(1, T, F32)
    kc = f"{stag}_c"
    P.dma("sync", iota, c_iota, writes=[kc + "i"])
    td_r = w_td.rearrange("(c p o) -> c p o", p=128, o=1)
    tf_r = w_tf.rearrange("(c p o) -> c p o", p=128, o=1)
    g_r = w_g.rearrange("(c p o) -> c p o", p=128, o=1)
    bta_r = w_bta.rearrange("(c p o) -> c p o", p=128, o=1)
    for c in range(DC):
        P.dma("sync", tdc[:, c:c + 1], td_r[c], writes=[kc + "td"])
        P.dma("sync", tfc[:, c:c + 1], tf_r[c], writes=[kc + "tf"])
        P.dma("sync", gcol[:, c:c + 1], g_r[c], writes=[kc + "g"])
        P.dma("sync", bcol[:, c:c + 1], bta_r[c], writes=[kc + "b"])
    P.op("scalar", lambda e: e.activation(tdc, tdc, AF.Sigmoid),
         reads=[kc + "td"], writes=[kc + "td"])
    P.op("scalar", lambda e: e.activation(tfc, tfc, AF.Sigmoid),
         reads=[kc + "tf"], writes=[kc + "tf"])
    P.dma("sync", ltd, w_td.rearrange("(o c) -> o c", o=1), writes=[kc + "ltd"])
    P.op("scalar", lambda e: e.activation(ltd, ltd, AF.Sigmoid),
         reads=[kc + "ltd"], writes=[kc + "ltd"])
    P.op("scalar", lambda e: e.activation(ltd, ltd, AF.Ln),
         reads=[kc + "ltd"], writes=[kc + "ltd"])

    xnn16 = [ar.salloc(128, D, BF16) for _ in range(2)]

    def inner_ln(i, xn32, kxn):
        r2, nm2, ks2 = lnr.stats(xn32, kxn)
        out = xnn16[i % 2]
        ko = f"{stag}_xnn_{i % 2}"
        P.op("scalar", lambda e: e.activation(out, xn32, AF.Identity,
                                              bias=nm2, scale=r2),
             reads=[kxn, ks2], writes=[ko])
        return out, ko

    HT = T // 2
    kvr2 = [[ar.salloc(128, HT, BF16) for _ in range(3)] for _ in range(2)]
    bufA = ar.salloc(128, HT, F32)
    bufB = ar.salloc(128, HT, F32)
    bufC = ar.salloc(128, HT, F32)
    bufD = bufC
    carry = ar.salloc(128, 2, F32)
    zq = bufB.bitcast(BF16)[:, 0:HT]
    zscr = nc.dram_tensor("l3a_zscr", [DC, 128, T], BF16, kind="Internal").ap()
    zsl = [ar.salloc(128, 512, BF16) for _ in range(DC)]
    bo = [ar.salloc(128, 512, F32) for _ in range(DC)]
    bo_keys = [f"{stag}_bo_{c}" for c in range(DC)]

    for b in range(B_CORE):
        sw.preln(in_dram[b], b, pre_cb=inner_ln)
        for c in range(DC):
            P.op("scalar", lambda e, c=c: e.activation(
                sw.xnT[c], sw.xnT[c], AF.Identity,
                bias=bcol[:, c:c + 1], scale=gcol[:, c:c + 1]),
                reads=[sw.xnT_key(c), kc + "g", kc + "b"],
                writes=[sw.xnT_key(c)])
        for c in range(DC):
            P.op("vector", lambda e: e.memset(carry[:, 0:1], 0.0),
                 writes=[f"{stag}_carry"])
            for hf in range(2):
                h0 = hf * HT
                pu = (c * 2 + hf) % 2
                kvr = kvr2[pu]
                wkeys = [f"{stag}_wk", f"{stag}_wv", f"{stag}_wr"]
                for pi, wT in enumerate((wkT, wvT, wrT)):
                    kp = f"{stag}_kvr_{pu}_{pi}"
                    kwT = wkeys[pi]
                    for s2 in range(HT // 512):
                        bank = s2 % 2
                        sl = slice(h0 + s2 * 512, h0 + (s2 + 1) * 512)
                        for k in range(DC):
                            P.op("tensor", lambda e, wT=wT, k=k, c=c, sl=sl,
                                 bank=bank, kwT=kwT: e.matmul(
                                     S.psum[bank][:, :],
                                     wT[k][:, c * 128:(c + 1) * 128],
                                     sw.xnT[k][:, sl],
                                     start=(k == 0), stop=(k == DC - 1)),
                                 reads=[kwT, sw.xnT_key(k)],
                                 writes=[f"psum{bank}"])
                        P.op("scalar", lambda e, pi=pi, s2=s2, bank=bank,
                             kvr=kvr:
                             e.copy(kvr[pi][:, s2 * 512:(s2 + 1) * 512],
                                    S.psum[bank][:, :]),
                             reads=[f"psum{bank}"], writes=[kp])
                kA, kB, kC = f"{stag}_A", f"{stag}_B", f"{stag}_C"
                kD = kC
                for s2 in range(HT // 512):
                    sl = slice(s2 * 512, (s2 + 1) * 512)
                    gsl = slice(h0 + s2 * 512, h0 + (s2 + 1) * 512)
                    P.op("vector", lambda e, sl=sl, kvr=kvr: e.tensor_mul(
                        bufA[:, sl], kvr[0][:, sl], kvr[1][:, sl]),
                        reads=[f"{stag}_kvr_{pu}_0", f"{stag}_kvr_{pu}_1"],
                        writes=[kA])
                    bank = 2 + s2 % 2
                    P.op("tensor", lambda e, c=c, gsl=gsl, bank=bank: e.matmul(
                        S.psum[bank][:, :], ltd[0:1, c * 128:(c + 1) * 128],
                        iota[0:1, gsl], start=True, stop=True),
                        reads=[kc + "ltd", kc + "i"], writes=[f"psum{bank}"])
                    P.op("scalar", lambda e, sl=sl, bank=bank: e.activation(
                        bufB[:, sl], S.psum[bank][:, :], AF.Exp),
                        reads=[f"psum{bank}"], writes=[kB])
                    P.op("vector", lambda e, sl=sl: e.tensor_mul(
                        bufB[:, sl], bufB[:, sl], bufA[:, sl]),
                        reads=[kA, kB], writes=[kB])
                P.op("vector", lambda e, hf=hf: e.tensor_tensor_scan(
                    bufC, bufB, bufB,
                    (0.0 if hf == 0 else carry[:, 0:1]), ALU.add, ALU.bypass),
                    reads=[kB] + ([] if hf == 0 else [f"{stag}_carry"]),
                    writes=[kC])
                P.op("vector", lambda e: e.tensor_copy(carry[:, 1:2],
                                                       bufC[:, HT - 1:HT]),
                     reads=[kC], writes=[f"{stag}_carry2"])
                P.op("vector", lambda e, hf=hf: (
                    e.memset(bufB[:, 0:1], 0.0) if hf == 0 else
                    e.tensor_copy(bufB[:, 0:1], carry[:, 0:1])),
                    reads=([] if hf == 0 else [f"{stag}_carry"]), writes=[kB])
                P.op("vector", lambda e: e.tensor_copy(bufB[:, 1:HT],
                                                       bufC[:, 0:HT - 1]),
                     reads=[kC], writes=[kB])
                P.op("vector", lambda e, c=c: e.tensor_scalar_mul(
                    bufA, bufA, tfc[:, c:c + 1]),
                    reads=[kA, kc + "tf"], writes=[kA])
                P.op("vector", lambda e, c=c: e.scalar_tensor_tensor(
                    bufD, bufB, tdc[:, c:c + 1], bufA, ALU.mult, ALU.add),
                    reads=[kA, kB, kc + "td"], writes=[kD])
                P.op("scalar", lambda e, kvr=kvr: e.activation(
                    bufA, kvr[2], AF.Sigmoid),
                    reads=[f"{stag}_kvr_{pu}_2"], writes=[kA])
                P.op("vector", lambda e: e.tensor_mul(zq, bufD, bufA),
                    reads=[kA, kD], writes=[kB])
                P.dma("sync", zscr[c][:, h0:h0 + HT], zq,
                      reads=[kB], writes=[f"{stag}_z_{c}_{hf}"])
                P.op("vector", lambda e: e.tensor_copy(carry[:, 0:1],
                                                       carry[:, 1:2]),
                     reads=[f"{stag}_carry2"], writes=[f"{stag}_carry"])
        for slab in range(T // 512):
            sl = slice(slab * 512, (slab + 1) * 512)
            hf = slab // 2
            for k in range(DC):
                P.dma("sync", zsl[k], zscr[k][:, sl],
                      reads=[f"{stag}_z_{k}_{hf}"],
                      writes=[f"{stag}_zsl_{k}"])
            for e0 in range(DC):
                bank = e0 % 2
                for k in range(DC):
                    P.op("tensor", lambda e, k=k, e0=e0, bank=bank:
                         e.matmul(S.psum[bank][:, :],
                                  woT[k][:, e0 * 128:(e0 + 1) * 128],
                                  zsl[k],
                                  start=(k == 0), stop=(k == DC - 1)),
                         reads=[f"{stag}_wo", f"{stag}_zsl_{k}"],
                         writes=[f"psum{bank}"])
                P.op("scalar", lambda e, e0=e0, bank=bank: e.copy(
                    bo[e0], S.psum[bank][:, :]),
                    reads=[f"psum{bank}"], writes=[bo_keys[e0]])
            sw.post_slab_dmajor(slab, b, bo, bo_keys, in_dram[b], out_dram[b])




def stage_l2_attn(S, consts, in_dram, out_dram):
    nc, P, ar = S.nc, S.P, S.ar
    stag = "l2t"
    ar.reset_stage()
    ones = consts["ones"]
    ones16 = consts["ones16"]
    ident16 = consts["ident16"]

    w_wq = S.inp("L2_attn_Wq", (D, D))
    w_bq = S.inp("L2_attn_bq", (D,))
    w_wk = S.inp("L2_attn_Wk", (D, D))
    w_bk = S.inp("L2_attn_bk", (D,))
    w_wv = S.inp("L2_attn_Wv", (D, D))
    w_bv = S.inp("L2_attn_bv", (D,))
    w_wo = S.inp("L2_attn_Wo", (D, D))
    w_bo = S.inp("L2_attn_bo", (D,))
    c_rc = S.inp("c_ropeC", (128, T), BF16)
    c_rs = S.inp("c_ropeS", (128, T), BF16)

    sw = Sandwich(S, consts, stag, use_bt=False)

    kwA, kwB = f"{stag}_wA", f"{stag}_wB"
    wA = [ar.salloc(128, D, BF16) for _ in range(DC)]
    wB = [ar.salloc(128, D, BF16) for _ in range(DC)]
    wsrc = {"L2_attn_Wq": w_wq, "L2_attn_Wk": w_wk,
            "L2_attn_Wv": w_wv, "L2_attn_Wo": w_wo}

    bqc = ar.salloc(128, DC, F32)
    bkc = ar.salloc(128, DC, F32)
    kb = f"{stag}_b"
    for c in range(DC):
        P.dma("sync", bqc[:, c:c + 1],
              w_bq.rearrange("(c p o) -> c p o", p=128, o=1)[c], writes=[kb])
        P.dma("sync", bkc[:, c:c + 1],
              w_bk.rearrange("(c p o) -> c p o", p=128, o=1)[c], writes=[kb])
    bv_row = ar.salloc(1, D, BF16)
    P.dma("gpsimd", bv_row, w_bv.rearrange("(o c) -> o c", o=1), writes=[kb + "v"])
    bo_row = ar.salloc(1, D, BF16)
    P.dma("gpsimd", bo_row, w_bo.rearrange("(o c) -> o c", o=1), writes=[kb + "o"])
    ropeC = ar.salloc(128, T, BF16)
    ropeS = ar.salloc(128, T, BF16)
    P.dma("sync", ropeC, c_rc, writes=["ropeC"])
    P.dma("sync", ropeS, c_rs, writes=["ropeS"])

    k16 = [ar.salloc(128, T, BF16) for _ in range(DC)]
    qx = [ar.salloc(128, 512, BF16) for _ in range(DC)]
    v_tok = [ar.salloc(128, 16 * 65, BF16) for _ in range(NT)]
    rsw = ar.salloc(128, 512, BF16)
    ctxT = [ar.salloc(128, 512, BF16) for _ in range(DC)]
    pT = [ar.salloc(128, 512, BF16) for _ in range(8)]
    recip_row2 = [ar.salloc(1, 512, F32) for _ in range(2)]
    po = ar.salloc(128, D, F32)

    def q_ap(c, qsl):
        if qsl == 0:
            return qx[c]
        return sw.xnT[c][:, (qsl - 1) * 512:qsl * 512]

    def q_key(c, qsl):
        return f"{stag}_qx" if qsl == 0 else sw.xnT_key(c)

    def pT_ap(kt):
        return pT[kt % 8]

    def pT_key(kt):
        return f"{stag}_pT_{kt % 8}"

    def load_w(name, bufs, wk):
        src_r = wsrc[name].rearrange("(r p) c -> r p c", p=128)
        for r in range(DC):
            P.dma("gpsimd", bufs[r], src_r[r], writes=[wk])

    def rope_cs(dst_ap, kd, src_cols):
        for pc in range(4):
            srcp = (pc // 2) * 64 + (1 - pc % 2) * 32
            d0 = (pc // 2) * 64 + (pc % 2) * 32
            P.op("vector", lambda e, srcp=srcp, d0=d0, dst_ap=dst_ap:
                 e.tensor_copy(rsw[d0:d0 + 32, :], dst_ap[srcp:srcp + 32, :]),
                 reads=[kd], writes=[f"{stag}_rsw"])
        P.op("vector", lambda e: e.tensor_mul(
            rsw, rsw, ropeS[:, src_cols]),
            reads=[f"{stag}_rsw", "ropeC"], writes=[f"{stag}_rsw"])
        P.op("vector", lambda e, dst_ap=dst_ap: e.tensor_mul(
            dst_ap, dst_ap, ropeC[:, src_cols]),
            reads=[kd, "ropeC"], writes=[kd])
        P.op("vector", lambda e, dst_ap=dst_ap: e.tensor_add(
            dst_ap, dst_ap, rsw),
            reads=[kd, f"{stag}_rsw"], writes=[kd])

    def emit_B_ctx(b, qsl, h, pending=None):
        ch, off = h // 2, (h % 2) * 64
        kq = q_key(ch, qsl)
        kk = f"{stag}_k_{ch}"
        qh = q_ap(ch, qsl)[off:off + 64, :]
        kh = k16[ch][off:off + 64, :]
        for kth in range(2):
            if kth == 1 and pending is not None:
                pending(); pending = None
            for kt in range(kth * 8, kth * 8 + 8):
                bank = 2 + kt % 4
                P.op("tensor", lambda e, kt=kt, bank=bank, qh=qh, kh=kh:
                     e.matmul(S.psum[bank][:, :],
                              kh[:, kt * 128:(kt + 1) * 128],
                              qh, start=True, stop=True),
                     reads=[kq, kk], writes=[f"psum{bank}"])
                P.op("scalar", lambda e, kt=kt, bank=bank: e.activation(
                    pT_ap(kt), S.psum[bank][:, :], AF.Exp, scale=0.125),
                    reads=[f"psum{bank}"], writes=[pT_key(kt)])
            for kt in range(kth * 8, kth * 8 + 8):
                P.op("tensor", lambda e, kt=kt, h=h: e.matmul(
                    S.psum[6 + h % 2][0:65, 0:512],
                    v_tok[kt][:, h * 65:h * 65 + 65], pT_ap(kt),
                    start=(kt == 0), stop=(kt == NT - 1)),
                    reads=[f"{stag}_vt_{kt}", pT_key(kt)],
                    writes=[f"psum{6 + h % 2}"])
        cb, bb = 6 + h % 2, h % 2
        rr = recip_row2[h % 2]
        krr = f"{stag}_rr_{h % 2}"
        P.op("vector", lambda e, cb=cb, rr=rr: e.reciprocal(
            rr, S.psum[cb][64:65, 0:512]),
            reads=[f"psum{cb}"], writes=[krr])

        def norm(ch=ch, off=off, cb=cb, bb=bb, rr=rr, krr=krr):
            P.op("tensor", lambda e: e.matmul(
                S.psum[bb][0:64, 0:512], ones[0:1, 0:64], rr,
                start=True, stop=True),
                reads=[krr, "ones"], writes=[f"psum{bb}"])
            P.op("scalar", lambda e: e.copy(
                ctxT[ch][off:off + 64, :], S.psum[cb][0:64, 0:512]),
                reads=[f"psum{cb}"], writes=[f"{stag}_ctx_{ch}"])
            P.op("vector", lambda e: e.tensor_mul(
                ctxT[ch][off:off + 64, :], ctxT[ch][off:off + 64, :],
                S.psum[bb][0:64, 0:512]),
                reads=[f"{stag}_ctx_{ch}", f"psum{bb}"],
                writes=[f"{stag}_ctx_{ch}"])
        return norm

    for b in range(B_CORE):
        sw.preln(in_dram[b], b)
        load_w("L2_attn_Wq", wA, kwA)
        load_w("L2_attn_Wk", wB, kwB)
        for c in range(DC):
            for s2 in range(T // 512):
                bank = s2 % 2
                sl = slice(s2 * 512, (s2 + 1) * 512)
                for k in range(DC):
                    P.op("tensor", lambda e, k=k, c=c, sl=sl, bank=bank:
                         e.matmul(S.psum[bank][:, :],
                                  wB[k][:, c * 128:(c + 1) * 128],
                                  sw.xnT[k][:, sl],
                                  start=(k == 0), stop=(k == DC - 1)),
                         reads=[kwB, sw.xnT_key(k)], writes=[f"psum{bank}"])
                P.op("scalar", lambda e, c=c, sl=sl, bank=bank: e.activation(
                    k16[c][:, sl], S.psum[bank][:, :], AF.Identity,
                    bias=bkc[:, c:c + 1]),
                    reads=[f"psum{bank}", kb], writes=[f"{stag}_k_{c}"])
                rope_cs(k16[c][:, sl], f"{stag}_k_{c}", sl)
        load_w("L2_attn_Wv", wB, kwB)
        for i in range(NT):
            kvt = f"{stag}_vt_{i}"
            for hh in range(16):
                P.op("vector", lambda e, i=i, hh=hh: e.memset(
                    v_tok[i][:, hh * 65 + 64:hh * 65 + 65], 1.0),
                    writes=[kvt])
            for half in range(2):
                bank = half
                for k in range(DC):
                    P.op("tensor", lambda e, i=i, k=k, half=half, bank=bank:
                         e.matmul(S.psum[bank][:, :],
                                  sw.xnT[k][:, i * 128:(i + 1) * 128],
                                  wB[k][:, half * 512:(half + 1) * 512],
                                  start=(k == 0), stop=False),
                         reads=[sw.xnT_key(k), kwB], writes=[f"psum{bank}"])
                P.op("tensor", lambda e, half=half, bank=bank: e.matmul(
                    S.psum[bank][:, :], ones16[0:1, 0:128],
                    bv_row[0:1, half * 512:(half + 1) * 512],
                    start=False, stop=True),
                    reads=[kb + "v", "ones16"], writes=[f"psum{bank}"])
                for hh in range(8):
                    h = half * 8 + hh
                    P.op("scalar", lambda e, i=i, h=h, hh=hh, bank=bank:
                         e.copy(v_tok[i][:, h * 65:h * 65 + 64],
                                S.psum[bank][:, hh * 64:(hh + 1) * 64]),
                         reads=[f"psum{bank}"], writes=[kvt])
        for s2 in range(T // 512):
            sl = slice(s2 * 512, (s2 + 1) * 512)
            for c in range(DC):
                bank = c % 2
                for k in range(DC):
                    P.op("tensor", lambda e, k=k, c=c, sl=sl, bank=bank:
                         e.matmul(S.psum[bank][:, :],
                                  wA[k][:, c * 128:(c + 1) * 128],
                                  sw.xnT[k][:, sl],
                                  start=(k == 0), stop=(k == DC - 1)),
                         reads=[kwA, sw.xnT_key(k)], writes=[f"psum{bank}"])
                P.op("scalar", lambda e, c=c, s2=s2, bank=bank: e.activation(
                    q_ap(c, s2), S.psum[bank][:, :], AF.Identity,
                    bias=bqc[:, c:c + 1]),
                    reads=[f"psum{bank}", kb], writes=[q_key(c, s2)])
                rope_cs(q_ap(c, s2), q_key(c, s2), sl)
        load_w("L2_attn_Wo", wB, kwB)
        pending = None
        for qsl in range(T // 512):
            for h in range(16):
                pending = emit_B_ctx(b, qsl, h, pending)
            pending(); pending = None
            for jt in range(4):
                i = qsl * 4 + jt
                kpt = f"{stag}_po"
                for half in range(2):
                    bank = half
                    for k in range(DC):
                        P.op("tensor", lambda e, k=k, jt=jt, half=half,
                             bank=bank: e.matmul(
                                 S.psum[bank][:, :],
                                 ctxT[k][:, jt * 128:(jt + 1) * 128],
                                 wB[k][:, half * 512:(half + 1) * 512],
                                 start=(k == 0), stop=False),
                             reads=[f"{stag}_ctx_{k}", kwB],
                             writes=[f"psum{bank}"])
                    P.op("tensor", lambda e, half=half, bank=bank: e.matmul(
                        S.psum[bank][:, :], ones16[0:1, 0:128],
                        bo_row[0:1, half * 512:(half + 1) * 512],
                        start=False, stop=True),
                        reads=[kb + "o", "ones16"], writes=[f"psum{bank}"])
                    P.op("scalar", lambda e, half=half, bank=bank: e.copy(
                        po[:, half * 512:(half + 1) * 512], S.psum[bank][:, :]),
                        reads=[f"psum{bank}"], writes=[kpt])
                sw.post_tile(i, b, po, kpt, in_dram[b], out_dram[b])



def stage_l3_ffn(S, consts, in_dram, out_dram):
    nc, P, ar = S.nc, S.P, S.ar
    stag = "l3f"
    ar.reset_stage()
    ones16 = consts["ones16"]

    w_gw = S.inp("L3_ffn_gate_w", (D, 8))
    w_gb = S.inp("L3_ffn_bias", (8,))
    w_up = S.inp("L3_ffn_up_w", (8, D, 4096))
    w_upb = S.inp("L3_ffn_up_b", (8, 4096))
    w_dn = S.inp("L3_ffn_down_w", (8, 4096, D))
    w_dnb = S.inp("L3_ffn_down_b", (8, D))
    w_su = S.inp("L3_ffn_sw_up", (D, 4096))
    w_sub = S.inp("L3_ffn_sb_up", (4096,))
    w_sd = S.inp("L3_ffn_sw_down", (4096, D))
    w_sdb = S.inp("L3_ffn_sb_down", (D,))

    sw = Sandwich(S, consts, stag)

    xn_scr = nc.dram_tensor("l3f_xn", [B_CORE, DC, 128, T], BF16,
                            kind="Internal").ap()
    acc = nc.dram_tensor("l3f_acc", [B_CORE, T, D], F32, kind="Internal").ap()

    gate16 = ar.salloc(128, DC * 8, BF16)
    gbrow = ar.salloc(1, 8, BF16)
    w_all = ar.salloc(128, B_CORE * NT * 8, F32)
    gst = ar.salloc(128, 16, F32)
    kg = f"{stag}_g"
    g_r = w_gw.rearrange("(c p) m -> c p m", p=128)
    for c in range(DC):
        P.dma("gpsimd", gate16[:, c * 8:(c + 1) * 8], g_r[c], writes=[kg])
    P.dma("gpsimd", gbrow, w_gb.rearrange("(o c) -> o c", o=1), writes=[kg])

    up_half = [ar.salloc(128, 2048, BF16) for _ in range(DC)]
    dn_half = [ar.salloc(128, D, BF16) for _ in range(16)]
    hact = [ar.salloc(128, 512, BF16) for _ in range(16)]
    eo_sb = ar.salloc(128, 4096, F32)
    xnsl = [[ar.salloc(128, 512, BF16) for _ in range(DC)] for _ in range(2)]
    upb = [ar.salloc(128, 16, F32) for _ in range(2)]
    dnb_row = [ar.salloc(1, D, BF16) for _ in range(2)]
    kw = f"{stag}_w"
    upB_src = [sw.xbuf[0], sw.xbuf[1], sw.bt[0], sw.bt[1],
               sw.xb2[0], sw.xb2[1], S.gscr[0], S.gscr[1]]
    up_sets = [up_half, [a.bitcast(BF16)[:, 0:2048] for a in upB_src]]
    dn_sets = [dn_half,
               [sw.xnT[kk // 2][:, (kk % 2) * 1024:(kk % 2 + 1) * 1024]
                for kk in range(16)]]

    for b in range(B_CORE):
        sw.preln(in_dram[b], b)
        for c in range(DC):
            P.dma("sync", xn_scr[b][c], sw.xnT[c],
                  reads=[sw.xnT_key(c)], writes=[f"{stag}_xs_{b}_{c}"])
        for i in range(NT):
            kgs = f"{stag}_gs"
            for c in range(DC):
                P.op("tensor", lambda e, c=c, i=i: e.matmul(
                    S.psum[6][0:128, 0:8],
                    sw.xnT[c][:, i * 128:(i + 1) * 128],
                    gate16[:, c * 8:(c + 1) * 8],
                    start=(c == 0), stop=False),
                    reads=[sw.xnT_key(c), kg], writes=["psum6"])
            P.op("tensor", lambda e: e.matmul(
                S.psum[6][0:128, 0:8], ones16[0:1, 0:128], gbrow,
                start=False, stop=True),
                reads=[kg, "ones16"], writes=["psum6"])
            sc = gst[:, 0:8]
            P.op("scalar", lambda e: e.activation(sc, S.psum[6][0:128, 0:8],
                                                  AF.Sigmoid),
                 reads=["psum6"], writes=[kgs])
            wcol = w_all[:, (b * NT + i) * 8:(b * NT + i + 1) * 8]
            kwa = f"{stag}_wall"
            msk = gst[:, 8:16]
            m1 = gst[:, 8:9]
            kms = f"{stag}_msk"
            P.op("vector", lambda e, sc=sc, wcol=wcol: e.reduce_max(
                wcol[:, 0:1], sc, axis=AXX), reads=[kgs], writes=[kwa])
            P.op("vector", lambda e, sc=sc, wcol=wcol: e.tensor_scalar(
                msk, sc, wcol[:, 0:1], None, ALU.is_equal),
                reads=[kgs, kwa], writes=[kms])
            P.op("vector", lambda e, sc=sc: e.scalar_tensor_tensor(
                msk, msk, -1e30, sc, ALU.mult, ALU.add),
                reads=[kgs, kms], writes=[kms])
            P.op("vector", lambda e, wcol=wcol: e.reduce_max(
                wcol[:, 1:2], msk, axis=AXX),
                reads=[kms], writes=[kwa])
            P.op("vector", lambda e, sc=sc, wcol=wcol: e.tensor_scalar(
                msk, sc, wcol[:, 1:2], None, ALU.is_ge),
                reads=[kgs, kwa], writes=[kms])
            P.op("vector", lambda e, sc=sc: e.tensor_mul(msk, msk, sc),
                reads=[kgs, kms], writes=[kms])
            P.op("vector", lambda e: e.tensor_scalar(
                gst[:, 8:9], msk, 0.0, 1e-6, ALU.mult, ALU.add,
                accum_out=None), reads=[kms], writes=[f"{stag}_den"]) \
                if False else None
            P.op("vector", lambda e, wcol=wcol: e.reduce_sum(
                wcol[:, 2:3], msk, axis=AXX),
                reads=[kms], writes=[kwa])
            P.op("vector", lambda e, wcol=wcol: e.tensor_scalar_add(
                wcol[:, 2:3], wcol[:, 2:3], 1e-6),
                reads=[kwa], writes=[kwa])
            P.op("vector", lambda e, wcol=wcol: e.reciprocal(
                wcol[:, 3:4], wcol[:, 2:3]),
                reads=[kwa], writes=[kwa])
            P.op("vector", lambda e, wcol=wcol: e.tensor_scalar(
                wcol, msk, wcol[:, 3:4], None, ALU.mult),
                reads=[kms, kwa], writes=[kwa])

    def emit_loads(e_, ih):
        u = (e_ * 2 + ih) % 2
        upS, dnS = up_sets[u], dn_sets[u]
        up_src = w_up[e_] if e_ < 8 else w_su
        dn_src = w_dn[e_] if e_ < 8 else w_sd
        for k in range(DC):
            P.dma("gpsimd", upS[k],
                  up_src[k * 128:(k + 1) * 128,
                         ih * 2048:(ih + 1) * 2048],
                  writes=[kw + f"u{u}_{k}"])
        for kk in range(16):
            P.dma("gpsimd", dnS[kk],
                  dn_src[ih * 2048 + kk * 128:ih * 2048 + (kk + 1) * 128, :],
                  writes=[kw + f"d{u}_{kk}"])
        if e_ < 8:
            ub_r = w_upb[e_].rearrange("(m p o) -> m p o", p=128, o=1)
        else:
            ub_r = w_sub.rearrange("(m p o) -> m p o", p=128, o=1)
        for m in range(16):
            P.dma("sync", upb[u][:, m:m + 1], ub_r[ih * 16 + m],
                  writes=[kw + f"ub{u}"])
        P.dma("gpsimd", dnb_row[u],
              (w_dnb[e_] if e_ < 8 else w_sdb).rearrange("(o c) -> o c", o=1),
              writes=[kw + f"db{u}"])

    P.barrier()
    for e_ in range(9):
        for ih in range(2):
            u = (e_ * 2 + ih) % 2
            upS, dnS = up_sets[u], dn_sets[u]
            emit_loads(e_, ih)
            for b in range(B_CORE):
                for slab in range(T // 512):
                    par = slab % 2
                    ksl = f"{stag}_xnsl_{par}"
                    for k in range(DC):
                        P.dma("scalar", xnsl[par][k],
                              xn_scr[b][k][:, slab * 512:(slab + 1) * 512],
                              reads=[f"{stag}_xs_{b}_{k}"],
                              writes=[ksl + f"_{k}"])
                    for m in range(16):
                        bank = m % 2
                        for k in range(DC):
                            P.op("tensor", lambda e, k=k, m=m, bank=bank,
                                 par=par, upS=upS, u=u: e.matmul(
                                     S.psum[bank][:, :],
                                     upS[k][:, m * 128:(m + 1) * 128],
                                     xnsl[par][k],
                                     start=(k == 0), stop=(k == DC - 1)),
                                 reads=[kw + f"u{u}_{k}", ksl + f"_{k}",
                                        ] , writes=[f"psum{bank}"])
                        P.op("scalar", lambda e, m=m, bank=bank, u=u:
                             e.activation(
                                 hact[m], S.psum[bank][:, :], AF.Gelu,
                                 bias=upb[u][:, m:m + 1]),
                             reads=[f"psum{bank}", kw + f"ub{u}"],
                             writes=[f"{stag}_h_{m}"])
                    for tc in range(4):
                        i = slab * 4 + tc
                        for dh in range(2):
                            bank = 2 + (tc * 2 + dh) % 2
                            for kk in range(16):
                                P.op("tensor", lambda e, kk=kk, tc=tc, dh=dh,
                                     bank=bank, ih=ih, dnS=dnS, u=u: e.matmul(
                                         S.psum[bank][:, :],
                                         hact[kk][:, tc * 128:(tc + 1) * 128],
                                         dnS[kk][:, dh * 512:(dh + 1) * 512],
                                         start=(kk == 0),
                                         stop=(kk == 15 and ih == 0)),
                                     reads=[f"{stag}_h_{kk}", kw + f"d{u}_{kk}"],
                                     writes=[f"psum{bank}"])
                            if ih == 1:
                                P.op("tensor", lambda e, dh=dh, bank=bank, u=u:
                                     e.matmul(
                                         S.psum[bank][:, :], ones16[0:1, 0:128],
                                         dnb_row[u][0:1,
                                                    dh * 512:(dh + 1) * 512],
                                         start=False, stop=True),
                                     reads=[kw + f"db{u}", "ones16"],
                                     writes=[f"psum{bank}"])
                            if e_ < 8:
                                P.op("vector", lambda e, tc=tc, dh=dh,
                                     bank=bank, b=b, i=i, e_=e_:
                                     e.tensor_scalar(
                                         eo_sb[:, tc * 1024 + dh * 512:
                                               tc * 1024 + (dh + 1) * 512],
                                         S.psum[bank][:, :],
                                         w_all[:, (b * NT + i) * 8 + e_:
                                               (b * NT + i) * 8 + e_ + 1],
                                         0.1, ALU.mult, ALU.mult),
                                     reads=[f"psum{bank}", f"{stag}_wall"],
                                     writes=[f"{stag}_eos"])
                            else:
                                P.op("vector", lambda e, tc=tc, dh=dh,
                                     bank=bank: e.tensor_scalar_mul(
                                         eo_sb[:, tc * 1024 + dh * 512:
                                               tc * 1024 + (dh + 1) * 512],
                                         S.psum[bank][:, :], 0.1),
                                     reads=[f"psum{bank}"],
                                     writes=[f"{stag}_eos"])
                    first = (e_ == 0 and ih == 0)
                    P.dma("gpsimd",
                          acc[b].rearrange("(s tc p) d -> s p tc d",
                                           tc=4, p=128)[slab],
                          eo_sb.rearrange("p (tc d) -> p tc d", tc=4),
                          reads=[f"{stag}_eos"],
                          writes=[f"{stag}_acc_{b}_{slab}"],
                          **({} if first else {"accum_op": ALU.add}))

    P.barrier()
    fin = [ar.salloc(128, D, F32) for _ in range(2)]
    for b in range(B_CORE):
        for i in range(NT):
            ft = fin[i % 2]
            kf = f"{stag}_fin_{i % 2}"
            P.dma("sync", ft, acc[b][i * 128:(i + 1) * 128, :],
                  reads=[f"{stag}_acc_{b}_{i // 4}"], writes=[kf])
            P.op("vector", lambda e, ft=ft: e.reduce_max(
                gst[:, 10:11], ft, axis=AXX, apply_absolute_value=True),
                reads=[kf], writes=[f"{stag}_fm"])
            P.op("vector", lambda e: e.tensor_scalar_add(
                gst[:, 10:11], gst[:, 10:11], 1e-6),
                reads=[f"{stag}_fm"], writes=[f"{stag}_fm"])
            P.op("vector", lambda e: e.reciprocal(gst[:, 11:12], gst[:, 10:11]),
                 reads=[f"{stag}_fm"], writes=[f"{stag}_fm"])
            P.op("scalar", lambda e, ft=ft: e.activation(
                ft, ft, AF.Identity, scale=gst[:, 11:12]),
                reads=[kf, f"{stag}_fm"], writes=[kf])
            sw.post_tile(i, b, ft, kf, in_dram[b], out_dram[b])



def stage_l2_ffn(S, consts, in_dram, out_dram):
    nc, P, ar = S.nc, S.P, S.ar
    stag = "l2f"
    ar.reset_stage()
    ones = consts["ones"]
    ones16 = consts["ones16"]
    ident16 = consts["ident16"]

    w_wq = S.inp("L2_ffn_W_Q", (D, D))
    w_wk = S.inp("L2_ffn_W_K", (D, D))
    w_wo = S.inp("L2_ffn_W_O", (D, D))
    w_wi = S.inp("L2_ffn_W_inv", (D, D))
    w_bf = S.inp("L2_ffn_b_fwd", (8, D))
    w_bi = S.inp("L2_ffn_b_inv", (8, D))
    w_r1b = S.inp("L2_ffn_router_b1", (D,))
    w_r2 = S.inp("L2_ffn_router_w2", (8, D))
    w_r2b = S.inp("L2_ffn_router_b2", (8,))
    w_al = S.inp("L2_ffn_alpha_bi", (1, 1))
    w_eb = S.inp("L2_ffn_expert_bias", (8,))
    c_invn = S.inp("c_invn", (128, NT))
    c_sel = S.inp("c_sel64", (32, 2048), BF16)

    sw = Sandwich(S, consts, stag)

    kw1, kw2 = f"{stag}_w1", f"{stag}_w2"
    wb1 = [ar.salloc(128, D, BF16) for _ in range(DC)]
    wb2 = [ar.salloc(128, D, BF16) for _ in range(DC)]
    for nm, dap in (("L2_ffn_W_Q", w_wq), ("L2_ffn_W_K", w_wk),
                    ("L2_ffn_W_inv", w_wi), ("L2_ffn_W_O", w_wo)):
        S.prep16(nm, dap, D, D)
    S.prep16("L2_ffn_router_w1", S.inp("L2_ffn_router_w1", (D, D)), D, D)

    def loadT(name, bufs, wk):
        img, ikey = S.w16[name]
        for c in range(DC):
            P.dma("sync", bufs[c], img[:, c * 128:(c + 1) * 128],
                  reads=[ikey], writes=[wk], transpose=True)

    for nm in ("L2_ffn_V_fwd", "L2_ffn_W_fwd", "L2_ffn_U_fwd",
               "L2_ffn_V_inv", "L2_ffn_W_inv_exp", "L2_ffn_U_inv"):
        img = nc.dram_tensor(f"w16_{nm}", [8, D, 64], BF16, kind="Internal").ap()
        P.dma("gpsimd", img, S.inp(nm, (8, D, 64)), writes=[f"w16_{nm}"])
        S.w16[nm] = (img, f"w16_{nm}")

    vk = [ar.salloc(128, 8 * 64, BF16) for _ in range(DC)]
    wk_ = [ar.salloc(128, 8 * 64, BF16) for _ in range(DC)]
    u_sb = [ar.salloc(128, 8 * 64, BF16) for _ in range(DC)]
    uaug = [ar.salloc(128, D, BF16) for _ in range(8)]
    kvw, kua, kusb = f"{stag}_vw", f"{stag}_ua", f"{stag}_usb"

    def load_branch(xv_name, yw_name, u_name, b_ap):
        xv_img, kxi = S.w16[xv_name]
        yw_img, kyi = S.w16[yw_name]
        u_img, kui = S.w16[u_name]
        for c in range(DC):
            P.dma("sync", vk[c].rearrange("p (k m) -> p k m", k=8),
                  xv_img.rearrange("k (c p) m -> c p k m", p=128)[c],
                  reads=[kxi], writes=[kvw])
            P.dma("sync", wk_[c].rearrange("p (k m) -> p k m", k=8),
                  yw_img.rearrange("k (c p) m -> c p k m", p=128)[c],
                  reads=[kyi], writes=[kvw])
            P.dma("scalar", u_sb[c].rearrange("p (k m) -> p k m", k=8),
                  u_img.rearrange("k (c p) m -> c p k m", p=128)[c],
                  reads=[kui], writes=[kusb])
        for k in range(8):
            for c in range(DC):
                bank = c % 2
                P.op("tensor", lambda e, k=k, c=c, bank=bank: e.transpose(
                    S.psum_bf16(bank)[0:64, 0:128],
                    u_sb[c][:, k * 64:(k + 1) * 64], ident16),
                    reads=[kusb, "ident"], writes=[f"psum{bank}"])
                P.op("scalar", lambda e, k=k, c=c, bank=bank: e.copy(
                    uaug[k][0:64, c * 128:(c + 1) * 128],
                    S.psum_bf16(bank)[0:64, 0:128]),
                    reads=[f"psum{bank}"], writes=[kua])
            P.dma("gpsimd", uaug[k][64:65, :],
                  b_ap[k].rearrange("(o c) -> o c", o=1), writes=[kua])

    r1bc = ar.salloc(128, DC, F32)
    r2T = ar.salloc(128, 64, BF16)
    r2nat = ar.salloc(8, D, BF16)
    biasrows = ar.salloc(1, 16, BF16)
    acol = ar.salloc(128, 2, F32)
    invn = ar.salloc(128, NT, F32)
    selc = ar.salloc(32, 2048, BF16)
    alrow = ar.salloc(1, 2, F32)
    kc = f"{stag}_c"
    for c in range(DC):
        P.dma("sync", r1bc[:, c:c + 1],
              w_r1b.rearrange("(c p o) -> c p o", p=128, o=1)[c], writes=[kc])
    P.dma("gpsimd", r2nat, w_r2, writes=[kc + "r2n"])
    for c in range(DC):
        bank = c % 2
        P.op("tensor", lambda e, c=c, bank=bank: e.transpose(
            S.psum_bf16(bank)[0:128, 0:8], r2nat[:, c * 128:(c + 1) * 128],
            ident16[0:8, 0:8]),
            reads=[kc + "r2n", "ident"], writes=[f"psum{bank}"])
        P.op("scalar", lambda e, c=c, bank=bank: e.copy(
            r2T[:, c * 8:(c + 1) * 8], S.psum_bf16(bank)[0:128, 0:8]),
            reads=[f"psum{bank}"], writes=[kc + "r2T"])
    P.dma("gpsimd", biasrows[0:1, 0:8],
          w_r2b.rearrange("(o c) -> o c", o=1), writes=[kc + "br"])
    P.dma("gpsimd", biasrows[0:1, 8:16],
          w_eb.rearrange("(o c) -> o c", o=1), writes=[kc + "br"])
    P.dma("sync", invn, c_invn, writes=[kc + "n"])
    P.dma("sync", selc, c_sel, writes=[kc + "sel"])
    P.dma("sync", alrow[0:1, 0:1], w_al, writes=[kc + "al"])
    P.op("tensor", lambda e: e.matmul(S.psum[0][0:128, 0:1], ones[0:1, 0:128],
                                      alrow[0:1, 0:1], start=True, stop=True),
         reads=[kc + "al", "ones"], writes=["psum0"])
    P.op("scalar", lambda e: e.copy(acol[:, 0:1], S.psum[0][0:128, 0:1]),
         reads=["psum0"], writes=[kc + "ac"])

    qsl2 = [[ar.salloc(128, 512, BF16) for _ in range(DC)] for _ in range(2)]
    ksl2 = [[ar.salloc(128, 512, BF16) for _ in range(DC)] for _ in range(2)]
    qk_scr = nc.dram_tensor("l2f_qk", [2, DC, 128, T], BF16,
                            kind="Internal").ap()
    inv_scr = nc.dram_tensor("l2f_inv", [2, DC, 128, T], BF16,
                             kind="Internal").ap()
    facc = nc.dram_tensor("l2f_facc", [DC, 128, T], F32, kind="Internal").ap()

    ywb2 = [ar.salloc(64, 512, F32) for _ in range(2)]
    cumb2 = [ar.salloc(64, 512, F32) for _ in range(2)]
    prodb2 = [ar.salloc(64, 512, F32) for _ in range(2)]
    carries = ar.salloc(64, 8, F32)
    wts = ar.salloc(128, 4 * 8, F32)
    wts16 = ar.salloc(128, 4 * 8, BF16)
    wrow_sb = ar.salloc(32, 128, BF16)
    fch = [ar.salloc(128, 512, F32) for _ in range(2)]
    bo16 = [ar.salloc(128, 512, BF16) for _ in range(DC)]
    bo32 = [None] * DC
    bo_keys = [f"{stag}_bo32_{c}" for c in range(DC)]

    def h_ap(c):
        return sw.xnT[c][:, 0:512]

    def bo32_ap(c):
        return sw.xnT[c].bitcast(F32)[:, 512:1024]

    def aug_ap(k):
        return sw.xnT[k][:, 512:1024]

    def mm_to_dram(dst_scr, kdst_fn, wbufs, kwb, src_slab_scr, ksrc_fn,
                   from_xnt=False):
        for s2 in range(T // 512):
            sl = slice(s2 * 512, (s2 + 1) * 512)
            qsl = qsl2[s2 % 2]
            kq = f"{stag}_qsl{s2 % 2}"
            if not from_xnt:
                for k in range(DC):
                    P.dma("scalar", qsl[k], src_slab_scr[k][:, sl],
                          reads=[ksrc_fn(k, s2)], writes=[kq])
            for c in range(DC):
                bank = 2 + c % 2
                for k in range(DC):
                    P.op("tensor", lambda e, k=k, c=c, sl=sl, bank=bank,
                         qsl=qsl:
                         e.matmul(S.psum[bank][:, :],
                                  wbufs[k][:, c * 128:(c + 1) * 128],
                                  (sw.xnT[k][:, sl] if from_xnt else qsl[k]),
                                  start=(k == 0), stop=(k == DC - 1)),
                         reads=[kwb, (sw.xnT_key(k) if from_xnt
                                      else kq)],
                         writes=[f"psum{bank}"])
                P.op("scalar", lambda e, c=c, bank=bank: e.copy(
                    fch[c % 2].bitcast(BF16)[:, 0:512], S.psum[bank][:, :]),
                    reads=[f"psum{bank}"], writes=[f"{stag}_fc{c % 2}"])
                P.dma("sync", dst_scr[c][:, sl],
                      fch[c % 2].bitcast(BF16)[:, 0:512],
                      reads=[f"{stag}_fc{c % 2}"], writes=[kdst_fn(c, s2)])

    for b in range(B_CORE):
        P.barrier()
        sw.preln(in_dram[b], b)
        loadT("L2_ffn_W_Q", wb1, kw1)
        loadT("L2_ffn_W_K", wb2, kw2)
        mm_to_dram(qk_scr[0], lambda c, s: f"{stag}_qs_{c}_{s}", wb1, kw1,
                   None, None, from_xnt=True)
        mm_to_dram(qk_scr[1], lambda c, s: f"{stag}_ks_{c}_{s}", wb2, kw2,
                   None, None, from_xnt=True)
        loadT("L2_ffn_W_inv", wb1, kw1)
        mm_to_dram(inv_scr[0], lambda c, s: f"{stag}_is0_{c}_{s}", wb1, kw1,
                   qk_scr[0], lambda k, s: f"{stag}_qs_{k}_{s}")
        mm_to_dram(inv_scr[1], lambda c, s: f"{stag}_is1_{c}_{s}", wb1, kw1,
                   qk_scr[1], lambda k, s: f"{stag}_ks_{k}_{s}")
        loadT("L2_ffn_router_w1", wb2, kw2)
        P.barrier()
        for branch in range(2):
            if branch == 1:
                loadT("L2_ffn_W_O", wb1, kw1)
                load_branch("L2_ffn_W_inv_exp", "L2_ffn_V_inv",
                            "L2_ffn_U_inv", w_bi)
                src_q, src_k = inv_scr[0], inv_scr[1]
                kqf = lambda c, s: f"{stag}_is0_{c}_{s}"
                kkf = lambda c, s: f"{stag}_is1_{c}_{s}"
            else:
                load_branch("L2_ffn_V_fwd", "L2_ffn_W_fwd",
                            "L2_ffn_U_fwd", w_bf)
                src_q, src_k = qk_scr[0], qk_scr[1]
                kqf = lambda c, s: f"{stag}_qs_{c}_{s}"
                kkf = lambda c, s: f"{stag}_ks_{c}_{s}"
            for slab in range(T // 512):
                sl = slice(slab * 512, (slab + 1) * 512)
                sp = slab % 2
                qsl, ksl = qsl2[sp], ksl2[sp]
                kqs, kks = f"{stag}_qsl{sp}", f"{stag}_ksl{sp}"
                for k in range(DC):
                    P.dma("scalar", qsl[k], src_q[k][:, sl],
                          reads=[kqf(k, slab)], writes=[kqs])
                    P.dma("scalar", ksl[k], src_k[k][:, sl],
                          reads=[kkf(k, slab)], writes=[kks])
                for c in range(DC):
                    bank = 4 + c % 2
                    for k in range(DC):
                        P.op("tensor", lambda e, k=k, c=c, bank=bank, qsl=qsl:
                             e.matmul(S.psum[bank][:, :],
                                      wb2[k][:, c * 128:(c + 1) * 128],
                                      qsl[k],
                                      start=(k == 0), stop=(k == DC - 1)),
                             reads=[kw2, kqs], writes=[f"psum{bank}"])
                    P.op("scalar", lambda e, c=c, bank=bank: e.activation(
                        h_ap(c), S.psum[bank][:, :], AF.Gelu,
                        bias=r1bc[:, c:c + 1]),
                        reads=[f"psum{bank}", kc], writes=[f"{stag}_h"])
                for tc in range(4):
                    i = slab * 4 + tc
                    rb = 6 + tc % 2
                    krb = f"psum{rb}"
                    for c in range(DC):
                        P.op("tensor", lambda e, c=c, tc=tc, rb=rb: e.matmul(
                            S.psum[rb][0:128, 0:8],
                            h_ap(c)[:, tc * 128:(tc + 1) * 128],
                            r2T[:, c * 8:(c + 1) * 8],
                            start=(c == 0), stop=False),
                            reads=[f"{stag}_h", kc + "r2T"], writes=[krb])
                    P.op("tensor", lambda e, rb=rb: e.matmul(
                        S.psum[rb][0:128, 0:8], ones16[0:1, 0:128],
                        biasrows[0:1, 0:8], start=False, stop=False),
                        reads=[kc + "br", "ones16"], writes=[krb])
                    P.op("tensor", lambda e, rb=rb: e.matmul(
                        S.psum[rb][0:128, 0:8], ones16[0:1, 0:128],
                        biasrows[0:1, 8:16], start=False, stop=True),
                        reads=[kc + "br", "ones16"], writes=[krb])
                    wt = wts[:, tc * 8:(tc + 1) * 8]
                    kwt = f"{stag}_wt{tc % 2}"
                    P.op("vector", lambda e, wt=wt, rb=rb: e.reduce_max(
                        wt[:, 0:1], S.psum[rb][0:128, 0:8], axis=AXX),
                        reads=[krb], writes=[kwt])
                    P.op("vector", lambda e, wt=wt: e.tensor_scalar_mul(
                        wt[:, 1:2], wt[:, 0:1], -1.0),
                        reads=[kwt], writes=[kwt])
                    P.op("scalar", lambda e, wt=wt, rb=rb: e.activation(
                        wt, S.psum[rb][0:128, 0:8], AF.Exp, bias=wt[:, 1:2]),
                        reads=[krb, kwt], writes=[kwt])
                    P.op("vector", lambda e, wt=wt: e.reduce_sum(
                        acol[:, 1:2], wt, axis=AXX),
                        reads=[kwt], writes=[f"{stag}_den"])
                    P.op("vector", lambda e: e.reciprocal(
                        acol[:, 1:2], acol[:, 1:2]),
                        reads=[f"{stag}_den"], writes=[f"{stag}_den"])
                    P.op("vector", lambda e, i=i: e.tensor_mul(
                        acol[:, 1:2], acol[:, 1:2], invn[:, i:i + 1]),
                        reads=[f"{stag}_den", kc + "n"], writes=[f"{stag}_den"])
                    if branch == 1:
                        P.op("vector", lambda e: e.tensor_mul(
                            acol[:, 1:2], acol[:, 1:2], acol[:, 0:1]),
                            reads=[f"{stag}_den", kc + "ac"],
                            writes=[f"{stag}_den"])
                    P.op("vector", lambda e, wt=wt, tc=tc: e.tensor_scalar(
                        wts16[:, tc * 8:(tc + 1) * 8], wt, acol[:, 1:2], None,
                        ALU.mult),
                        reads=[kwt, f"{stag}_den"], writes=[f"{stag}_wt16"])
                P.op("tensor", lambda e: e.matmul(
                    S.psum[6][0:32, 0:128], wts16[:, 0:32], ident16,
                    start=True, stop=True),
                    reads=[f"{stag}_wt16", "ident"], writes=["psum6"])
                P.op("scalar", lambda e: e.copy(wrow_sb, S.psum[6][0:32, 0:128]),
                     reads=["psum6"], writes=[f"{stag}_wr"])
                for k in range(8):
                    kp = k % 2
                    ywb, cumb, prodb = ywb2[kp], cumb2[kp], prodb2[kp]
                    byw, bxv, bw = 5 - kp * 3, 4 - kp * 1 if False else 0, 0
                    byw = (5, 3)[kp]
                    bxv = (4, 2)[kp]
                    bw = (7, 6)[kp]
                    kyw, kcum, kpr = (f"{stag}_yw{kp}", f"{stag}_cum{kp}",
                                      f"{stag}_pr{kp}")
                    for c in range(DC):
                        P.op("tensor", lambda e, k=k, c=c, byw=byw, ksl=ksl:
                             e.matmul(
                                 S.psum[byw][0:64, 0:512],
                                 wk_[c][:, k * 64:(k + 1) * 64], ksl[c],
                                 start=(c == 0), stop=(c == DC - 1)),
                             reads=[kvw, kks], writes=[f"psum{byw}"])
                        P.op("tensor", lambda e, k=k, c=c, bxv=bxv, qsl=qsl:
                             e.matmul(
                                 S.psum[bxv][0:64, 0:512],
                                 vk[c][:, k * 64:(k + 1) * 64], qsl[c],
                                 start=(c == 0), stop=(c == DC - 1)),
                             reads=[kvw, kqs], writes=[f"psum{bxv}"])
                    P.op("scalar", lambda e, ywb=ywb, byw=byw: e.copy(
                        ywb, S.psum[byw][0:64, 0:512]),
                        reads=[f"psum{byw}"], writes=[kyw])
                    P.op("vector", lambda e, k=k, slab=slab, ywb=ywb,
                         cumb=cumb: e.tensor_tensor_scan(
                             cumb, ywb, ywb,
                             (0.0 if slab == 0 else carries[:, k:k + 1]),
                             ALU.add, ALU.bypass),
                        reads=[kyw] + ([] if slab == 0
                                       else [f"{stag}_cy_{k}"]),
                        writes=[kcum])
                    P.op("vector", lambda e, k=k, cumb=cumb: e.tensor_copy(
                        carries[:, k:k + 1], cumb[:, 511:512]),
                        reads=[kcum], writes=[f"{stag}_cy_{k}"])
                    P.op("vector", lambda e, cumb=cumb, prodb=prodb, bxv=bxv:
                         e.tensor_mul(prodb, cumb, S.psum[bxv][0:64, 0:512]),
                         reads=[kcum, f"psum{bxv}"], writes=[kpr])
                    for tc in range(4):
                        P.op("tensor", lambda e, k=k, tc=tc, bw=bw: e.matmul(
                            S.psum[bw][0:64, tc * 128:(tc + 1) * 128],
                            selc[:, (tc * 8 + k) * 64:(tc * 8 + k) * 64 + 64],
                            wrow_sb, start=True, stop=True),
                            reads=[f"{stag}_wr", kc + "sel"],
                            writes=[f"psum{bw}"])
                    P.op("vector", lambda e, k=k, prodb=prodb, bw=bw:
                         e.tensor_mul(
                             aug_ap(k)[0:64, :], prodb,
                             S.psum[bw][0:64, 0:512]),
                         reads=[kpr, f"psum{bw}"], writes=[f"{stag}_aug"])
                    P.op("scalar", lambda e, k=k, bw=bw: e.copy(
                        aug_ap(k)[64:65, :], S.psum[bw][0:1, 0:512]),
                        reads=[f"psum{bw}"], writes=[f"{stag}_aug"])
                for c in range(DC):
                    bank = c % 2
                    for k in range(8):
                        P.op("tensor", lambda e, k=k, c=c, bank=bank: e.matmul(
                            S.psum[bank][:, :],
                            uaug[k][0:65, c * 128:(c + 1) * 128],
                            aug_ap(k)[0:65, :],
                            start=(k == 0), stop=(k == 7)),
                            reads=[kua, f"{stag}_aug"], writes=[f"psum{bank}"])
                    if branch == 0:
                        P.op("scalar", lambda e, c=c, bank=bank: e.copy(
                            fch[c % 2], S.psum[bank][:, :]),
                            reads=[f"psum{bank}"], writes=[f"{stag}_fc{c % 2}"])
                        P.dma("sync", facc[c][:, sl], fch[c % 2],
                              reads=[f"{stag}_fc{c % 2}"],
                              writes=[f"{stag}_fa_{c}_{slab}"])
                    else:
                        P.dma("sync", fch[c % 2], facc[c][:, sl],
                              reads=[f"{stag}_fa_{c}_{slab}"],
                              writes=[f"{stag}_fc{c % 2}"])
                        P.op("vector", lambda e, c=c, bank=bank: e.tensor_add(
                            bo16[c], fch[c % 2], S.psum[bank][:, :]),
                            reads=[f"psum{bank}", f"{stag}_fc{c % 2}"],
                            writes=[f"{stag}_bo16"])
                if branch == 1:
                    for c in range(DC):
                        bank = 2 + c % 2
                        for k in range(DC):
                            P.op("tensor", lambda e, k=k, c=c, bank=bank:
                                 e.matmul(S.psum[bank][:, :],
                                          wb1[k][:, c * 128:(c + 1) * 128],
                                          bo16[k],
                                          start=(k == 0), stop=(k == DC - 1)),
                                 reads=[kw1, f"{stag}_bo16"],
                                 writes=[f"psum{bank}"])
                        P.op("scalar", lambda e, c=c, bank=bank: e.copy(
                            bo32_ap(c), S.psum[bank][:, :]),
                            reads=[f"psum{bank}"], writes=[bo_keys[c]])
                    sw.post_slab_dmajor(slab, b, [bo32_ap(c) for c in range(DC)],
                                        bo_keys, in_dram[b], out_dram[b])



def stage_l1_attn(S, consts, in_dram, out_dram):
    nc, P, ar = S.nc, S.P, S.ar
    stag = "l1a"
    ar.reset_stage()
    ones = consts["ones"]
    ident16 = consts["ident16"]
    G = 256
    NG = T // G

    w_lnw = S.inp("L1_attn_ln_w", (D,))
    w_lnb = S.inp("L1_attn_ln_b", (D,))
    w_wh = S.inp("L1_attn_Wh", (D, 4096))
    w_bh = S.inp("L1_attn_bh", (4096,))
    w_wqk = S.inp("L1_attn_Wqk", (D, 128))
    w_bqk = S.inp("L1_attn_bqk", (128,))
    w_qkw = S.inp("L1_attn_qk_w", (4, 128))
    w_qkb = S.inp("L1_attn_qk_b", (4, 128))
    w_rel = S.inp("L1_attn_rel_table", (32,))
    w_wo = S.inp("L1_attn_Wout", (2 * D, D))
    w_bo = S.inp("L1_attn_bout", (D,))
    c_bkt = S.inp("c_bucketT", (32, G, G), BF16)
    c_tri = S.inp("c_triuTneg", (G, G))

    sw = Sandwich(S, consts, stag)
    lnr = LNPool(S, stag + "_lnr")
    lnr2 = LNPool(S, stag + "_lnr2")

    kwh, kwo, kwqk = f"{stag}_wh", f"{stag}_wo", f"{stag}_wqk"
    wh = [ar.salloc(128, 4096, BF16) for _ in range(DC)]
    wo = [ar.salloc(128, D, BF16) for _ in range(16)]
    wqk16 = [ar.salloc(128, 128, BF16) for _ in range(DC)]
    wh_r = w_wh.rearrange("(r p) c -> r p c", p=128)
    for r in range(DC):
        P.dma("gpsimd", wh[r], wh_r[r], writes=[kwh])
    wo_r = w_wo.rearrange("(r p) c -> r p c", p=128)
    for r in range(16):
        P.dma("gpsimd", wo[r], wo_r[r], writes=[kwo])
    wqk_r = w_wqk.rearrange("(r p) c -> r p c", p=128)
    for r in range(DC):
        P.dma("gpsimd", wqk16[r], wqk_r[r], writes=[kwqk])

    cols = ar.salloc(128, 64, F32)
    lnw_c = cols[:, 0:8]; lnb_c = cols[:, 8:16]
    bh_c = None
    bhc = ar.salloc(128, 32, F32)
    bqk_c = cols[:, 16:17]
    qkw_c = cols[:, 20:24]
    qkb_c = cols[:, 24:28]
    bo_c = cols[:, 32:40]
    kc = f"{stag}_c"
    for c in range(DC):
        P.dma("sync", lnw_c[:, c:c + 1],
              w_lnw.rearrange("(c p o) -> c p o", p=128, o=1)[c], writes=[kc])
        P.dma("sync", lnb_c[:, c:c + 1],
              w_lnb.rearrange("(c p o) -> c p o", p=128, o=1)[c], writes=[kc])
        P.dma("sync", bo_c[:, c:c + 1],
              w_bo.rearrange("(c p o) -> c p o", p=128, o=1)[c], writes=[kc])
    for m in range(32):
        P.dma("sync", bhc[:, m:m + 1],
              w_bh.rearrange("(m p o) -> m p o", p=128, o=1)[m], writes=[kc])
    P.dma("sync", bqk_c, w_bqk.rearrange("(p o) -> p o", o=1), writes=[kc])
    for h in range(4):
        P.dma("sync", qkw_c[:, h:h + 1],
              w_qkw.rearrange("h (p o) -> h p o", o=1)[h], writes=[kc])
        P.dma("sync", qkb_c[:, h:h + 1],
              w_qkb.rearrange("h (p o) -> h p o", o=1)[h], writes=[kc])

    biasT = [ar.salloc(128, G, F32) for _ in range(2)]
    kbt = f"{stag}_bT"
    sstmp = ar.salloc(128, G, F32)
    relrow = sstmp[0:1, 32:64]
    relbc = sstmp[:, 0:32]
    goutbuf = ar.salloc(128, 16 * G, BF16)
    mbuf = [goutbuf[:, 0:G], goutbuf[:, G:2 * G]]
    tbuf = goutbuf.bitcast(F32)[:, G:2 * G]
    P.dma("sync", relrow, w_rel.rearrange("(o c) -> o c", o=1),
          writes=[kc + "rel"])
    P.op("scalar", lambda e: e.mul(relrow, relrow, float(np.sqrt(128.0))),
         reads=[kc + "rel"], writes=[kc + "rel"])
    P.op("tensor", lambda e: e.matmul(S.psum[0][0:128, 0:32],
                                      ones[0:1, 0:128], relrow,
                                      start=True, stop=True),
         reads=[kc + "rel", "ones"], writes=["psum0"])
    P.op("scalar", lambda e: e.copy(relbc, S.psum[0][0:128, 0:32]),
         reads=["psum0"], writes=[kc + "relb"])
    for jb in range(2):
        P.op("vector", lambda e, jb=jb: e.memset(biasT[jb], 0.0), writes=[kbt])
    for v in range(32):
        for jb in range(2):
            km = f"{stag}_mb{v % 2}"
            P.dma("sync", mbuf[v % 2], c_bkt[v][jb * 128:(jb + 1) * 128, :],
                  writes=[km])
            P.op("vector", lambda e, v=v, jb=jb: e.scalar_tensor_tensor(
                biasT[jb], mbuf[v % 2], relbc[:, v:v + 1], biasT[jb],
                ALU.mult, ALU.add),
                reads=[km, kc + "relb", kbt], writes=[kbt])
    for jb in range(2):
        P.dma("sync", tbuf, c_tri[jb * 128:(jb + 1) * 128, :],
              writes=[kc + "tri"])
        P.op("vector", lambda e, jb=jb: e.tensor_add(biasT[jb], biasT[jb],
                                                     tbuf),
             reads=[kc + "tri", kbt], writes=[kbt])

    xn_scr = nc.dram_tensor("l1a_xn", [B_CORE, DC, 128, T], BF16,
                            kind="Internal").ap()
    xng = [ar.salloc(128, G, BF16) for _ in range(DC)]
    qkg = ar.salloc(128, G, BF16)
    qk4 = [ar.salloc(128, G, BF16) for _ in range(4)]
    lktok = [ar.salloc(128, 128, BF16) for _ in range(2)]
    attn16 = [ar.salloc(128, G, BF16) for _ in range(2)]
    state16 = ar.salloc(128, 2 * D, BF16)
    bo32 = [ar.salloc(128, G, F32) for _ in range(DC)]
    bo_keys = [f"{stag}_bo_{c}" for c in range(DC)]
    xn1buf = sw.xbuf[0]
    kx1 = f"{stag}_prex_0"
    xnn16 = [ar.salloc(128, D, BF16) for _ in range(2)]

    def hv_ap(m):
        return sw.xnT[m // 8][:, (m % 8) * G:(m % 8 + 1) * G]

    def hv_key(m):
        return sw.xnT_key(m // 8)

    def vtok_ap(jb):
        return sw.xnT[4 + jb]

    def vtok_key(jb):
        return sw.xnT_key(4 + jb)

    def st32_ap(e5):
        return sw.xnT[6 + e5 // 2].bitcast(F32)[:, (e5 % 2) * 512:
                                                (e5 % 2 + 1) * 512]

    def st32_key(e5):
        return sw.xnT_key(6 + e5 // 2)

    def gout_ap(ec):
        return goutbuf[:, ec * G:(ec + 1) * G]

    def inner_ln(i, xn32, kxn):
        r2, nm2, ks2 = lnr.stats(xn32, kxn)
        out = xnn16[i % 2]
        ko = f"{stag}_xnn_{i % 2}"
        P.op("scalar", lambda e, xn32=xn32, out=out, r2=r2, nm2=nm2:
             e.activation(out, xn32, AF.Identity, bias=nm2, scale=r2),
             reads=[kxn, ks2], writes=[ko])
        return out, ko

    def pre_add(i, bo_tok, kbt_, xb, kx):
        r3, nm3, ks3 = lnr2.stats(xb, kx)
        P.op("scalar", lambda e: e.activation(xn1buf, xb, AF.Identity,
                                              bias=nm3, scale=r3),
             reads=[kx, ks3], writes=[kx1])
        P.op("vector", lambda e, bo_tok=bo_tok: e.tensor_add(
            bo_tok, bo_tok, xn1buf),
            reads=[kbt_, kx1], writes=[kbt_])

    for b in range(B_CORE):
        P.barrier()
        sw.preln(in_dram[b], b, pre_cb=inner_ln)
        for c in range(DC):
            P.op("scalar", lambda e, c=c: e.activation(
                sw.xnT[c], sw.xnT[c], AF.Identity,
                bias=lnb_c[:, c:c + 1], scale=lnw_c[:, c:c + 1]),
                reads=[sw.xnT_key(c), kc], writes=[sw.xnT_key(c)])
            P.dma("sync", xn_scr[b][c], sw.xnT[c],
                  reads=[sw.xnT_key(c)], writes=[f"{stag}_xs_{b}_{c}"])
        for e5 in range(4):
            P.op("vector", lambda e, e5=e5: e.memset(st32_ap(e5), 0.0),
                 writes=[st32_key(e5)])
        P.op("vector", lambda e: e.memset(state16, 0.0),
             writes=[f"{stag}_s16"])
        for g in range(NG):
            ksl = f"{stag}_xng"
            gs = slice(g * G, (g + 1) * G)
            for c in range(DC):
                P.dma("scalar", xng[c], xn_scr[b][c][:, gs],
                      reads=[f"{stag}_xs_{b}_{c}"], writes=[ksl])
            for m in range(32):
                bank = m % 2
                for k in range(DC):
                    P.op("tensor", lambda e, k=k, m=m, bank=bank:
                         e.matmul(S.psum[bank][0:128, 0:G],
                                  wh[k][:, m * 128:(m + 1) * 128],
                                  xng[k],
                                  start=(k == 0), stop=(k == DC - 1)),
                         reads=[kwh, ksl], writes=[f"psum{bank}"])
                P.op("scalar", lambda e, m=m, bank=bank: e.activation(
                    hv_ap(m), S.psum[bank][0:128, 0:G], AF.Silu,
                    bias=bhc[:, m:m + 1]),
                    reads=[f"psum{bank}", kc], writes=[hv_key(m)])
            for k in range(DC):
                P.op("tensor", lambda e, k=k: e.matmul(
                    S.psum[2][0:128, 0:G], wqk16[k], xng[k],
                    start=(k == 0), stop=(k == DC - 1)),
                    reads=[kwqk, ksl], writes=["psum2"])
            P.op("scalar", lambda e: e.activation(
                qkg, S.psum[2][0:128, 0:G], AF.Silu, bias=bqk_c),
                reads=["psum2", kc], writes=[f"{stag}_qkg"])
            for h in range(4):
                P.op("scalar", lambda e, h=h: e.activation(
                    qk4[h], qkg, AF.Identity, bias=qkb_c[:, h:h + 1],
                    scale=qkw_c[:, h:h + 1]),
                    reads=[f"{stag}_qkg", kc], writes=[f"{stag}_qk4_{h}"])
            for jb in range(2):
                for ec in range(16):
                    bank = 4 + (ec % 2)
                    transpose_128(S, vtok_ap(jb)[:, ec * 128:(ec + 1) * 128],
                                  hv_ap(ec)[:, jb * 128:(jb + 1) * 128],
                                  ident16, bank, hv_key(ec),
                                  vtok_key(jb), BF16,
                                  evac_eng="vector" if ec % 2 else "scalar")
                bank = 4 + (jb % 2)
                transpose_128(S, lktok[jb],
                              qk4[3][:, jb * 128:(jb + 1) * 128],
                              ident16, bank, f"{stag}_qk4_3",
                              f"{stag}_lk_{jb}", BF16,
                              evac_eng="vector")
            for jb in range(2):
                P.op("tensor", lambda e, jb=jb: e.matmul(
                    S.psum[6][0:128, 0:G],
                    qk4[2][:, jb * 128:(jb + 1) * 128], qk4[0],
                    start=True, stop=True),
                    reads=[f"{stag}_qk4_2", f"{stag}_qk4_0"],
                    writes=["psum6"])
                P.op("vector", lambda e, jb=jb: e.scalar_tensor_tensor(
                    sstmp, S.psum[6][0:128, 0:G], 1.0 / G, biasT[jb],
                    ALU.mult, ALU.add),
                    reads=["psum6", kbt], writes=[f"{stag}_sst"])
                P.op("scalar", lambda e: e.activation(sstmp, sstmp, AF.Relu),
                     reads=[f"{stag}_sst"], writes=[f"{stag}_sst"])
                P.op("vector", lambda e, jb=jb: e.tensor_mul(
                    attn16[jb], sstmp, sstmp),
                    reads=[f"{stag}_sst"], writes=[f"{stag}_at_{jb}"])
            for ec in range(16):
                bank = ec % 2
                for jb in range(2):
                    P.op("tensor", lambda e, ec=ec, jb=jb, bank=bank: e.matmul(
                        S.psum[bank][0:128, 0:G],
                        vtok_ap(jb)[:, ec * 128:(ec + 1) * 128], attn16[jb],
                        start=(jb == 0), stop=False),
                        reads=[vtok_key(jb), f"{stag}_at_{jb}"],
                        writes=[f"psum{bank}"])
                P.op("tensor", lambda e, ec=ec, bank=bank: e.matmul(
                    S.psum[bank][0:128, 0:G],
                    state16[:, ec * 128:(ec + 1) * 128], qk4[1],
                    start=False, stop=True),
                    reads=[f"{stag}_s16", f"{stag}_qk4_1"],
                    writes=[f"psum{bank}"])
                P.op("vector", lambda e, ec=ec, bank=bank: e.tensor_mul(
                    gout_ap(ec), S.psum[bank][0:128, 0:G], hv_ap(16 + ec)),
                    reads=[f"psum{bank}", hv_key(16 + ec)],
                    writes=[f"{stag}_go_{ec}"])
            for e5 in range(4):
                bank = 2 + e5 % 2
                for jb in range(2):
                    P.op("tensor", lambda e, e5=e5, jb=jb, bank=bank: e.matmul(
                        S.psum[bank][0:128, 0:512],
                        lktok[jb], vtok_ap(jb)[:, e5 * 512:(e5 + 1) * 512],
                        start=(jb == 0), stop=(jb == 1)),
                        reads=[f"{stag}_lk_{jb}", vtok_key(jb)],
                        writes=[f"psum{bank}"])
                P.op("vector", lambda e, e5=e5, bank=bank: e.scalar_tensor_tensor(
                    st32_ap(e5),
                    S.psum[bank][0:128, 0:512], 1.0 / G,
                    st32_ap(e5),
                    ALU.mult, ALU.add),
                    reads=[f"psum{bank}", st32_key(e5)],
                    writes=[st32_key(e5)])
            for e5 in range(4):
                P.op("vector", lambda e, e5=e5: e.tensor_copy(
                    state16[:, e5 * 512:(e5 + 1) * 512], st32_ap(e5)),
                    reads=[st32_key(e5)], writes=[f"{stag}_s16"])
            for c in range(DC):
                bank = 6 + c % 2
                for k in range(16):
                    P.op("tensor", lambda e, k=k, c=c, bank=bank: e.matmul(
                        S.psum[bank][0:128, 0:G],
                        wo[k][:, c * 128:(c + 1) * 128], gout_ap(k),
                        start=(k == 0), stop=(k == 15)),
                        reads=[kwo, f"{stag}_go_{k}"], writes=[f"psum{bank}"])
                P.op("scalar", lambda e, c=c, bank=bank: e.activation(
                    bo32[c], S.psum[bank][0:128, 0:G], AF.Identity,
                    bias=bo_c[:, c:c + 1]),
                    reads=[f"psum{bank}", kc], writes=[bo_keys[c]])
            sw.post_slab_dmajor(g, b, bo32, bo_keys, in_dram[b], out_dram[b],
                                ncols=G, pre_add_cb=pre_add)



def stage_l0_attn(S, consts, in_dram, out_dram):
    nc, P, ar = S.nc, S.P, S.ar
    stag = "l0a"
    ar.reset_stage()
    ones = consts["ones"]
    consts_id16 = consts["ident16"]

    w_win = S.inp("L0_attn_W_in", (D, D))
    w_wat = S.inp("L0_attn_W_attn", (656, D))
    w_wo = S.inp("L0_attn_W_out", (D, D))
    w_sw = S.inp("L0_attn_scale_weights", (2,))

    CS = (21, 41)
    OFFS = (10, 0)
    COL0 = (0, 328)
    PADR = 20

    sw = Sandwich(S, consts, stag)

    winT = S.wT("L0_attn_W_in", w_win, D, D, f"{stag}_win")
    watT = S.wT("L0_attn_W_attn", w_wat, 656, D, f"{stag}_wat", pad_rows=112)
    woutT = S.wT("L0_attn_W_out", w_wo, D, D, f"{stag}_wout")

    NPAR = 2
    NR = 3
    A = nc.dram_tensor("l0a_A", [NT, 2, 8, 128, 256], BF16,
                       kind="Internal").ap()
    zpad = nc.dram_tensor("l0a_zpad", [2304, D], BF16, kind="Internal").ap()

    zbuf = [ar.salloc(128, T, BF16) for _ in range(DC)]
    kz = f"{stag}_z"
    ztok = [ar.salloc(128, D, BF16) for _ in range(2)]
    zslab = [ar.salloc(128, D, BF16) for _ in range(NR)]
    attn_sb = [ar.salloc(128, 768, F32) for _ in range(NPAR)]
    w16s = [[ar.salloc(128, 8 * c, BF16) for c in CS] for _ in range(NPAR)]
    atb = [[ar.salloc(128, 128, BF16) for _ in range(8)] for _ in range(2)]
    abig = [[ar.salloc(128, 2048, BF16) for _ in range(2)] for _ in range(2)]
    stt8 = None
    bo = [ar.salloc(128, 512, F32) for _ in range(DC)]
    bo_keys = [f"{stag}_bo_{c}" for c in range(DC)]
    stt_ = ar.salloc(128, 8 + NPAR * 16, F32)
    stt8p = [[stt_[:, 8 + p * 16:16 + p * 16],
              stt_[:, 16 + p * 16:24 + p * 16]] for p in range(NPAR)]
    swrow = ar.salloc(1, 4, F32)
    swcol = ar.salloc(128, 2, F32)
    kc = f"{stag}_c"

    def zr_ap(c):
        return sw.xnT[c]

    P.op("vector", lambda e: e.memset(ztok[0], 0.0), writes=[f"{stag}_zt_0"])
    P.dma("gpsimd", zpad[0:PADR, :], ztok[0][0:PADR, :],
          reads=[f"{stag}_zt_0"], writes=[f"{stag}_zz"])
    P.dma("gpsimd", zpad[2068:2196, :], ztok[0],
          reads=[f"{stag}_zt_0"], writes=[f"{stag}_zz"])
    P.dma("gpsimd", zpad[2196:2304, :], ztok[0][0:108, :],
          reads=[f"{stag}_zt_0"], writes=[f"{stag}_zz"])
    P.op("vector", lambda e: e.memset(zbuf[0], 0.0), writes=[kz])
    for i in range(NT):
        for sc in range(2):
            P.dma(DMA_QUEUES[(i * 2 + sc) % 3],
                  A[i][sc].rearrange("h p c -> p h c"),
                  zbuf[0].rearrange("p (h c) -> p h c", h=8),
                  reads=[kz],
                  writes=[f"{stag}_A_{i}_{sc}_{h}" for h in range(8)])
    P.dma("sync", swrow[0:1, 0:2], w_sw.rearrange("(o c) -> o c", o=1),
          writes=[kc])
    P.op("vector", lambda e: e.reduce_max(swrow[0:1, 2:3], swrow[0:1, 0:2],
                                          axis=AXX), reads=[kc], writes=[kc])
    P.op("vector", lambda e: e.tensor_scalar_mul(swrow[0:1, 2:3],
                                                 swrow[0:1, 2:3], -1.0),
         reads=[kc], writes=[kc])
    P.op("scalar", lambda e: e.activation(swrow[0:1, 0:2], swrow[0:1, 0:2],
                                          AF.Exp, bias=swrow[0:1, 2:3],
                                          accum_out=swrow[0:1, 3:4]),
         reads=[kc], writes=[kc])
    P.op("vector", lambda e: e.reciprocal(swrow[0:1, 3:4], swrow[0:1, 3:4]),
         reads=[kc], writes=[kc])
    P.op("vector", lambda e: e.tensor_scalar(swrow[0:1, 0:2], swrow[0:1, 0:2],
                                             swrow[0:1, 3:4], None, ALU.mult),
         reads=[kc], writes=[kc])
    P.op("tensor", lambda e: e.matmul(S.psum[0][0:128, 0:2], ones[0:1, 0:128],
                                      swrow[0:1, 0:2], start=True, stop=True),
         reads=[kc, "ones"], writes=["psum0"])
    P.op("scalar", lambda e: e.copy(swcol, S.psum[0][0:128, 0:2]),
         reads=["psum0"], writes=[kc + "sc"])

    for b in range(B_CORE):
        P.barrier()
        sw.preln(in_dram[b], b)
        for c in range(DC):
            for s2 in range(T // 512):
                bank = s2 % 2
                sl = slice(s2 * 512, (s2 + 1) * 512)
                for k in range(DC):
                    P.op("tensor", lambda e, k=k, c=c, sl=sl, bank=bank:
                         e.matmul(S.psum[bank][:, :],
                                  winT[k][:, c * 128:(c + 1) * 128],
                                  sw.xnT[k][:, sl],
                                  start=(k == 0), stop=(k == DC - 1)),
                         reads=[f"{stag}_win", sw.xnT_key(k)],
                         writes=[f"psum{bank}"])
                P.op("scalar", lambda e, c=c, sl=sl, bank=bank: e.copy(
                    zbuf[c][:, sl], S.psum[bank][:, :]),
                    reads=[f"psum{bank}"], writes=[kz])
        for i in range(NT):
            zt = ztok[i % 2]
            kzt = f"{stag}_zt_{i % 2}"
            for c in range(DC):
                bank = 4 + c % 2
                transpose_128(S, zt[:, c * 128:(c + 1) * 128],
                              zbuf[c][:, i * 128:(i + 1) * 128], 
                              consts["ident16"], bank, kz, kzt, BF16,
                              evac_eng="vector" if c % 2 else "scalar")
            P.dma("gpsimd", zpad[PADR + i * 128:PADR + (i + 1) * 128, :], zt,
                  reads=[kzt], writes=[f"{stag}_zp_{i}"])
        for c in range(DC):
            P.op("scalar", lambda e, c=c: e.activation(
                zr_ap(c), zbuf[c], AF.Relu),
                reads=[kz, sw.xnT_key(c)], writes=[sw.xnT_key(c)])
        for i in range(NT):
            par = i % NPAR
            asb = attn_sb[par]
            kat = f"{stag}_at_{par}"
            for nh in range(2):
                bank = 6 + nh
                ncols = 512 if nh == 0 else 256
                for k in range(DC):
                    P.op("tensor", lambda e, k=k, i=i, nh=nh, bank=bank,
                         ncols=ncols: e.matmul(
                             S.psum[bank][0:128, 0:ncols],
                             zr_ap(k)[:, i * 128:(i + 1) * 128],
                             watT[k][:, nh * 512:nh * 512 + ncols],
                             start=(k == 0), stop=(k == DC - 1)),
                         reads=[sw.xnT_key(k), f"{stag}_wat"],
                         writes=[f"psum{bank}"])
                P.op("scalar", lambda e, nh=nh, bank=bank, ncols=ncols,
                     asb=asb: e.copy(asb[:, nh * 512:nh * 512 + ncols],
                                     S.psum[bank][0:128, 0:ncols]),
                     reads=[f"psum{bank}"], writes=[kat])
            for sc in range(2):
                c = CS[sc]
                kst = f"{stag}_st{par}"
                stt8 = stt8p[par]
                a8 = asb[:, COL0[sc]:COL0[sc] + 8 * c]
                a8v = a8.rearrange("p (h j) -> p h j", h=8)
                P.op("vector", lambda e, a8v=a8v, sc=sc: e.reduce_max(
                    stt8[sc], a8v, axis=AXX),
                    reads=[kat], writes=[kst])
                P.op("vector", lambda e, a8=a8, a8v=a8v, sc=sc, c=c:
                     e.tensor_sub(a8v, a8v,
                                  stt8[sc].unsqueeze(2).broadcast_to(
                                      [128, 8, c])),
                     reads=[kat, kst], writes=[kat])
                P.op("scalar", lambda e, a8=a8: e.activation(a8, a8, AF.Exp),
                     reads=[kat], writes=[kat])
                P.op("vector", lambda e, a8v=a8v, sc=sc: e.reduce_sum(
                    stt8[sc], a8v, axis=AXX),
                    reads=[kat], writes=[kst])
                P.op("vector", lambda e, sc=sc: e.reciprocal(
                    stt8[sc], stt8[sc]),
                    reads=[kst], writes=[kst])
                P.op("vector", lambda e, sc=sc: e.tensor_scalar(
                    stt8[sc], stt8[sc], swcol[:, sc:sc + 1], None, ALU.mult),
                    reads=[kst, kc + "sc"], writes=[kst])
                P.op("vector", lambda e, a8v=a8v, sc=sc, par=par, c=c:
                     e.tensor_mul(
                         w16s[par][sc].rearrange("p (h j) -> p h j", h=8),
                         a8v,
                         stt8[sc].unsqueeze(2).broadcast_to([128, 8, c])),
                     reads=[kat, kst], writes=[f"{stag}_w16_{par}_{sc}"])
                for hg in range(2):
                    base = (((i * 2 + sc) * 8 + hg * 4) * 128 * 256
                            + OFFS[sc])
                    dst = bass.AP(A.tensor, base,
                                  [[257, 128], [128 * 256, 4], [1, c]])
                    src = w16s[par][sc][:, hg * 4 * c:(hg * 4 + 4) * c] \
                        .rearrange("p (h j) -> p h j", h=4)
                    P.dma(DMA_QUEUES[(i * 2 + sc + hg) % 3], dst, src,
                          reads=[f"{stag}_w16_{par}_{sc}"],
                          writes=[f"{stag}_A_{i}_{sc}_{hg * 4 + hh}"
                                  for hh in range(4)])
        def zr_reads(sl_i):
            r = [f"{stag}_zz"]
            for tt in (sl_i - 1, sl_i):
                if 0 <= tt < NT:
                    r.append(f"{stag}_zp_{tt}")
            return r
        for i in range(NT):
            pb = i % 2
            if i == 0:
                P.dma("sync", zslab[0], zpad[0:128, :],
                      reads=zr_reads(0), writes=[f"{stag}_zsl_0"])
            P.dma("sync", zslab[(i + 1) % NR],
                  zpad[(i + 1) * 128:(i + 2) * 128, :],
                  reads=zr_reads(i + 1), writes=[f"{stag}_zsl_{(i + 1) % NR}"])
            for sc in range(2):
                kab = f"{stag}_ab_{pb}_{sc}"
                P.dma(DMA_QUEUES[(i * 2 + sc) % 3],
                      abig[pb][sc].rearrange("p (h c) -> p h c", h=8),
                      A[i][sc].rearrange("h p c -> p h c"),
                      reads=[f"{stag}_A_{i}_{sc}_{h}" for h in range(8)],
                      writes=[kab])
            for h in range(8):
                bank = 2 + h % 2
                hp = h % 2
                for sc in range(2):
                    kab = f"{stag}_ab_{pb}_{sc}"
                    for pc in range(2):
                        transpose_128(
                            S, atb[pb][hp * 4 + sc * 2 + pc],
                            abig[pb][sc][:, h * 256 + pc * 128:
                                         h * 256 + (pc + 1) * 128],
                            consts_id16, 4 + (sc * 2 + pc) % 2, kab,
                            f"{stag}_AT_{pb}_{hp}_{sc}_{pc}", BF16,
                            evac_eng="vector" if pc else "scalar")
                for n4 in range(4):
                    sc, pc = n4 // 2, n4 % 2
                    P.op("tensor", lambda e, h=h, pb=pb, n4=n4, hp=hp,
                         pc=pc, bank=bank, i=i: e.matmul(
                             S.psum[bank][0:128, 0:128],
                             zslab[(i + pc) % NR][:, h * 128:(h + 1) * 128],
                             atb[pb][hp * 4 + n4],
                             start=(n4 == 0), stop=(n4 == 3)),
                         reads=[f"{stag}_AT_{pb}_{hp}_{sc}_{pc}",
                                f"{stag}_zsl_{(i + pc) % NR}"],
                         writes=[f"psum{bank}"])
                P.op("scalar", lambda e, h=h, i=i, bank=bank: e.copy(
                    zbuf[h][:, i * 128:(i + 1) * 128],
                    S.psum[bank][0:128, 0:128]),
                    reads=[f"psum{bank}"], writes=[kz])
        for slab in range(T // 512):
            sl = slice(slab * 512, (slab + 1) * 512)
            for c in range(DC):
                bank = 6 + c % 2
                for k in range(DC):
                    P.op("tensor", lambda e, k=k, c=c, sl=sl, bank=bank:
                         e.matmul(S.psum[bank][:, :],
                                  woutT[k][:, c * 128:(c + 1) * 128],
                                  zbuf[k][:, sl],
                                  start=(k == 0), stop=(k == DC - 1)),
                         reads=[f"{stag}_wout", kz],
                         writes=[f"psum{bank}"])
                P.op("scalar", lambda e, c=c, bank=bank: e.copy(
                    bo[c], S.psum[bank][:, :]),
                    reads=[f"psum{bank}"], writes=[bo_keys[c]])
            sw.post_slab_dmajor(slab, b, bo, bo_keys, in_dram[b], out_dram[b])


LAST_PROG = None

STAGE_FNS = {
    "l0_attn": stage_l0_attn,
    "l0_ffn": stage_l0_ffn,
    "l1_attn": stage_l1_attn,
    "l1_ffn": stage_l1_ffn,
    "l2_attn": stage_l2_attn,
    "l2_ffn": stage_l2_ffn,
    "l3_attn": stage_l3_attn,
    "l3_ffn": stage_l3_ffn,
}

STAGE_ORDER = ["l0_attn", "l0_ffn", "l1_attn", "l1_ffn",
               "l2_attn", "l2_ffn", "l3_attn", "l3_ffn"]


def make_consts():
    c = {}
    c["c_ident16"] = np.eye(128, dtype=np.float32)
    c["c_ones"] = np.ones((1, 128), dtype=np.float32)
    M = np.zeros((512, 256), np.float32)
    for leaf in range(256):
        for lev in range(8):
            j = leaf >> (8 - lev)
            side = (leaf >> (7 - lev)) & 1
            node = (1 << lev) - 1 + j
            M[node if side == 1 else 256 + node, leaf] = 1.0
    c["c_treeM"] = M
    c["c_iota_t"] = np.arange(T, dtype=np.float32).reshape(1, T)
    nv = 1.0 / (np.arange(T, dtype=np.float64) + 1.0)
    c["c_invn"] = nv.reshape(NT, 128).T.astype(np.float32).copy()
    import ml_dtypes as _md
    sel = np.zeros((32, 32 * 64), np.float32)
    for j in range(32):
        sel[j, j * 64:(j + 1) * 64] = 1.0
    c["c_sel64"] = sel.astype(_md.bfloat16)
    G = 256
    pos = np.arange(G)
    rel = pos[None, :] - pos[:, None]
    n = np.maximum(-rel, 0)
    max_exact = 16
    is_small = n < max_exact
    ln_ratio = np.log(128 / max_exact)
    with np.errstate(divide="ignore"):
        val_large = max_exact + (np.log(np.maximum(n, 1) / max_exact)
                                 / ln_ratio * 16).astype(np.int32)
    val_large = np.minimum(val_large, 31)
    bucket = np.where(is_small, n, val_large)
    bktT = np.zeros((32, G, G), np.float32)
    for v in range(32):
        bktT[v] = (bucket == v).T
    c["c_bucketT"] = bktT.astype(_md.bfloat16)
    triu = (rel > 0)
    c["c_triuTneg"] = (np.where(triu, -1e30, 0.0).T).astype(np.float32)
    j = np.arange(32)
    invf = 1.0 / (10000.0 ** (j / 32.0))
    t = np.arange(T)
    ang = np.outer(invf, t)
    cosr = np.cos(ang); sinr = np.sin(ang)
    Cb = np.concatenate([cosr, cosr], 0)
    Sb = np.concatenate([-sinr, sinr], 0)
    import ml_dtypes
    c["c_ropeC"] = np.concatenate([Cb, Cb], 0).astype(ml_dtypes.bfloat16)
    c["c_ropeS"] = np.concatenate([Sb, Sb], 0).astype(ml_dtypes.bfloat16)
    return c


def build_program(stages=None, snapshots=False):
    if stages is None:
        stages = [s for s in STAGE_ORDER if s in STAGE_FNS]
    nc = bass.Bass("TRN2", target_bir_lowering=False)
    inputs_decl = {}

    x_in = nc.dram_tensor("L0_attn_query", [B_CORE, T, D], F32,
                          kind="ExternalInput").ap()
    inputs_decl["L0_attn_query"] = x_in
    y_out = nc.dram_tensor("y", [B_CORE, T, D], F32, kind="ExternalOutput").ap()
    if snapshots:
        snaps = [nc.dram_tensor(f"ysnap{i}", [B_CORE, T, D], F32,
                                kind="ExternalOutput").ap()
                 for i in range(len(stages) - 1)]
    resid = nc.dram_tensor("resid", [B_CORE, T, D], F32, kind="Internal").ap()

    es = ExitStack()
    with es:
        ar = Arena(nc, 212800)
        ar.open()
        es.callback(ar.close)
        psum_cms = [nc.psum_tensor(f"pb{i}", [128, 512], F32) for i in range(8)]
        psum = [es.enter_context(cm).ap() for cm in psum_cms]
        sems = {}
        dma_sems = {}
        for e in ENGINES:
            sems[e] = es.enter_context(nc.semaphore(f"s_{e}"))
        for q in DMA_QUEUES:
            for ln in range(NLANES):
                dma_sems[(q, ln)] = es.enter_context(
                    nc.semaphore(f"d_{q}_{ln}"))

        P = Prog(nc)
        global LAST_PROG
        LAST_PROG = P
        S = Stage(nc, P, ar, psum, inputs_decl)
        S.gscr = [ar.galloc(128, D, F32) for _ in range(2)]

        ident32 = ar.galloc(128, 128, F32)
        ident16 = ar.galloc(128, 128, BF16)
        ones = ar.galloc(1, 128, F32)
        ones16 = ar.galloc(1, 128, BF16)
        ci = S.inp("c_ident16", (128, 128))
        co = S.inp("c_ones", (1, 128))
        P.dma("sync", ident32, ci, writes=["ident"])
        P.dma("gpsimd", ident16, ci, writes=["ident"])
        P.dma("sync", ones, co, writes=["ones"])
        P.dma("gpsimd", ones16, co, writes=["ones16"])
        consts = {"ident16": ident16, "ident32": ident32, "ones": ones,
                  "ones16": ones16}

        for si, st in enumerate(stages):
            if snapshots:
                in_dram = x_in if si == 0 else snaps[si - 1]
                out_dram = y_out if si == len(stages) - 1 else snaps[si]
            else:
                in_dram = x_in if si == 0 else resid
                out_dram = y_out if si == len(stages) - 1 else resid
            STAGE_FNS[st](S, consts, in_dram, out_dram)
            P.barrier()

        nfin = {}
        for q in DMA_QUEUES:
            n = P.dma_count[q]
            for ln in range(NLANES):
                cnt = (n - ln + NLANES - 1) // NLANES
                if cnt:
                    nfin[(q, ln)] = 16 * cnt
        P.finalize(sems, dma_sems)
        with nc.Block() as block:
            P.emit(block)

            @block.sync
            def _(sync):
                for lane, val in nfin.items():
                    sync.wait_ge(dma_sems[lane], val)

    return nc, sorted(inputs_decl.keys())


def make_in_map(inputs, names, consts, core):
    m = {}
    for n in names:
        if n == "L0_attn_query":
            m[n] = np.ascontiguousarray(
                inputs[n][core * B_CORE:(core + 1) * B_CORE])
        elif n.startswith("c_"):
            m[n] = consts[n]
        else:
            a = np.ascontiguousarray(inputs[n])
            if a.ndim == 0:
                a = a.reshape(1, 1)
            m[n] = a
    return m


def kernel(**inputs):
    nc, names = build_program()
    consts = make_consts()
    in_maps = [make_in_map(inputs, names, consts, core) for core in range(8)]
    res = run_bass_kernel_spmd(nc, in_maps, core_ids=list(range(8)))
    return np.concatenate([r["y"] for r in res.results], axis=0)
